# version 17
# baseline (speedup 1.0000x reference)
import math
import numpy as np
import concourse.bass as bass
import concourse.mybir as mybir
from concourse.bass_utils import run_bass_kernel_spmd

F32 = mybir.dt.float32
BF16 = mybir.dt.bfloat16
F16 = mybir.dt.float16
AF = mybir.ActivationFunctionType
ALU = mybir.AluOpType
MAGIC = float(3 * 2 ** 22)

D = 1024
FF = 4096
NB = 32
NT = 8
SCALE = 1.0 / 32.0


def _pwl_params():
    X = np.linspace(-4.0, 4.0, 33).astype(np.float32)
    erf = np.vectorize(math.erf)
    Y = (0.5 * X * (1.0 + erf(X.astype(np.float64) / math.sqrt(2.0)))).astype(np.float32)
    Gh = np.array([256.0 * (float(Y[16 + k]) - 0.125 * k) for k in range(17)])
    s = [(Gh[k + 1] - Gh[k]) / 64.0 for k in range(16)]
    s16 = 512.5 - Gh[16]
    Ks = [64.0 * k for k in range(1, 16)] + [1024.0, 1025.0]
    cs = [s[k] - s[k - 1] for k in range(1, 16)] + [s16 - s[15], 0.5 - s16]
    return float(s[0]), [float(k) for k in Ks], [float(c) for c in cs]


A0, KINK_K, KINK_C = _pwl_params()
NK = len(KINK_K)
GK = [5, 6, 7, 8]
TK = [15, 16]
RK = list(range(NK))
assert all(KINK_C[k] < 0 for k in GK)


class Sch:
    def __init__(self, nc, ctx):
        self.nc = nc
        self.sem = {}
        self.count = {}
        self.last_wait = {w: {} for w in ("T", "V", "S", "G", "Y")}
        for name in ("T", "V", "S", "G", "DL", "DS"):
            self.sem[name] = ctx.enter_context(nc.semaphore("sem_" + name))
            self.count[name] = 0

    def inc(self, name, inst):
        step = 16 if name in ("DL", "DS") else 1
        inst.then_inc(self.sem[name], step)
        self.count[name] += step
        return (name, self.count[name])

    def tok(self, name):
        return (name, self.count[name])

    def wait(self, eng, wname, token):
        if not token:
            return
        name, val = token
        if val <= 0:
            return
        lw = self.last_wait[wname]
        if lw.get(name, 0) >= val:
            return
        eng.wait_ge(self.sem[name], val)
        lw[name] = val


class St:
    def __init__(self, sch, H, B):
        self.sch = sch
        self.H = H
        self.B = B
        self.r = {}
        self.last_store = None
        self.last_ln_sq = None
        self.wvs_last = None
        self.qks_last = None
        self.b_s3b = None
        self.b_s1last = None
        self.b_vdone = None
        self.a2_ln_last = None
        self.a2_exp_last = None
        self.sc_guard = None
        self.pv_guard = None


def _barrier(block, st, include_y=False):
    sch = st.sch
    toks = [sch.tok(n) for n in ("T", "V", "S", "G", "DL", "DS")]
    pairs = [("T", block.tensor), ("V", block.vector), ("S", block.scalar),
             ("G", block.gpsimd)]
    if include_y:
        pairs.append(("Y", block.sync))
    for name, section in pairs:
        @section
        def _(e, _t=toks, _n=name):
            for tk in _t:
                sch.wait(e, _n, tk)


def build(limit_tiles=NT, phases="ab", debug=False):
    nc = bass.Bass("TRN2", target_bir_lowering=False, detect_race_conditions=False)

    H = {}
    H["x"] = nc.dram_tensor("x", [4096, D], F32, kind="ExternalInput").ap()
    H["wqkv"] = nc.dram_tensor("wqkv", [D, 3 * D], F32, kind="ExternalInput").ap()
    H["bqkv"] = nc.dram_tensor("bqkv", [3 * D], F32, kind="ExternalInput").ap()
    H["wout"] = nc.dram_tensor("wout", [D, D], F32, kind="ExternalInput").ap()
    H["bout"] = nc.dram_tensor("bout", [D], F32, kind="ExternalInput").ap()
    H["w1"] = nc.dram_tensor("w1", [D, FF], F32, kind="ExternalInput").ap()
    H["b1"] = nc.dram_tensor("b1", [FF], F32, kind="ExternalInput").ap()
    H["w2"] = nc.dram_tensor("w2", [FF, D], F32, kind="ExternalInput").ap()
    H["b2"] = nc.dram_tensor("b2", [D], F32, kind="ExternalInput").ap()
    H["out"] = nc.dram_tensor("out", [4096, D], F32, kind="ExternalOutput").ap()
    H["ydbg"] = nc.dram_tensor("ydbg", [4096, D], F32, kind="ExternalOutput").ap() if debug else None

    H["qk_hi"] = nc.dram_tensor("qk_hi", [16, 8, 128, 128], BF16).ap()
    H["qk_lo"] = nc.dram_tensor("qk_lo", [16, 8, 128, 128], BF16).ap()
    H["wv_hi"] = nc.dram_tensor("wv_hi", [8, 128, D], BF16).ap()
    H["wv_lo"] = nc.dram_tensor("wv_lo", [8, 128, D], BF16).ap()
    H["wo_hi"] = nc.dram_tensor("wo_hi", [8, 128, D], BF16).ap()
    H["wo_lo"] = nc.dram_tensor("wo_lo", [8, 128, D], BF16).ap()
    H["w1_hi"] = nc.dram_tensor("w1_hi", [32, 8, 128, 128], BF16).ap()
    H["w1_lo"] = nc.dram_tensor("w1_lo", [32, 8, 128, 128], BF16).ap()
    H["w2_hi"] = nc.dram_tensor("w2_hi", [32, 128, D], F16).ap()
    H["w2_lo"] = nc.dram_tensor("w2_lo", [32, 128, D], F16).ap()
    H["b1r_hi"] = nc.dram_tensor("b1r_hi", [32, 1, 128], BF16).ap()
    H["b1r_lo"] = nc.dram_tensor("b1r_lo", [32, 1, 128], BF16).ap()

    from contextlib import ExitStack
    es = ExitStack()

    def sb(name, shape, dt=F32):
        return es.enter_context(nc.sbuf_tensor(name, shape, dt))

    B = {}
    B["x_t"] = None
    B["q_t"] = None
    B["bqk_sb"] = sb("bqk_sb", [128, 16])
    B["bvr_hi"] = sb("bvr_hi", [1, D], BF16)
    B["bvr_lo"] = sb("bvr_lo", [1, D], BF16)
    B["bor_hi"] = sb("bor_hi", [1, D], BF16)
    B["bor_lo"] = sb("bor_lo", [1, D], BF16)
    B["b2r_hi"] = sb("b2r_hi", [1, D], F16)
    B["b2r_lo"] = sb("b2r_lo", [1, D], F16)
    B["iota_p"] = sb("iota_p", [128, 1])
    B["iota_f"] = sb("iota_f", [128, 128])
    B["id_f32"] = sb("id_f32", [128, 128])
    B["id_bf"] = sb("id_bf", [128, 128], BF16)
    B["m0t"] = sb("m0t", [128, 128], BF16)
    B["m2t"] = sb("m2t", [128, 128], BF16)
    B["ones_bf"] = sb("ones_bf", [1, 512], BF16)
    B["ones_f16"] = sb("ones_f16", [1, 128], F16)
    B["kinkb"] = sb("kinkb", [128, NK])
    B["pmag"] = sb("pmag", [128, 1])
    B["nmag"] = sb("nmag", [128, 1])
    B["st_t"] = sb("st_t", [128, 16])

    with es, nc.Block() as block:
        sch = Sch(nc, es)
        st = St(sch, H, B)
        _emit_consts(nc, block, st)
        _emit_prep(nc, block, st)
        _barrier(block, st, include_y=True)

        B["x_t"] = sb("x_t", [128, 5 * D])
        B["y_t"] = sb("y_t", [128, 4 * D])
        B["q_t"] = sb("q_t", [128, 8 * 5 * 128])
        B["k_t"] = sb("k_t", [128, 8 * 6 * 128])
        B["vt_t"] = sb("vt_t", [128, 6 * 1040])
        B["xt_hi"] = sb("xt_hi", [128, 4 * 8 * 128], BF16)
        B["xt_lo"] = sb("xt_lo", [128, 4 * 8 * 128], BF16)
        B["at_hi"] = sb("at_hi", [128, 8 * 128], BF16)
        B["at_lo"] = sb("at_lo", [128, 8 * 128], BF16)
        B["attn_tm"] = sb("attn_tm", [128, D])
        B["e_t"] = sb("e_t", [128, 2 * 384])
        B["a_t"] = sb("a_t", [128, 32 * 512], F16)
        B["qks_hi"] = sb("qks_hi", [128, 2 * 1024], BF16)
        B["qks_lo"] = sb("qks_lo", [128, 2 * 1024], BF16)
        B["wvs_hi"] = sb("wvs_hi", [128, 2 * D], BF16)
        B["wvs_lo"] = sb("wvs_lo", [128, 2 * D], BF16)
        B["w1s_hi"] = B["qks_hi"]
        B["w1s_lo"] = B["qks_lo"]
        B["w2s_hi"] = B["wvs_hi"][:].bitcast(F16)
        B["w2s_lo"] = B["wvs_lo"][:].bitcast(F16)
        B["b1s_hi"] = sb("b1s_hi", [1, 2 * 128], BF16)
        B["b1s_lo"] = sb("b1s_lo", [1, 2 * 128], BF16)
        B["s1_t"] = sb("s1_t", [128, 2 * 512])
        B["rk_t"] = sb("rk_t", [128, 2 * 1024])
        B["s3_t"] = B["e_t"][:, 0:512]
        B["r2_t"] = B["a_t"][:, 0:2048].bitcast(F32)
        B["sq_t"] = B["a_t"][:, 2048:4096].bitcast(F32)
        B["oh16"] = B["attn_tm"][:].bitcast(F16)
        B["acc_t"] = B["attn_tm"][:, 0:512]
        B["t1_t"] = B["attn_tm"][:, 512:1024]
        B["u_t"] = None
        B["u0"] = B["at_hi"][:].bitcast(F32)
        B["u1"] = B["at_lo"][:].bitcast(F32)
        @block.vector
        def _(v):
            i = None
            for s in range(6):
                i = v.memset(B["vt_t"][:, s * 1040 + 64: (s + 1) * 1040: 65], 1.0)
            sch.inc("V", i)
        for t in range(limit_tiles):
            _emit_A1(nc, block, st, t, sub=("all" if phases not in ("x", "tr", "tr1", "tr1a", "qk") else phases))
            if phases in ("a1", "x", "tr", "tr1", "tr1a", "qk"):
                _barrier(block, st)
                continue
            _emit_A2(nc, block, st, t, ms=(None if t < NT - 1 else list(range(27, 31))))
            if phases == "a2":
                _barrier(block, st)
                continue
            if t == NT - 1:
                _emit_B(nc, block, st, t, list(range(27, 31)))
                _barrier(block, st)
                _emit_A2(nc, block, st, t, ms=[31], tag="x")
                _barrier(block, st)
                _emit_B(nc, block, st, t, [31])
            else:
                _emit_B(nc, block, st, t, list(range(max(0, 4 * t - 1), 4 * t + 3)))

        @block.gpsimd
        def _(g):
            g.wait_ge(sch.sem["DS"], sch.count["DS"])
    return nc


def _emit_consts(nc, block, st):
    sch, B = st.sch, st.B

    @block.gpsimd
    def _(g):
        g.iota(B["iota_p"][:], pattern=[[0, 1]], channel_multiplier=1,
               allow_small_or_imprecise_dtypes=True)
        i = g.iota(B["iota_f"][:], pattern=[[1, 128]], channel_multiplier=0,
                   allow_small_or_imprecise_dtypes=True)
        sch.inc("G", i)

    @block.vector
    def _(v):
        sch.wait(v, "V", sch.tok("G"))
        v.tensor_scalar(out=B["id_f32"][:], in0=B["iota_f"][:],
                        scalar1=B["iota_p"][:, 0:1], scalar2=None, op0=ALU.is_equal)
        v.tensor_copy(out=B["id_bf"][:], in_=B["id_f32"][:])
        v.tensor_scalar(out=B["m0t"][:], in0=B["iota_f"][:],
                        scalar1=B["iota_p"][:, 0:1], scalar2=-1.0e30,
                        op0=ALU.is_lt, op1=ALU.mult)
        v.tensor_scalar(out=B["m2t"][:], in0=B["iota_f"][:],
                        scalar1=B["iota_p"][:, 0:1], scalar2=-1.0e30,
                        op0=ALU.is_gt, op1=ALU.mult)
        v.memset(B["ones_bf"][:], 1.0)
        v.memset(B["ones_f16"][:], 1.0)
        for k in range(NK):
            v.memset(B["kinkb"][:, k:k + 1],
                     -KINK_K[k] if k in TK else -abs(KINK_C[k]) * KINK_K[k])
        v.memset(B["pmag"][:], MAGIC)
        i = v.memset(B["nmag"][:], -MAGIC)
        sch.inc("V", i)


def _emit_prep(nc, block, st):
    sch, H, B = st.sch, st.H, st.B
    with nc.sbuf_tensor("pf0", [128, 4096], F32) as pf0, \
         nc.sbuf_tensor("pf1", [128, 4096], F32) as pf1, \
         nc.sbuf_tensor("ph0", [128, 4096], BF16) as ph0, \
         nc.sbuf_tensor("ph1", [128, 4096], BF16) as ph1, \
         nc.sbuf_tensor("pl0", [128, 4096], BF16) as pl0, \
         nc.sbuf_tensor("pl1", [128, 4096], BF16) as pl1, \
         nc.sbuf_tensor("ph16", [128, 2 * 1024], F16) as ph16, \
         nc.sbuf_tensor("pl16", [128, 2 * 1024], F16) as pl16, \
         nc.sbuf_tensor("brow", [1, 4096], F32) as brow, \
         nc.sbuf_tensor("brh", [1, 4096], BF16) as brh, \
         nc.sbuf_tensor("brl", [1, 4096], BF16) as brl:

        pf = [pf0, pf1]
        phh = [ph0, ph1]
        pll = [pl0, pl1]
        jobs = []
        for dc in range(8):
            jobs.append((H["wqkv"][dc * 128:(dc + 1) * 128, 0:2048], 2048,
                         H["qk_hi"][:, dc], H["qk_lo"][:, dc], "cc"))
        for dc in range(8):
            jobs.append((H["wqkv"][dc * 128:(dc + 1) * 128, 2048:3072], 1024,
                         H["wv_hi"][dc], H["wv_lo"][dc], None))
        for dc in range(8):
            jobs.append((H["wout"][dc * 128:(dc + 1) * 128, :], 1024,
                         H["wo_hi"][dc], H["wo_lo"][dc], None))
        for dc in range(8):
            jobs.append((H["w1"][dc * 128:(dc + 1) * 128, :], 4096,
                         H["w1_hi"][:, dc], H["w1_lo"][:, dc], "fc"))
        for fc in range(32):
            jobs.append((H["w2"][fc * 128:(fc + 1) * 128, :], 1024,
                         H["w2_hi"][fc], H["w2_lo"][fc], "f16"))

        for j, (src, n, hid, lod, mode) in enumerate(jobs):
            p = j % 2

            @block.sync
            def _(y, j=j, p=p, src=src, n=n):
                sch.wait(y, "Y", st.r.get(("pst", j - 2), None))
                d = y.dma_start(out=pf[p][:, 0:n], in_=src)
                st.r[("pld", j)] = sch.inc("DL", d)

            @block.vector
            def _(v, j=j, p=p, n=n, mode=mode):
                sch.wait(v, "V", st.r[("pld", j)])
                sch.wait(v, "V", st.r.get(("pst", j - 2), None))
                if mode == "f16":
                    v.tensor_copy(out=ph16[:, p * 1024:p * 1024 + n], in_=pf[p][:, 0:n])
                    i = v.scalar_tensor_tensor(
                        out=pl16[:, p * 1024:p * 1024 + n],
                        in0=ph16[:, p * 1024:p * 1024 + n], scalar=-1.0,
                        in1=pf[p][:, 0:n], op0=ALU.mult, op1=ALU.add)
                else:
                    v.tensor_copy(out=phh[p][:, 0:n], in_=pf[p][:, 0:n])
                    i = v.scalar_tensor_tensor(
                        out=pll[p][:, 0:n], in0=phh[p][:, 0:n], scalar=-1.0,
                        in1=pf[p][:, 0:n], op0=ALU.mult, op1=ALU.add)
                st.r[("psp", j)] = sch.inc("V", i)

            @block.gpsimd
            def _(g, j=j, p=p, n=n, hid=hid, lod=lod, mode=mode):
                sch.wait(g, "G", st.r[("psp", j)])
                if mode == "f16":
                    shi = ph16[:, p * 1024:p * 1024 + n]
                    slo = pl16[:, p * 1024:p * 1024 + n]
                else:
                    shi = phh[p][:, 0:n]
                    slo = pll[p][:, 0:n]
                if mode in ("cc", "fc"):
                    shi = shi.rearrange("p (c m) -> p c m", m=128)
                    slo = slo.rearrange("p (c m) -> p c m", m=128)
                    hid = hid.rearrange("c p m -> p c m")
                    lod = lod.rearrange("c p m -> p c m")
                d = g.dma_start(out=hid, in_=shi)
                sch.inc("DS", d)
                d = g.dma_start(out=lod, in_=slo)
                st.r[("pst", j)] = sch.inc("DS", d)

        @block.sync
        def _(y):
            sch.wait(y, "Y", st.r[("pst", len(jobs) - 1)])
            for cc in range(16):
                d = y.dma_start(out=B["bqk_sb"][:, cc:cc + 1],
                                in_=H["bqkv"][cc * 128:(cc + 1) * 128])
                sch.inc("DL", d)
            d = y.dma_start(out=brow[0:1, 0:1024], in_=H["bqkv"][2048:3072])
            sch.inc("DL", d)
            d = y.dma_start(out=brow[0:1, 1024:2048], in_=H["bout"])
            sch.inc("DL", d)
            d = y.dma_start(out=brow[0:1, 2048:3072], in_=H["b2"])
            sch.inc("DL", d)

        @block.vector
        def _(v):
            sch.wait(v, "V", sch.tok("DL"))
            v.tensor_copy(out=B["bvr_hi"][:], in_=brow[0:1, 0:1024])
            v.scalar_tensor_tensor(out=B["bvr_lo"][:], in0=B["bvr_hi"][:], scalar=-1.0,
                                   in1=brow[0:1, 0:1024], op0=ALU.mult, op1=ALU.add)
            v.tensor_copy(out=B["bor_hi"][:], in_=brow[0:1, 1024:2048])
            v.scalar_tensor_tensor(out=B["bor_lo"][:], in0=B["bor_hi"][:], scalar=-1.0,
                                   in1=brow[0:1, 1024:2048], op0=ALU.mult, op1=ALU.add)
            v.tensor_scalar(out=brow[0:1, 3072:4096], in0=brow[0:1, 2048:3072],
                            scalar1=256.0, scalar2=None, op0=ALU.mult)
            v.tensor_copy(out=B["b2r_hi"][:], in_=brow[0:1, 3072:4096])
            i = v.scalar_tensor_tensor(out=B["b2r_lo"][:], in0=B["b2r_hi"][:], scalar=-1.0,
                                       in1=brow[0:1, 3072:4096], op0=ALU.mult, op1=ALU.add)
            st.r[("bprep",)] = sch.inc("V", i)

        @block.sync
        def _(y):
            sch.wait(y, "Y", st.r[("bprep",)])
            d = y.dma_start(out=brow[0:1, 0:4096], in_=H["b1"])
            sch.inc("DL", d)

        @block.vector
        def _(v):
            sch.wait(v, "V", sch.tok("DL"))
            v.tensor_copy(out=brh[:], in_=brow[0:1, 0:4096])
            i = v.scalar_tensor_tensor(out=brl[:], in0=brh[:], scalar=-1.0,
                                       in1=brow[0:1, 0:4096], op0=ALU.mult, op1=ALU.add)
            st.r[("b1sp",)] = sch.inc("V", i)

        @block.gpsimd
        def _(g):
            sch.wait(g, "G", st.r[("b1sp",)])
            d = g.dma_start(out=H["b1r_hi"].rearrange("fc o m -> o fc m"),
                            in_=brh[:].rearrange("o (fc m) -> o fc m", m=128))
            sch.inc("DS", d)
            d = g.dma_start(out=H["b1r_lo"].rearrange("fc o m -> o fc m"),
                            in_=brl[:].rearrange("o (fc m) -> o fc m", m=128))
            sch.inc("DS", d)


def _emit_A1(nc, block, st, t, sub="all"):
    sch, H, B = st.sch, st.H, st.B
    blocks = list(range(4 * t, 4 * t + 4))

    @block.sync
    def _(y):
        sch.wait(y, "Y", sch.tok("DS"))
        for n in blocks:
            sch.wait(y, "Y", st.r.get((("ln1", n - 5), "r"), None))
            d = y.dma_start(out=B["x_t"][:, (n % 5) * D:(n % 5 + 1) * D],
                            in_=H["x"][n * 128:(n + 1) * 128, :])
            st.r[("x", n)] = sch.inc("DL", d)

    if sub == "x":
        return
    with nc.psum_tensor(f"ptr_{t}", [128, 1024], F32) as ptr:
        for bi, n in enumerate(blocks):
            if sub == "tr1" and bi > 0:
                break
            for dc in range(8):
                if sub in ("tr1", "tr1a") and dc >= 2:
                    break
                @block.tensor
                def _(te, n=n, dc=dc, bi=bi):
                    if dc == 0:
                        if bi == 0:
                            sch.wait(te, "T", st.b_s1last)
                        sch.wait(te, "T", st.r[("x", n)])
                    pb = dc % 2
                    prev = ("trev", n, dc - 2) if dc >= 2 else ("trev", n - 1, 6 + dc)
                    sch.wait(te, "T", st.r.get(prev, None))
                    i = te.transpose(ptr[:, pb * 512: pb * 512 + 128],
                                     B["x_t"][:, (n % 5) * D + dc * 128:(n % 5) * D + (dc + 1) * 128],
                                     B["id_f32"][:])
                    st.r[("tr", n, dc)] = sch.inc("T", i)

                if sub != "tr1a":
                    @block.vector
                    def _(v, n=n, dc=dc, bi=bi):
                        pb = dc % 2
                        col = ((n % 4) * 8 + dc) * 128
                        if bi == 0 and dc == 0:
                            sch.wait(v, "V", st.qks_last)
                        sch.wait(v, "V", st.r[("tr", n, dc)])
                        v.tensor_copy(out=B["xt_hi"][:, col:col + 128],
                                      in_=ptr[:, pb * 512: pb * 512 + 128])
                        i = v.scalar_tensor_tensor(
                            out=B["xt_lo"][:, col:col + 128],
                            in0=B["xt_hi"][:, col:col + 128], scalar=-1.0,
                            in1=ptr[:, pb * 512: pb * 512 + 128], op0=ALU.mult, op1=ALU.add)
                        st.r[("trev", n, dc)] = sch.inc("V", i)

    xt_ready = sch.tok("V")
    if sub in ("tr", "tr1", "tr1a"):
        return

    with nc.psum_tensor(f"pqk_{t}", [128, 2 * 512], F32) as pqk:
        q_base = sch.count["S"]
        for cc in range(16):
            st.r[("qkev", t, cc)] = ("S", q_base + cc + 1)
        for cc in range(16):
            sl = cc % 2

            @block.sync
            def _(y, cc=cc, sl=sl):
                if cc < 2:
                    sch.wait(y, "Y", st.qks_last)
                sch.wait(y, "Y", st.r.get(("qkmm", t, cc - 2), None))
                d = y.dma_start(out=B["qks_hi"][:, sl * 1024:(sl + 1) * 1024].rearrange(
                    "p (dc m) -> p dc m", m=128), in_=H["qk_hi"][cc].rearrange("dc p m -> p dc m"))
                sch.inc("DL", d)
                d = y.dma_start(out=B["qks_lo"][:, sl * 1024:(sl + 1) * 1024].rearrange(
                    "p (dc m) -> p dc m", m=128), in_=H["qk_lo"][cc].rearrange("dc p m -> p dc m"))
                st.r[("qkw", t, cc)] = sch.inc("DL", d)

            @block.tensor
            def _(te, cc=cc, sl=sl):
                sch.wait(te, "T", st.r[("qkw", t, cc)])
                sch.wait(te, "T", xt_ready)
                sch.wait(te, "T", st.r.get(("qkev", t, cc - 2), None))
                rh4 = B["xt_hi"][:].rearrange("p (s dc m) -> p dc s m", dc=8, m=128)
                rl4 = B["xt_lo"][:].rearrange("p (s dc m) -> p dc s m", dc=8, m=128)
                i = None
                for dc in range(8):
                    whi = B["qks_hi"][:, sl * 1024 + dc * 128: sl * 1024 + (dc + 1) * 128]
                    wlo = B["qks_lo"][:, sl * 1024 + dc * 128: sl * 1024 + (dc + 1) * 128]
                    te.matmul(pqk[:, sl * 512:(sl + 1) * 512], whi, rh4[:, dc],
                              start=(dc == 0), stop=False)
                    te.matmul(pqk[:, sl * 512:(sl + 1) * 512], whi, rl4[:, dc],
                              start=False, stop=False)
                    i = te.matmul(pqk[:, sl * 512:(sl + 1) * 512], wlo, rh4[:, dc],
                                  start=False, stop=(dc == 7))
                st.r[("qkmm", t, cc)] = sch.inc("T", i)

        @block.scalar
        def _(s):
            sch.wait(s, "S", st.sc_guard)
            for cc in range(16):
                sl = cc % 2
                sch.wait(s, "S", st.r[("qkmm", t, cc)])
                dst = B["q_t"] if cc < 8 else B["k_t"]
                ring = 5 if cc < 8 else 6
                ccp = cc % 8
                for bi, n in enumerate(blocks):
                    rs = n % ring
                    i = s.activation(
                        out=dst[:, (ccp * ring + rs) * 128:(ccp * ring + rs + 1) * 128],
                        in_=pqk[:, sl * 512 + bi * 128: sl * 512 + (bi + 1) * 128],
                        func=AF.Identity, bias=B["bqk_sb"][:, cc:cc + 1], scale=1.0)
                sch.inc("S", i)

    qk_done = sch.tok("S")
    if sub == "qk":
        return

    with nc.psum_tensor(f"pv_{t}", [128, 4 * 1024], F32) as pv:
        for dc in range(8):
            sl = dc % 2

            @block.sync
            def _(y, dc=dc, sl=sl):
                if dc < 2:
                    sch.wait(y, "Y", st.wvs_last)
                sch.wait(y, "Y", st.r.get(("vmm", t, dc - 2), None))
                d = y.dma_start(out=B["wvs_hi"][:, sl * D:(sl + 1) * D], in_=H["wv_hi"][dc])
                sch.inc("DL", d)
                d = y.dma_start(out=B["wvs_lo"][:, sl * D:(sl + 1) * D], in_=H["wv_lo"][dc])
                st.r[("wv", t, dc)] = sch.inc("DL", d)

            @block.tensor
            def _(te, dc=dc, sl=sl):
                if dc == 0:
                    sch.wait(te, "T", qk_done)
                    sch.wait(te, "T", st.b_s3b)
                sch.wait(te, "T", st.r[("wv", t, dc)])
                i = None
                for bi, n in enumerate(blocks):
                    col = ((n % 4) * 8 + dc) * 128
                    xh = B["xt_hi"][:, col:col + 128]
                    xl = B["xt_lo"][:, col:col + 128]
                    for hf in range(2):
                        pvs = pv[:, bi * 1024 + hf * 512: bi * 1024 + (hf + 1) * 512]
                        te.matmul(pvs, xh,
                                  B["wvs_hi"][:, sl * D + hf * 512: sl * D + (hf + 1) * 512],
                                  start=(dc == 0), stop=False)
                        te.matmul(pvs, xh,
                                  B["wvs_lo"][:, sl * D + hf * 512: sl * D + (hf + 1) * 512],
                                  start=False, stop=False)
                        i = te.matmul(pvs, xl,
                                      B["wvs_hi"][:, sl * D + hf * 512: sl * D + (hf + 1) * 512],
                                      start=False, stop=False)
                st.r[("vmm", t, dc)] = sch.inc("T", i)

        @block.tensor
        def _(te):
            i = None
            for bi in range(4):
                for hf in range(2):
                    pvs = pv[:, bi * 1024 + hf * 512: bi * 1024 + (hf + 1) * 512]
                    te.matmul(pvs, B["ones_bf"][0:1, 0:128],
                              B["bvr_hi"][0:1, hf * 512:(hf + 1) * 512],
                              start=False, stop=False)
                    i = te.matmul(pvs, B["ones_bf"][0:1, 0:128],
                                  B["bvr_lo"][0:1, hf * 512:(hf + 1) * 512],
                                  start=False, stop=True)
            st.r[("vmm_all", t)] = sch.inc("T", i)
        st.qks_last = st.r[("qkmm", t, 15)]
        st.wvs_last = st.r[("vmm", t, 7)]

        @block.vector
        def _(v):
            sch.wait(v, "V", st.pv_guard)
            sch.wait(v, "V", st.r[("vmm_all", t)])
            for bi, n in enumerate(blocks):
                rs = n % 6
                dst = B["vt_t"][:, rs * 1040:(rs + 1) * 1040].rearrange(
                    "p (h c) -> p h c", c=65)[:, :, 0:64]
                src = pv[:, bi * 1024:(bi + 1) * 1024].rearrange(
                    "p (h c) -> p h c", c=64)
                i = v.tensor_copy(out=dst, in_=src)
                st.r[("v", n)] = sch.inc("V", i)


def _emit_A2(nc, block, st, t, ms=None, tag=""):
    sch, H, B = st.sch, st.H, st.B
    if ms is None:
        ms = list(range(max(0, 4 * t - 1), 4 * t + 3))

    with nc.psum_tensor(f"psc_{t}{tag}", [128, 2 * 512], F32) as psc, \
         nc.psum_tensor(f"ppv_{t}{tag}", [128, 1024], F32) as ppv, \
         nc.psum_tensor(f"ptr2_{t}{tag}", [128, 1024], F32) as ptr2, \
         nc.psum_tensor(f"pwo_{t}{tag}", [128, 1024], F32) as pwo:

        def emit_sc(m, h, chunks):
            cc = h // 2
            half = (h % 2) * 64
            pb = h % 2
            qs = m % 5

            @block.tensor
            def _(te):
                if m == ms[0] and h == 0:
                    sch.wait(te, "T", st.r.get(("qkev", t, 15), None))
                    sch.wait(te, "T", st.r.get(("v", 4 * t), None))
                    sch.wait(te, "T", st.b_s3b)
                sch.wait(te, "T", st.r.get(("exp", m, h - 2), None))
                if h <= 1:
                    sch.wait(te, "T", st.r.get(("exp", m - 1, 14 + h), None))
                i = None
                for (c, ks) in chunks:
                    lhs = B["k_t"][half:half + 64,
                                   (cc * 6 + ks) * 128:(cc * 6 + ks + 1) * 128]
                    rhs = B["q_t"][half:half + 64,
                                   (cc * 5 + qs) * 128:(cc * 5 + qs + 1) * 128]
                    i = te.matmul(psc[:, pb * 512 + c * 128: pb * 512 + (c + 1) * 128],
                                  lhs, rhs, start=True, stop=(c == 1))
                    if c == 0:
                        i = te.matmul(psc[:, pb * 512: pb * 512 + 128],
                                      B["m0t"][:], B["id_bf"][:], start=False, stop=True)
                    elif c == 2:
                        i = te.matmul(psc[:, pb * 512 + 256: pb * 512 + 384],
                                      B["m2t"][:], B["id_bf"][:], start=False, stop=True)
                st.r[("sc", m, h)] = sch.inc("T", i)

        def emit_exp_section(m, chunks):
            @block.scalar
            def _(s):
                if m == ms[0]:
                    sch.wait(s, "S", st.b_vdone)
                for h in range(16):
                    pb = h % 2
                    sch.wait(s, "S", st.r[("sc", m, h)])
                    sch.wait(s, "S", st.r.get(("pv", m, h - 2), None))
                    if h <= 1:
                        sch.wait(s, "S", st.r.get(("pv", m - 1, 14 + h), None))
                    c0, c1 = chunks[0][0], chunks[-1][0]
                    i = s.activation(out=B["e_t"][:, pb * 384 + c0 * 128: pb * 384 + (c1 + 1) * 128],
                                     in_=psc[:, pb * 512 + c0 * 128: pb * 512 + (c1 + 1) * 128],
                                     func=AF.Exp, bias=0.0, scale=SCALE)
                    sch.inc("S", i)

        def emit_pv(m, h, chunks):
            pb = h % 2

            @block.tensor
            def _(te):
                if m == ms[0] and h == 0:
                    sch.wait(te, "T", st.r.get(("v", 4 * t + 1), None))
                sch.wait(te, "T", st.r[("exp", m, h)])
                if h % 8 == 0:
                    sch.wait(te, "T", st.r.get(("anorm", m, h - 8), None))
                    if h == 0:
                        sch.wait(te, "T", st.r.get(("anorm", m - 1, 8), None))
                po = (h % 8) * 128
                first = True
                i = None
                for (c, ks) in chunks:
                    lhs = B["e_t"][:, pb * 384 + c * 128: pb * 384 + (c + 1) * 128]
                    rhs = B["vt_t"][:, ks * 1040 + h * 65: ks * 1040 + (h + 1) * 65]
                    i = te.matmul(ppv[:, po:po + 65], lhs, rhs,
                                  start=first, stop=(c == chunks[-1][0]))
                    first = False
                st.r[("pv", m, h)] = sch.inc("T", i)

        for m in ms:
            chunks = []
            if m > 0:
                chunks.append((0, (m - 1) % 6))
            chunks.append((1, m % 6))
            if m < 31:
                chunks.append((2, (m + 1) % 6))

            e_base = sch.count["S"]
            for h in range(16):
                st.r[("exp", m, h)] = ("S", e_base + h + 1)
            for h in range(16):
                emit_sc(m, h, chunks)
                if h >= 1:
                    emit_pv(m, h - 1, chunks)
                    if h == 8:
                        _emit_wave_norm(nc, block, st, m, 0, ppv)
            emit_pv(m, 15, chunks)
            emit_exp_section(m, chunks)
            _emit_wave_norm(nc, block, st, m, 1, ppv)

            for dc in range(8):
                @block.tensor
                def _(te, m=m, dc=dc):
                    if dc == 0:
                        if m == ms[0]:
                            sch.wait(te, "T", st.r.get(("v", 4 * t + 2), None))
                        sch.wait(te, "T", st.r[("anorm", m, 0)])
                    if dc == 4:
                        sch.wait(te, "T", st.r[("anorm", m, 8)])
                    pb2 = dc % 2
                    prev = ("atr_ev", m, dc - 2) if dc >= 2 else ("atr_ev", m - 1, 6 + dc)
                    sch.wait(te, "T", st.r.get(prev, None))
                    i = te.transpose(ptr2[:, pb2 * 512: pb2 * 512 + 128],
                                     B["attn_tm"][:, dc * 128:(dc + 1) * 128],
                                     B["id_f32"][:])
                    st.r[("atr", m, dc)] = sch.inc("T", i)

                @block.vector
                def _(v, m=m, dc=dc):
                    pb2 = dc % 2
                    sch.wait(v, "V", st.r[("atr", m, dc)])
                    sch.wait(v, "V", st.r.get(("womm", m - 1, dc), None))
                    v.tensor_copy(out=B["at_hi"][:, dc * 128:(dc + 1) * 128],
                                  in_=ptr2[:, pb2 * 512: pb2 * 512 + 128])
                    i = v.scalar_tensor_tensor(
                        out=B["at_lo"][:, dc * 128:(dc + 1) * 128],
                        in0=B["at_hi"][:, dc * 128:(dc + 1) * 128], scalar=-1.0,
                        in1=ptr2[:, pb2 * 512: pb2 * 512 + 128], op0=ALU.mult, op1=ALU.add)
                    st.r[("atr_ev", m, dc)] = sch.inc("V", i)

            for dc in range(8):
                sl = dc % 2

                @block.sync
                def _(y, m=m, dc=dc, sl=sl):
                    if dc < 2:
                        sch.wait(y, "Y", st.wvs_last)
                    sch.wait(y, "Y", st.r.get(("womm", m, dc - 2) if dc >= 2
                                              else ("womm", m - 1, 6 + dc), None))
                    d = y.dma_start(out=B["wvs_hi"][:, sl * D:(sl + 1) * D],
                                    in_=H["wo_hi"][dc])
                    sch.inc("DL", d)
                    d = y.dma_start(out=B["wvs_lo"][:, sl * D:(sl + 1) * D],
                                    in_=H["wo_lo"][dc])
                    st.r[("wow", m, dc)] = sch.inc("DL", d)

                @block.tensor
                def _(te, m=m, dc=dc, sl=sl):
                    if dc == 0:
                        if m == ms[0]:
                            sch.wait(te, "T", st.r.get(("v", 4 * t + 3), None))
                        sch.wait(te, "T", st.r.get((("ln1", m - 1), "r"), None))
                    sch.wait(te, "T", st.r[("atr_ev", m, dc)])
                    sch.wait(te, "T", st.r[("wow", m, dc)])
                    ah = B["at_hi"][:, dc * 128:(dc + 1) * 128]
                    al = B["at_lo"][:, dc * 128:(dc + 1) * 128]
                    i = None
                    for hf in range(2):
                        pws = pwo[:, hf * 512:(hf + 1) * 512]
                        te.matmul(pws, ah,
                                  B["wvs_hi"][:, sl * D + hf * 512: sl * D + (hf + 1) * 512],
                                  start=(dc == 0), stop=False)
                        te.matmul(pws, ah,
                                  B["wvs_lo"][:, sl * D + hf * 512: sl * D + (hf + 1) * 512],
                                  start=False, stop=False)
                        i = te.matmul(pws, al,
                                      B["wvs_hi"][:, sl * D + hf * 512: sl * D + (hf + 1) * 512],
                                      start=False, stop=False)
                    st.r[("womm", m, dc)] = sch.inc("T", i)

            @block.tensor
            def _(te, m=m):
                i = None
                for hf in range(2):
                    pws = pwo[:, hf * 512:(hf + 1) * 512]
                    te.matmul(pws, B["ones_bf"][0:1, 0:128],
                              B["bor_hi"][0:1, hf * 512:(hf + 1) * 512],
                              start=False, stop=False)
                    i = te.matmul(pws, B["ones_bf"][0:1, 0:128],
                                  B["bor_lo"][0:1, hf * 512:(hf + 1) * 512],
                                  start=False, stop=True)
                st.r[("wo_mm", m)] = sch.inc("T", i)
            st.wvs_last = st.r[("womm", m, 7)]

            _emit_ln(nc, block, st, src=pwo[:],
                     res_ap=B["x_t"][:, (m % 5) * D:(m % 5 + 1) * D],
                     out_ap=B["y_t"][:, (m % 4) * D:(m % 4 + 1) * D],
                     key=("ln1", m), dep=st.r[("wo_mm", m)])
            st.r[("y", m)] = st.r[("ln1", m)]
        st.sc_guard = st.r[("sc", ms[-1], 15)]
        st.pv_guard = st.r[("pv", ms[-1], 15)]
        st.a2_ln_last = st.r[("ln1", ms[-1])]
        st.a2_exp_last = st.r[("exp", ms[-1], 15)]


def _emit_wave_norm(nc, block, st, m, wave, ppv):
    sch, B = st.sch, st.B

    @block.vector
    def _(v):
        for h in range(wave * 8, wave * 8 + 8):
            sch.wait(v, "V", st.r[("pv", m, h)])
        if wave == 0:
            sch.wait(v, "V", st.r.get(("atr", m - 1, 7), None))
        i = v.reciprocal(out=B["st_t"][:, 8:16], in_=ppv[:, 64::128])
        sch.wait(v, "V", sch.inc("V", i))
        i = None
        for hh in range(8):
            h = wave * 8 + hh
            i = v.tensor_scalar(out=B["attn_tm"][:, h * 64:(h + 1) * 64],
                                in0=ppv[:, hh * 128: hh * 128 + 64],
                                scalar1=B["st_t"][:, 8 + hh:9 + hh], scalar2=None,
                                op0=ALU.mult)
        st.r[("anorm", m, wave * 8)] = sch.inc("V", i)


def _emit_ln(nc, block, st, src, res_ap, out_ap, key, dep):
    sch, B = st.sch, st.B
    r_buf = B["r2_t"]

    @block.vector
    def _(v):
        sch.wait(v, "V", dep)
        sch.wait(v, "V", st.last_ln_sq)
        i = v.scalar_tensor_tensor(out=r_buf, in0=src, scalar=1.0, in1=res_ap,
                                   op0=ALU.mult, op1=ALU.add,
                                   accum_out=B["st_t"][:, 0:1])
        st.r[(key, "r")] = sch.inc("V", i)

    _emit_ln_tail(nc, block, st, key, r_buf, out_ap)


def _emit_ln_tail(nc, block, st, key, r_buf, out_ap, pre_tok=None, sum2=False):
    sch, B = st.sch, st.B

    @block.scalar
    def _(s):
        sch.wait(s, "S", st.r[(key, "r")])
        i = s.activation(out=B["sq_t"], in_=r_buf, func=AF.Square,
                         bias=0.0, scale=1.0, accum_out=B["st_t"][:, 1:2])
        st.r[(key, "sq")] = sch.inc("S", i)
    st.last_ln_sq = st.r[(key, "sq")]

    @block.vector
    def _(v):
        sch.wait(v, "V", st.r[(key, "sq")])
        stt = B["st_t"]
        if sum2:
            i = v.tensor_tensor(out=stt[:, 0:1], in0=stt[:, 0:1],
                                in1=stt[:, 9:10], op=ALU.add)
            sch.wait(v, "V", sch.inc("V", i))
        i = v.tensor_scalar(out=stt[:, 2:3], in0=stt[:, 0:1], scalar1=1.0 / D,
                            scalar2=None, op0=ALU.mult)
        sch.wait(v, "V", sch.inc("V", i))
        i = v.tensor_tensor(out=stt[:, 3:4], in0=stt[:, 2:3], in1=stt[:, 2:3],
                            op=ALU.mult)
        sch.wait(v, "V", sch.inc("V", i))
        i = v.scalar_tensor_tensor(out=stt[:, 4:5], in0=stt[:, 1:2], scalar=1.0 / D,
                                   in1=stt[:, 3:4], op0=ALU.mult, op1=ALU.subtract)
        sch.wait(v, "V", sch.inc("V", i))
        i = v.tensor_scalar(out=stt[:, 4:5], in0=stt[:, 4:5], scalar1=1.0e-5,
                            scalar2=None, op0=ALU.add)
        st.r[(key, "v")] = sch.inc("V", i)

    @block.scalar
    def _(s):
        sch.wait(s, "S", st.r[(key, "v")])
        i = s.activation(out=B["st_t"][:, 5:6], in_=B["st_t"][:, 4:5],
                         func=AF.Sqrt, bias=0.0, scale=1.0)
        st.r[(key, "s0")] = sch.inc("S", i)

    @block.vector
    def _(v):
        sch.wait(v, "V", st.r[(key, "s0")])
        sch.wait(v, "V", pre_tok)
        stt = B["st_t"]
        i = v.reciprocal(out=stt[:, 6:7], in_=stt[:, 5:6])
        sch.wait(v, "V", sch.inc("V", i))
        if isinstance(out_ap, tuple):
            v.tensor_scalar(out=out_ap[0], in0=r_buf[:, 0:512], scalar1=stt[:, 2:3],
                            scalar2=stt[:, 6:7], op0=ALU.subtract, op1=ALU.mult)
            i = v.tensor_scalar(out=out_ap[1], in0=r_buf[:, 512:1024], scalar1=stt[:, 2:3],
                                scalar2=stt[:, 6:7], op0=ALU.subtract, op1=ALU.mult)
        else:
            i = v.tensor_scalar(out=out_ap, in0=r_buf, scalar1=stt[:, 2:3],
                                scalar2=stt[:, 6:7], op0=ALU.subtract, op1=ALU.mult)
        st.r[key] = sch.inc("V", i)


def _emit_B(nc, block, st, t, blocks):
    sch, H, B = st.sch, st.H, st.B
    nb = len(blocks)
    ntok = nb * 128
    bid = (t, blocks[0])

    if H.get("ydbg") is not None:
        @block.gpsimd
        def _(g):
            for m in blocks:
                sch.wait(g, "G", st.r[("y", m)])
                d = g.dma_start(out=H["ydbg"][m * 128:(m + 1) * 128, :],
                                in_=B["y_t"][:, (m % 4) * D:(m % 4 + 1) * D])
                sch.inc("DS", d)

    with nc.psum_tensor(f"ptr3_{t}_{blocks[0]}", [128, 1024], F32) as ptr3:
        for bi, m in enumerate(blocks):
            for dc in range(8):
                @block.tensor
                def _(te, m=m, bi=bi, dc=dc):
                    if dc == 0:
                        if bi == 0:
                            sch.wait(te, "T", st.a2_exp_last)
                        sch.wait(te, "T", st.r[("y", m)])
                    pb = dc % 2
                    prev = ("ytev", bid, bi, dc - 2) if dc >= 2 else ("ytev", bid, bi - 1, 6 + dc)
                    sch.wait(te, "T", st.r.get(prev, None))
                    i = te.transpose(ptr3[:, pb * 512: pb * 512 + 128],
                                     B["y_t"][:, (m % 4) * D + dc * 128:(m % 4) * D + (dc + 1) * 128],
                                     B["id_f32"][:])
                    st.r[("ytr", bid, bi, dc)] = sch.inc("T", i)

                @block.vector
                def _(v, m=m, bi=bi, dc=dc):
                    pb = dc % 2
                    col = (bi * 8 + dc) * 128
                    sch.wait(v, "V", st.r[("ytr", bid, bi, dc)])
                    i = v.tensor_copy(out=B["xt_hi"][:, col:col + 128],
                                      in_=ptr3[:, pb * 512: pb * 512 + 128])
                    i = v.scalar_tensor_tensor(
                        out=B["xt_lo"][:, col:col + 128],
                        in0=B["xt_hi"][:, col:col + 128], scalar=-1.0,
                        in1=ptr3[:, pb * 512: pb * 512 + 128], op0=ALU.mult, op1=ALU.add)
                    st.r[("ytev", bid, bi, dc)] = sch.inc("V", i)

    yt_ready = sch.tok("V")

    def emit_w1_stream(fc):
        sl = fc % 2

        @block.sync
        def _(y):
            if fc < 2:
                sch.wait(y, "Y", st.qks_last)
            sch.wait(y, "Y", st.r.get(("ff1mm", bid, fc - 2), None))
            d = y.dma_start(out=B["w1s_hi"][:, sl * 1024:(sl + 1) * 1024].rearrange(
                "p (dc m) -> p dc m", m=128), in_=H["w1_hi"][fc].rearrange("dc p m -> p dc m"))
            sch.inc("DL", d)
            d = y.dma_start(out=B["w1s_lo"][:, sl * 1024:(sl + 1) * 1024].rearrange(
                "p (dc m) -> p dc m", m=128), in_=H["w1_lo"][fc].rearrange("dc p m -> p dc m"))
            sch.inc("DL", d)
            d = y.dma_start(out=B["b1s_hi"][0:1, sl * 128:(sl + 1) * 128],
                            in_=H["b1r_hi"][fc])
            sch.inc("DL", d)
            d = y.dma_start(out=B["b1s_lo"][0:1, sl * 128:(sl + 1) * 128],
                            in_=H["b1r_lo"][fc])
            st.r[("w1w", bid, fc)] = sch.inc("DL", d)

    def emit_w2_stream(fc):
        sl = fc % 2

        @block.sync
        def _(y):
            if fc < 2:
                sch.wait(y, "Y", st.wvs_last)
            sch.wait(y, "Y", st.r.get(("ff2b", bid, fc - 2), None))
            d = y.dma_start(out=B["w2s_hi"][:, sl * D: (sl + 1) * D],
                            in_=H["w2_hi"][fc])
            sch.inc("DL", d)
            d = y.dma_start(out=B["w2s_lo"][:, sl * D: (sl + 1) * D],
                            in_=H["w2_lo"][fc])
            st.r[("w2w", bid, fc)] = sch.inc("DL", d)

    with nc.psum_tensor(f"ph_{t}_{blocks[0]}", [128, 2 * 512], F32) as ph, \
         nc.psum_tensor(f"po_{t}_{blocks[0]}", [128, 4 * 512], F32) as po, \
         nc.psum_tensor(f"po1_{t}_{blocks[0]}", [128, 2 * 512], F32) as po1:

        def emit_ff1_T(fc):
            sl = fc % 2

            @block.tensor
            def _(te):
                sch.wait(te, "T", st.r[("w1w", bid, fc)])
                sch.wait(te, "T", yt_ready)
                if fc >= 2:
                    sch.wait(te, "T", ("S", st.b_sbase
                                       + S_PER * ((fc - 2) // gf)
                                       + ((fc - 2) % gf) + 1))
                rh4 = B["xt_hi"][:, 0:nb * 8 * 128].rearrange(
                    "p (s dc m) -> p dc s m", dc=8, m=128)
                rl4 = B["xt_lo"][:, 0:nb * 8 * 128].rearrange(
                    "p (s dc m) -> p dc s m", dc=8, m=128)
                pp = ph[:, (fc % 2) * 512:(fc % 2) * 512 + ntok]
                for dc in range(8):
                    whi = B["w1s_hi"][:, sl * 1024 + dc * 128: sl * 1024 + (dc + 1) * 128]
                    wlo = B["w1s_lo"][:, sl * 1024 + dc * 128: sl * 1024 + (dc + 1) * 128]
                    te.matmul(pp, whi, rh4[:, dc], start=(dc == 0), stop=False)
                    te.matmul(pp, whi, rl4[:, dc], start=False, stop=False)
                    te.matmul(pp, wlo, rh4[:, dc], start=False, stop=False)
                te.matmul(pp, B["b1s_hi"][0:1, sl * 128:(sl + 1) * 128],
                          B["ones_bf"][0:1, 0:ntok], start=False, stop=False)
                i = te.matmul(pp, B["b1s_lo"][0:1, sl * 128:(sl + 1) * 128],
                              B["ones_bf"][0:1, 0:ntok], start=False, stop=True)
                st.r[("ff1mm", bid, fc)] = sch.inc("T", i)

        def rslot(g, k):
            return (RK.index(k) % 4) * 512

        def s_cnt(g, off):
            return ("S", st.b_sbase + S_PER * g + off)

        def v_cnt(g, off):
            return ("V", st.b_vbase + V_PER * g + off)

        def emit_epi_S():
            @block.scalar
            def _(s):
                sch.wait(s, "S", st.last_store)
                for fc in range(32):
                    p5 = (fc % 2) * 512
                    sch.wait(s, "S", v_cnt(fc - 1, 20))
                    sch.wait(s, "S", st.r[("ff1mm", bid, fc)])
                    pp = ph[:, (fc % 2) * 512:(fc % 2) * 512 + ntok]
                    i = s.activation(out=B["s1_t"][:, p5:p5 + ntok], in_=pp,
                                     func=AF.Identity, bias=B["pmag"][:, 0:1],
                                     scale=256.0)
                    sch.inc("S", i)
                    uap = B["u0"] if fc % 2 == 0 else B["u1"]
                    i = s.activation(out=uap[:, 0:ntok],
                                     in_=B["s1_t"][:, p5:p5 + ntok],
                                     func=AF.Abs, bias=B["nmag"][:, 0:1], scale=1.0)
                    sch.inc("S", i)
                    for k in RK:
                        i = s.activation(
                            out=B["rk_t"][:, rslot(fc, k): rslot(fc, k) + ntok],
                            in_=uap[:, 0:ntok], func=AF.Relu,
                            bias=B["kinkb"][:, k:k + 1],
                            scale=(1.0 if k in TK else abs(KINK_C[k])))
                        sch.inc("S", i)

        def emit_epi_V(g):
            @block.vector
            def _(v):
                p5 = (g % 2) * 512
                uap = B["u0"] if g % 2 == 0 else B["u1"]
                sch.wait(v, "V", s_cnt(g, gf))
                i = v.tensor_scalar(out=B["acc_t"][:, 0:ntok_g],
                                    in0=B["s1_t"][:, p5:p5 + ntok_g],
                                    scalar1=0.5, scalar2=-MAGIC / 2,
                                    op0=ALU.mult, op1=ALU.add)
                sch.inc("V", i)
                sch.wait(v, "V", s_cnt(g, gf + 1))
                i = v.scalar_tensor_tensor(out=B["acc_t"][:, 0:ntok_g],
                                           in0=uap[:, 0:ntok_g],
                                           scalar=A0, in1=B["acc_t"][:, 0:ntok_g],
                                           op0=ALU.mult, op1=ALU.add)
                sch.inc("V", i)
                for idx, k in enumerate(RK):
                    sch.wait(v, "V", s_cnt(g, gf + 2 + idx))
                    i = v.scalar_tensor_tensor(
                        out=B["acc_t"][:, 0:ntok_g],
                        in0=B["rk_t"][:, rslot(g, k): rslot(g, k) + ntok_g],
                        scalar=(KINK_C[k] if k in TK
                                else (1.0 if KINK_C[k] > 0 else -1.0)),
                        in1=B["acc_t"][:, 0:ntok_g], op0=ALU.mult, op1=ALU.add)
                    sch.inc("V", i)
                i = v.tensor_scalar(out=B["a_t"][:, g * 512: g * 512 + ntok_g],
                                    in0=B["acc_t"][:, 0:ntok_g], scalar1=MAGIC,
                                    scalar2=-MAGIC, op0=ALU.add, op1=ALU.add)
                st.r[("aq", bid, g)] = sch.inc("V", i)

        nb1 = min(2, nb)
        gf = 1
        ngrp = 32 // gf
        ntok_g = ntok * gf
        S_PER = gf + 1 + len(RK)
        V_PER = 20

        def a_slice(fc, sub):
            if gf == 1:
                return B["a_t"][:, fc * 512 + sub * 128: fc * 512 + (sub + 1) * 128]
            return B["a_t"][:, fc * 128: (fc + 1) * 128]

        def emit_ff2a(fc):
            sl2 = fc % 2

            @block.tensor
            def _(te):
                sch.wait(te, "T", ("V", st.b_vbase + V_PER * (fc // gf) + 20))
                sch.wait(te, "T", st.r[("w2w", bid, fc)])
                i = None
                for sub in range(nb):
                    a_ap = a_slice(fc, sub)
                    te.matmul(po[:, sub * 512:(sub + 1) * 512], a_ap,
                              B["w2s_hi"][:, sl2 * D: sl2 * D + 512],
                              start=(fc == 0), stop=False)
                    i = te.matmul(po[:, sub * 512:(sub + 1) * 512], a_ap,
                                  B["w2s_lo"][:, sl2 * D: sl2 * D + 512],
                                  start=False, stop=(fc == 31))
                    if fc == 0:
                        te.matmul(po[:, sub * 512:(sub + 1) * 512],
                                  B["ones_f16"][0:1, 0:128],
                                  B["b2r_hi"][0:1, 0:512],
                                  start=False, stop=False)
                        i = te.matmul(po[:, sub * 512:(sub + 1) * 512],
                                      B["ones_f16"][0:1, 0:128],
                                      B["b2r_lo"][0:1, 0:512],
                                      start=False, stop=False)
                st.r[("ff2a", bid, fc)] = sch.inc("T", i)

        def emit_ff2b(fc):
            sl2 = fc % 2

            @block.tensor
            def _(te):
                i = None
                for sub in range(nb1):
                    a_ap = a_slice(fc, sub)
                    te.matmul(po1[:, sub * 512:(sub + 1) * 512], a_ap,
                              B["w2s_hi"][:, sl2 * D + 512: (sl2 + 1) * D],
                              start=(fc == 0), stop=False)
                    i = te.matmul(po1[:, sub * 512:(sub + 1) * 512], a_ap,
                                  B["w2s_lo"][:, sl2 * D + 512: (sl2 + 1) * D],
                                  start=False, stop=(fc == 31))
                    if fc == 0:
                        te.matmul(po1[:, sub * 512:(sub + 1) * 512],
                                  B["ones_f16"][0:1, 0:128],
                                  B["b2r_hi"][0:1, 512:1024],
                                  start=False, stop=False)
                        i = te.matmul(po1[:, sub * 512:(sub + 1) * 512],
                                      B["ones_f16"][0:1, 0:128],
                                      B["b2r_lo"][0:1, 512:1024],
                                      start=False, stop=False)
                st.r[("ff2b", bid, fc)] = sch.inc("T", i)

        st.b_sbase = sch.count["S"]
        st.b_vbase = sch.count["V"]
        lag = 2 * gf
        emit_w2_stream(0)
        emit_w2_stream(1)
        for pr in range(16):
            f0 = 2 * pr
            emit_w1_stream(f0)
            emit_w1_stream(f0 + 1)
            emit_ff1_T(f0)
            emit_ff1_T(f0 + 1)
            if f0 >= lag:
                emit_ff2a(f0 - lag)
                emit_ff2b(f0 - lag)
                emit_ff2a(f0 - lag + 1)
                emit_ff2b(f0 - lag + 1)
                if f0 - lag + 2 < 32:
                    emit_w2_stream(f0 - lag + 2)
                if f0 - lag + 3 < 32:
                    emit_w2_stream(f0 - lag + 3)
        for fc in range(32 - lag, 32):
            emit_ff2a(fc)
            emit_ff2b(fc)
            if fc + 2 < 32:
                emit_w2_stream(fc + 2)
        emit_epi_S()
        for g in range(ngrp):
            emit_epi_V(g)

        for sub, m in enumerate(blocks):
            @block.scalar
            def _(s, sub=sub, m=m):
                sch.wait(s, "S", st.r[("ff2a", bid, 31)])
                if sub > 0:
                    sch.wait(s, "S", st.r.get(("oqa", blocks[sub - 1]), None))
                i = s.activation(out=B["s3_t"], in_=po[:, sub * 512:(sub + 1) * 512],
                                 func=AF.Identity, bias=B["pmag"][:, 0:1], scale=1.0)
                st.r[("s3a", m)] = sch.inc("S", i)

            @block.vector
            def _(v, sub=sub, m=m):
                sch.wait(v, "V", st.r[("s3a", m)])
                i = v.tensor_scalar(out=B["oh16"][:, sub * 512:(sub + 1) * 512],
                                    in0=B["s3_t"], scalar1=-MAGIC,
                                    scalar2=None, op0=ALU.add)
                st.r[("oqa", m)] = sch.inc("V", i)

        def emit_ln2(sub, m, src_po, gate_key):
            @block.scalar
            def _(s, sub=sub, m=m):
                sch.wait(s, "S", st.r[gate_key])
                if sub == 0:
                    sch.wait(s, "S", st.r[("oqa", blocks[-1])])
                sch.wait(s, "S", st.r.get(("r2", m - 1), None))
                i = s.activation(out=B["s3_t"], in_=src_po,
                                 func=AF.Identity, bias=B["pmag"][:, 0:1], scale=1.0)
                st.r[("s3b", m)] = sch.inc("S", i)

            @block.vector
            def _(v, sub=sub, m=m):
                sch.wait(v, "V", st.r[("s3b", m)])
                sch.wait(v, "V", st.last_ln_sq)
                sch.wait(v, "V", st.r.get(("ff2c", bid, 15), None))
                v.tensor_scalar(out=B["u0"][:, 0:512], in0=B["s3_t"],
                                scalar1=-MAGIC, scalar2=1.0 / 256.0,
                                op0=ALU.add, op1=ALU.mult)
                v.scalar_tensor_tensor(
                    out=B["r2_t"][:, 0:512], in0=B["oh16"][:, sub * 512:(sub + 1) * 512],
                    scalar=1.0 / 256.0, in1=B["y_t"][:, (m % 4) * D:(m % 4) * D + 512],
                    op0=ALU.mult, op1=ALU.add, accum_out=B["st_t"][:, 0:1])
                i = v.scalar_tensor_tensor(
                    out=B["r2_t"][:, 512:1024], in0=B["u0"][:, 0:512],
                    scalar=1.0, in1=B["y_t"][:, (m % 4) * D + 512:(m % 4 + 1) * D],
                    op0=ALU.mult, op1=ALU.add, accum_out=B["st_t"][:, 9:10])
                st.r[(("ln2", m), "r")] = sch.inc("V", i)
                st.r[("r2", m)] = st.r[(("ln2", m), "r")]

            _emit_ln_tail(nc, block, st, ("ln2", m), B["r2_t"],
                          (B["s1_t"][:, 0:512], B["s1_t"][:, 512:1024]),
                          pre_tok=st.last_store, sum2=True)
            st.r[("ob", m)] = st.r[("ln2", m)]

            @block.gpsimd
            def _(g, m=m):
                sch.wait(g, "G", st.r[("ob", m)])
                d = g.dma_start(out=st.H["out"][m * 128:(m + 1) * 128, 0:512],
                                in_=B["s1_t"][:, 0:512])
                sch.inc("DS", d)
                d = g.dma_start(out=st.H["out"][m * 128:(m + 1) * 128, 512:1024],
                                in_=B["s1_t"][:, 512:1024])
                st.last_store = sch.inc("DS", d)

        if nb > 2:
            for fc in range(32):
                sl = fc % 2

                @block.sync
                def _(y, fc=fc, sl=sl):
                    if fc < 2:
                        sch.wait(y, "Y", st.r[("ff2b", bid, 30 + sl)])
                    sch.wait(y, "Y", st.r.get(("ff2c", bid, fc - 2), None))
                    d = y.dma_start(out=B["w2s_hi"][:, sl * D: sl * D + 512],
                                    in_=H["w2_hi"][fc][:, 512:1024])
                    sch.inc("DL", d)
                    d = y.dma_start(out=B["w2s_lo"][:, sl * D: sl * D + 512],
                                    in_=H["w2_lo"][fc][:, 512:1024])
                    st.r[("w2wc", bid, fc)] = sch.inc("DL", d)

                @block.tensor
                def _(te, fc=fc, sl=sl):
                    if fc == 0:
                        sch.wait(te, "T", st.r[("s3a", blocks[-1])])
                    sch.wait(te, "T", st.r[("w2wc", bid, fc)])
                    i = None
                    for j, sub in enumerate(range(2, nb)):
                        a_ap = a_slice(fc, sub)
                        te.matmul(po[:, j * 512:(j + 1) * 512], a_ap,
                                  B["w2s_hi"][:, sl * D: sl * D + 512],
                                  start=(fc == 0), stop=False)
                        i = te.matmul(po[:, j * 512:(j + 1) * 512], a_ap,
                                      B["w2s_lo"][:, sl * D: sl * D + 512],
                                      start=False, stop=(fc == 31))
                        if fc == 0:
                            te.matmul(po[:, j * 512:(j + 1) * 512],
                                      B["ones_f16"][0:1, 0:128],
                                      B["b2r_hi"][0:1, 512:1024],
                                      start=False, stop=False)
                            i = te.matmul(po[:, j * 512:(j + 1) * 512],
                                          B["ones_f16"][0:1, 0:128],
                                          B["b2r_lo"][0:1, 512:1024],
                                          start=False, stop=False)
                    st.r[("ff2c", bid, fc)] = sch.inc("T", i)

            st.wvs_last = st.r[("ff2c", bid, 31)]
        else:
            st.wvs_last = st.r[("ff2b", bid, 31)]
        for sub in range(nb1):
            emit_ln2(sub, blocks[sub], po1[:, sub * 512:(sub + 1) * 512],
                     ("ff2b", bid, 31))
        for j, sub in enumerate(range(2, nb)):
            emit_ln2(sub, blocks[sub], po[:, j * 512:(j + 1) * 512],
                     ("ff2c", bid, 31))
        st.qks_last = st.r[("ff1mm", bid, 31)]

        st.b_s3b = st.r[("s3b", blocks[-1])]
        st.b_s1last = ("S", st.b_sbase + S_PER * (ngrp - 1) + gf)
        st.b_vdone = sch.tok("V")


def kernel(L0_attn_x, L0_attn_Wqkv, L0_attn_bqkv, L0_attn_Wout, L0_attn_bout,
           L0_ffn_w1, L0_ffn_b1, L0_ffn_w2, L0_ffn_b2):
    nc = build()
    in_maps = []
    for c in range(8):
        in_maps.append(dict(
            x=np.ascontiguousarray(L0_attn_x[c]),
            wqkv=np.ascontiguousarray(L0_attn_Wqkv),
            bqkv=np.ascontiguousarray(L0_attn_bqkv),
            wout=np.ascontiguousarray(L0_attn_Wout),
            bout=np.ascontiguousarray(L0_attn_bout),
            w1=np.ascontiguousarray(L0_ffn_w1),
            b1=np.ascontiguousarray(L0_ffn_b1),
            w2=np.ascontiguousarray(L0_ffn_w2),
            b2=np.ascontiguousarray(L0_ffn_b2),
        ))
    res = run_bass_kernel_spmd(nc, in_maps, core_ids=list(range(8)))
    return np.stack([r["out"] for r in res.results], axis=0)


# version 18
# speedup vs baseline: 1.0185x; 1.0185x over previous
import math
import numpy as np
import concourse.bass as bass
import concourse.mybir as mybir
from concourse.bass_utils import run_bass_kernel_spmd

F32 = mybir.dt.float32
BF16 = mybir.dt.bfloat16
F16 = mybir.dt.float16
AF = mybir.ActivationFunctionType
ALU = mybir.AluOpType
MAGIC = float(3 * 2 ** 22)

D = 1024
FF = 4096
NB = 32
NT = 8
SCALE = 1.0 / 32.0


def _pwl_params():
    X = np.linspace(-4.0, 4.0, 33).astype(np.float32)
    erf = np.vectorize(math.erf)
    Y = (0.5 * X * (1.0 + erf(X.astype(np.float64) / math.sqrt(2.0)))).astype(np.float32)
    Gh = np.array([256.0 * (float(Y[16 + k]) - 0.125 * k) for k in range(17)])
    s = [(Gh[k + 1] - Gh[k]) / 64.0 for k in range(16)]
    s16 = 512.5 - Gh[16]
    Ks = [64.0 * k for k in range(1, 16)] + [1024.0, 1025.0]
    cs = [s[k] - s[k - 1] for k in range(1, 16)] + [s16 - s[15], 0.5 - s16]
    return float(s[0]), [float(k) for k in Ks], [float(c) for c in cs]


A0, KINK_K, KINK_C = _pwl_params()
NK = len(KINK_K)
GK = [5, 6, 7, 8]
TK = [15, 16]
RK = list(range(NK))
assert all(KINK_C[k] < 0 for k in GK)


class Sch:
    def __init__(self, nc, ctx):
        self.nc = nc
        self.sem = {}
        self.count = {}
        self.last_wait = {w: {} for w in ("T", "V", "S", "G", "Y")}
        for name in ("T", "V", "S", "G", "DL", "DS"):
            self.sem[name] = ctx.enter_context(nc.semaphore("sem_" + name))
            self.count[name] = 0

    def inc(self, name, inst):
        step = 16 if name in ("DL", "DS") else 1
        inst.then_inc(self.sem[name], step)
        self.count[name] += step
        return (name, self.count[name])

    def tok(self, name):
        return (name, self.count[name])

    def wait(self, eng, wname, token):
        if not token:
            return
        name, val = token
        if val <= 0:
            return
        lw = self.last_wait[wname]
        if lw.get(name, 0) >= val:
            return
        eng.wait_ge(self.sem[name], val)
        lw[name] = val


class St:
    def __init__(self, sch, H, B):
        self.sch = sch
        self.H = H
        self.B = B
        self.r = {}
        self.last_store = None
        self.last_ln_sq = None
        self.wvs_last = None
        self.qks_last = None
        self.b_s3b = None
        self.b_s1last = None
        self.b_vdone = None
        self.a2_ln_last = None
        self.a2_exp_last = None
        self.sc_guard = None
        self.pv_guard = None


def _barrier(block, st, include_y=False):
    sch = st.sch
    toks = [sch.tok(n) for n in ("T", "V", "S", "G", "DL", "DS")]
    pairs = [("T", block.tensor), ("V", block.vector), ("S", block.scalar),
             ("G", block.gpsimd)]
    if include_y:
        pairs.append(("Y", block.sync))
    for name, section in pairs:
        @section
        def _(e, _t=toks, _n=name):
            for tk in _t:
                sch.wait(e, _n, tk)


def build(limit_tiles=NT, phases="ab", debug=False):
    nc = bass.Bass("TRN2", target_bir_lowering=False, detect_race_conditions=False)

    H = {}
    H["x"] = nc.dram_tensor("x", [4096, D], F32, kind="ExternalInput").ap()
    H["wqkv"] = nc.dram_tensor("wqkv", [D, 3 * D], F32, kind="ExternalInput").ap()
    H["bqkv"] = nc.dram_tensor("bqkv", [3 * D], F32, kind="ExternalInput").ap()
    H["wout"] = nc.dram_tensor("wout", [D, D], F32, kind="ExternalInput").ap()
    H["bout"] = nc.dram_tensor("bout", [D], F32, kind="ExternalInput").ap()
    H["w1"] = nc.dram_tensor("w1", [D, FF], F32, kind="ExternalInput").ap()
    H["b1"] = nc.dram_tensor("b1", [FF], F32, kind="ExternalInput").ap()
    H["w2"] = nc.dram_tensor("w2", [FF, D], F32, kind="ExternalInput").ap()
    H["b2"] = nc.dram_tensor("b2", [D], F32, kind="ExternalInput").ap()
    H["out"] = nc.dram_tensor("out", [4096, D], F32, kind="ExternalOutput").ap()
    H["ydbg"] = nc.dram_tensor("ydbg", [4096, D], F32, kind="ExternalOutput").ap() if debug else None

    H["qk_hi"] = nc.dram_tensor("qk_hi", [16, 8, 128, 128], BF16).ap()
    H["qk_lo"] = nc.dram_tensor("qk_lo", [16, 8, 128, 128], BF16).ap()
    H["wv_hi"] = nc.dram_tensor("wv_hi", [8, 128, D], BF16).ap()
    H["wv_lo"] = nc.dram_tensor("wv_lo", [8, 128, D], BF16).ap()
    H["wo_hi"] = nc.dram_tensor("wo_hi", [8, 128, D], BF16).ap()
    H["wo_lo"] = nc.dram_tensor("wo_lo", [8, 128, D], BF16).ap()
    H["w1_hi"] = nc.dram_tensor("w1_hi", [32, 8, 128, 128], BF16).ap()
    H["w1_lo"] = nc.dram_tensor("w1_lo", [32, 8, 128, 128], BF16).ap()
    H["w2_hi"] = nc.dram_tensor("w2_hi", [32, 128, D], F16).ap()
    H["w2_lo"] = nc.dram_tensor("w2_lo", [32, 128, D], F16).ap()
    H["b1r_hi"] = nc.dram_tensor("b1r_hi", [32, 1, 128], BF16).ap()
    H["b1r_lo"] = nc.dram_tensor("b1r_lo", [32, 1, 128], BF16).ap()

    from contextlib import ExitStack
    es = ExitStack()

    def sb(name, shape, dt=F32):
        return es.enter_context(nc.sbuf_tensor(name, shape, dt))

    B = {}
    B["x_t"] = None
    B["q_t"] = None
    B["bqk_sb"] = sb("bqk_sb", [128, 16])
    B["bvr_hi"] = sb("bvr_hi", [1, D], BF16)
    B["bvr_lo"] = sb("bvr_lo", [1, D], BF16)
    B["bor_hi"] = sb("bor_hi", [1, D], BF16)
    B["bor_lo"] = sb("bor_lo", [1, D], BF16)
    B["b2r_hi"] = sb("b2r_hi", [1, D], F16)
    B["b2r_lo"] = sb("b2r_lo", [1, D], F16)
    B["iota_p"] = sb("iota_p", [128, 1])
    B["iota_f"] = sb("iota_f", [128, 128])
    B["id_f32"] = sb("id_f32", [128, 128])
    B["id_bf"] = sb("id_bf", [128, 128], BF16)
    B["m0t"] = sb("m0t", [128, 128], BF16)
    B["m2t"] = sb("m2t", [128, 128], BF16)
    B["ones_bf"] = sb("ones_bf", [1, 512], BF16)
    B["ones_f16"] = sb("ones_f16", [1, 128], F16)
    B["kinkb"] = sb("kinkb", [128, NK])
    B["pmag"] = sb("pmag", [128, 1])
    B["nmag"] = sb("nmag", [128, 1])
    B["st_t"] = sb("st_t", [128, 16])

    with es, nc.Block() as block:
        sch = Sch(nc, es)
        st = St(sch, H, B)
        _emit_consts(nc, block, st)
        _emit_prep(nc, block, st)
        _barrier(block, st, include_y=True)

        B["x_t"] = sb("x_t", [128, 5 * D])
        B["y_t"] = sb("y_t", [128, 4 * D])
        B["q_t"] = sb("q_t", [128, 8 * 5 * 128])
        B["k_t"] = sb("k_t", [128, 8 * 6 * 128])
        B["vt_t"] = sb("vt_t", [128, 6 * 1040])
        B["xt_hi"] = sb("xt_hi", [128, 4 * 8 * 128], BF16)
        B["xt_lo"] = sb("xt_lo", [128, 4 * 8 * 128], BF16)
        B["at_hi"] = sb("at_hi", [128, 8 * 128], BF16)
        B["at_lo"] = sb("at_lo", [128, 8 * 128], BF16)
        B["attn_tm"] = sb("attn_tm", [128, D])
        B["e_t"] = sb("e_t", [128, 2 * 384])
        B["a_t"] = sb("a_t", [128, 32 * 512], F16)
        B["qks_hi"] = sb("qks_hi", [128, 2 * 1024], BF16)
        B["qks_lo"] = sb("qks_lo", [128, 2 * 1024], BF16)
        B["wvs_hi"] = sb("wvs_hi", [128, 2 * D], BF16)
        B["wvs_lo"] = sb("wvs_lo", [128, 2 * D], BF16)
        B["w1s_hi"] = B["qks_hi"]
        B["w1s_lo"] = B["qks_lo"]
        B["w2s_hi"] = B["wvs_hi"][:].bitcast(F16)
        B["w2s_lo"] = B["wvs_lo"][:].bitcast(F16)
        B["b1s_hi"] = sb("b1s_hi", [1, 2 * 128], BF16)
        B["b1s_lo"] = sb("b1s_lo", [1, 2 * 128], BF16)
        B["s1_t"] = sb("s1_t", [128, 2 * 512])
        B["rk_t"] = sb("rk_t", [128, 2 * 1024])
        B["s3_t"] = B["e_t"][:, 0:512]
        B["r2_t"] = B["a_t"][:, 0:2048].bitcast(F32)
        B["sq_t"] = B["a_t"][:, 2048:4096].bitcast(F32)
        B["oh16"] = B["attn_tm"][:].bitcast(F16)
        B["acc_t"] = B["attn_tm"][:, 0:512]
        B["t1_t"] = B["attn_tm"][:, 512:1024]
        B["u_t"] = None
        B["u0"] = B["at_hi"][:].bitcast(F32)
        B["u1"] = B["at_lo"][:].bitcast(F32)
        @block.vector
        def _(v):
            i = None
            for s in range(6):
                i = v.memset(B["vt_t"][:, s * 1040 + 64: (s + 1) * 1040: 65], 1.0)
            sch.inc("V", i)
        for t in range(limit_tiles):
            _emit_A1(nc, block, st, t, sub=("all" if phases not in ("x", "tr", "tr1", "tr1a", "qk") else phases))
            if phases in ("a1", "x", "tr", "tr1", "tr1a", "qk"):
                _barrier(block, st)
                continue
            _emit_A2(nc, block, st, t, ms=(None if t < NT - 1 else list(range(27, 31))))
            if phases == "a2":
                _barrier(block, st)
                continue
            if t == NT - 1:
                _emit_B(nc, block, st, t, list(range(27, 31)))
                _barrier(block, st)
                _emit_A2(nc, block, st, t, ms=[31], tag="x")
                _barrier(block, st)
                _emit_B(nc, block, st, t, [31])
            else:
                _emit_B(nc, block, st, t, list(range(max(0, 4 * t - 1), 4 * t + 3)))

        @block.gpsimd
        def _(g):
            g.wait_ge(sch.sem["DS"], sch.count["DS"])
    return nc


def _emit_consts(nc, block, st):
    sch, B = st.sch, st.B

    @block.gpsimd
    def _(g):
        g.iota(B["iota_p"][:], pattern=[[0, 1]], channel_multiplier=1,
               allow_small_or_imprecise_dtypes=True)
        i = g.iota(B["iota_f"][:], pattern=[[1, 128]], channel_multiplier=0,
                   allow_small_or_imprecise_dtypes=True)
        sch.inc("G", i)

    @block.vector
    def _(v):
        sch.wait(v, "V", sch.tok("G"))
        v.tensor_scalar(out=B["id_f32"][:], in0=B["iota_f"][:],
                        scalar1=B["iota_p"][:, 0:1], scalar2=None, op0=ALU.is_equal)
        v.tensor_copy(out=B["id_bf"][:], in_=B["id_f32"][:])
        v.tensor_scalar(out=B["m0t"][:], in0=B["iota_f"][:],
                        scalar1=B["iota_p"][:, 0:1], scalar2=-1.0e30,
                        op0=ALU.is_lt, op1=ALU.mult)
        v.tensor_scalar(out=B["m2t"][:], in0=B["iota_f"][:],
                        scalar1=B["iota_p"][:, 0:1], scalar2=-1.0e30,
                        op0=ALU.is_gt, op1=ALU.mult)
        v.memset(B["ones_bf"][:], 1.0)
        v.memset(B["ones_f16"][:], 1.0)
        for k in range(NK):
            v.memset(B["kinkb"][:, k:k + 1],
                     -KINK_K[k] if k in TK else -abs(KINK_C[k]) * KINK_K[k])
        v.memset(B["pmag"][:], MAGIC)
        i = v.memset(B["nmag"][:], -MAGIC)
        sch.inc("V", i)


def _emit_prep(nc, block, st):
    sch, H, B = st.sch, st.H, st.B
    with nc.sbuf_tensor("pf0", [128, 4096], F32) as pf0, \
         nc.sbuf_tensor("pf1", [128, 4096], F32) as pf1, \
         nc.sbuf_tensor("ph0", [128, 4096], BF16) as ph0, \
         nc.sbuf_tensor("ph1", [128, 4096], BF16) as ph1, \
         nc.sbuf_tensor("pl0", [128, 4096], BF16) as pl0, \
         nc.sbuf_tensor("pl1", [128, 4096], BF16) as pl1, \
         nc.sbuf_tensor("ph16", [128, 2 * 1024], F16) as ph16, \
         nc.sbuf_tensor("pl16", [128, 2 * 1024], F16) as pl16, \
         nc.sbuf_tensor("brow", [1, 4096], F32) as brow, \
         nc.sbuf_tensor("brh", [1, 4096], BF16) as brh, \
         nc.sbuf_tensor("brl", [1, 4096], BF16) as brl:

        pf = [pf0, pf1]
        phh = [ph0, ph1]
        pll = [pl0, pl1]
        jobs = []
        for dc in range(8):
            jobs.append((H["wqkv"][dc * 128:(dc + 1) * 128, 0:2048], 2048,
                         H["qk_hi"][:, dc], H["qk_lo"][:, dc], "cc"))
        for dc in range(8):
            jobs.append((H["wqkv"][dc * 128:(dc + 1) * 128, 2048:3072], 1024,
                         H["wv_hi"][dc], H["wv_lo"][dc], None))
        for dc in range(8):
            jobs.append((H["wout"][dc * 128:(dc + 1) * 128, :], 1024,
                         H["wo_hi"][dc], H["wo_lo"][dc], None))
        for dc in range(8):
            jobs.append((H["w1"][dc * 128:(dc + 1) * 128, :], 4096,
                         H["w1_hi"][:, dc], H["w1_lo"][:, dc], "fc"))
        for fc in range(32):
            jobs.append((H["w2"][fc * 128:(fc + 1) * 128, :], 1024,
                         H["w2_hi"][fc], H["w2_lo"][fc], "f16"))

        for j, (src, n, hid, lod, mode) in enumerate(jobs):
            p = j % 2

            @block.sync
            def _(y, j=j, p=p, src=src, n=n):
                sch.wait(y, "Y", st.r.get(("pst", j - 2), None))
                d = y.dma_start(out=pf[p][:, 0:n], in_=src)
                st.r[("pld", j)] = sch.inc("DL", d)

            @block.vector
            def _(v, j=j, p=p, n=n, mode=mode):
                sch.wait(v, "V", st.r[("pld", j)])
                sch.wait(v, "V", st.r.get(("pst", j - 2), None))
                if mode == "f16":
                    v.tensor_copy(out=ph16[:, p * 1024:p * 1024 + n], in_=pf[p][:, 0:n])
                    i = v.scalar_tensor_tensor(
                        out=pl16[:, p * 1024:p * 1024 + n],
                        in0=ph16[:, p * 1024:p * 1024 + n], scalar=-1.0,
                        in1=pf[p][:, 0:n], op0=ALU.mult, op1=ALU.add)
                else:
                    v.tensor_copy(out=phh[p][:, 0:n], in_=pf[p][:, 0:n])
                    i = v.scalar_tensor_tensor(
                        out=pll[p][:, 0:n], in0=phh[p][:, 0:n], scalar=-1.0,
                        in1=pf[p][:, 0:n], op0=ALU.mult, op1=ALU.add)
                st.r[("psp", j)] = sch.inc("V", i)

            @block.gpsimd
            def _(g, j=j, p=p, n=n, hid=hid, lod=lod, mode=mode):
                sch.wait(g, "G", st.r[("psp", j)])
                if mode == "f16":
                    shi = ph16[:, p * 1024:p * 1024 + n]
                    slo = pl16[:, p * 1024:p * 1024 + n]
                else:
                    shi = phh[p][:, 0:n]
                    slo = pll[p][:, 0:n]
                if mode in ("cc", "fc"):
                    shi = shi.rearrange("p (c m) -> p c m", m=128)
                    slo = slo.rearrange("p (c m) -> p c m", m=128)
                    hid = hid.rearrange("c p m -> p c m")
                    lod = lod.rearrange("c p m -> p c m")
                d = g.dma_start(out=hid, in_=shi)
                sch.inc("DS", d)
                d = g.dma_start(out=lod, in_=slo)
                st.r[("pst", j)] = sch.inc("DS", d)

        @block.sync
        def _(y):
            sch.wait(y, "Y", st.r[("pst", len(jobs) - 1)])
            for cc in range(16):
                d = y.dma_start(out=B["bqk_sb"][:, cc:cc + 1],
                                in_=H["bqkv"][cc * 128:(cc + 1) * 128])
                sch.inc("DL", d)
            d = y.dma_start(out=brow[0:1, 0:1024], in_=H["bqkv"][2048:3072])
            sch.inc("DL", d)
            d = y.dma_start(out=brow[0:1, 1024:2048], in_=H["bout"])
            sch.inc("DL", d)
            d = y.dma_start(out=brow[0:1, 2048:3072], in_=H["b2"])
            sch.inc("DL", d)

        @block.vector
        def _(v):
            sch.wait(v, "V", sch.tok("DL"))
            v.tensor_copy(out=B["bvr_hi"][:], in_=brow[0:1, 0:1024])
            v.scalar_tensor_tensor(out=B["bvr_lo"][:], in0=B["bvr_hi"][:], scalar=-1.0,
                                   in1=brow[0:1, 0:1024], op0=ALU.mult, op1=ALU.add)
            v.tensor_copy(out=B["bor_hi"][:], in_=brow[0:1, 1024:2048])
            v.scalar_tensor_tensor(out=B["bor_lo"][:], in0=B["bor_hi"][:], scalar=-1.0,
                                   in1=brow[0:1, 1024:2048], op0=ALU.mult, op1=ALU.add)
            v.tensor_scalar(out=brow[0:1, 3072:4096], in0=brow[0:1, 2048:3072],
                            scalar1=256.0, scalar2=None, op0=ALU.mult)
            v.tensor_copy(out=B["b2r_hi"][:], in_=brow[0:1, 3072:4096])
            i = v.scalar_tensor_tensor(out=B["b2r_lo"][:], in0=B["b2r_hi"][:], scalar=-1.0,
                                       in1=brow[0:1, 3072:4096], op0=ALU.mult, op1=ALU.add)
            st.r[("bprep",)] = sch.inc("V", i)

        @block.sync
        def _(y):
            sch.wait(y, "Y", st.r[("bprep",)])
            d = y.dma_start(out=brow[0:1, 0:4096], in_=H["b1"])
            sch.inc("DL", d)

        @block.vector
        def _(v):
            sch.wait(v, "V", sch.tok("DL"))
            v.tensor_copy(out=brh[:], in_=brow[0:1, 0:4096])
            i = v.scalar_tensor_tensor(out=brl[:], in0=brh[:], scalar=-1.0,
                                       in1=brow[0:1, 0:4096], op0=ALU.mult, op1=ALU.add)
            st.r[("b1sp",)] = sch.inc("V", i)

        @block.gpsimd
        def _(g):
            sch.wait(g, "G", st.r[("b1sp",)])
            d = g.dma_start(out=H["b1r_hi"].rearrange("fc o m -> o fc m"),
                            in_=brh[:].rearrange("o (fc m) -> o fc m", m=128))
            sch.inc("DS", d)
            d = g.dma_start(out=H["b1r_lo"].rearrange("fc o m -> o fc m"),
                            in_=brl[:].rearrange("o (fc m) -> o fc m", m=128))
            sch.inc("DS", d)


def _emit_A1(nc, block, st, t, sub="all"):
    sch, H, B = st.sch, st.H, st.B
    blocks = list(range(4 * t, 4 * t + 4))

    @block.sync
    def _(y):
        sch.wait(y, "Y", sch.tok("DS"))
        for n in blocks:
            sch.wait(y, "Y", st.r.get((("ln1", n - 5), "r"), None))
            d = y.dma_start(out=B["x_t"][:, (n % 5) * D:(n % 5 + 1) * D],
                            in_=H["x"][n * 128:(n + 1) * 128, :])
            st.r[("x", n)] = sch.inc("DL", d)

    if sub == "x":
        return
    with nc.psum_tensor(f"ptr_{t}", [128, 1024], F32) as ptr:
        for bi, n in enumerate(blocks):
            if sub == "tr1" and bi > 0:
                break
            for dc in range(8):
                if sub in ("tr1", "tr1a") and dc >= 2:
                    break
                @block.tensor
                def _(te, n=n, dc=dc, bi=bi):
                    if dc == 0:
                        if bi == 0:
                            sch.wait(te, "T", st.b_s1last)
                        sch.wait(te, "T", st.r[("x", n)])
                    pb = dc % 2
                    prev = ("trev", n, dc - 2) if dc >= 2 else ("trev", n - 1, 6 + dc)
                    sch.wait(te, "T", st.r.get(prev, None))
                    i = te.transpose(ptr[:, pb * 512: pb * 512 + 128],
                                     B["x_t"][:, (n % 5) * D + dc * 128:(n % 5) * D + (dc + 1) * 128],
                                     B["id_f32"][:])
                    st.r[("tr", n, dc)] = sch.inc("T", i)

                if sub != "tr1a":
                    @block.vector
                    def _(v, n=n, dc=dc, bi=bi):
                        pb = dc % 2
                        col = ((n % 4) * 8 + dc) * 128
                        if bi == 0 and dc == 0:
                            sch.wait(v, "V", st.qks_last)
                        sch.wait(v, "V", st.r[("tr", n, dc)])
                        v.tensor_copy(out=B["xt_hi"][:, col:col + 128],
                                      in_=ptr[:, pb * 512: pb * 512 + 128])
                        i = v.scalar_tensor_tensor(
                            out=B["xt_lo"][:, col:col + 128],
                            in0=B["xt_hi"][:, col:col + 128], scalar=-1.0,
                            in1=ptr[:, pb * 512: pb * 512 + 128], op0=ALU.mult, op1=ALU.add)
                        st.r[("trev", n, dc)] = sch.inc("V", i)

    xt_ready = sch.tok("V")
    if sub in ("tr", "tr1", "tr1a"):
        return

    with nc.psum_tensor(f"pqk_{t}", [128, 2 * 512], F32) as pqk:
        q_base = sch.count["S"]
        for cc in range(16):
            st.r[("qkev", t, cc)] = ("S", q_base + cc + 1)
        for cc in range(16):
            sl = cc % 2

            @block.sync
            def _(y, cc=cc, sl=sl):
                if cc < 2:
                    sch.wait(y, "Y", st.qks_last)
                sch.wait(y, "Y", st.r.get(("qkmm", t, cc - 2), None))
                d = y.dma_start(out=B["qks_hi"][:, sl * 1024:(sl + 1) * 1024].rearrange(
                    "p (dc m) -> p dc m", m=128), in_=H["qk_hi"][cc].rearrange("dc p m -> p dc m"))
                sch.inc("DL", d)
                d = y.dma_start(out=B["qks_lo"][:, sl * 1024:(sl + 1) * 1024].rearrange(
                    "p (dc m) -> p dc m", m=128), in_=H["qk_lo"][cc].rearrange("dc p m -> p dc m"))
                st.r[("qkw", t, cc)] = sch.inc("DL", d)

            @block.tensor
            def _(te, cc=cc, sl=sl):
                sch.wait(te, "T", st.r[("qkw", t, cc)])
                sch.wait(te, "T", xt_ready)
                sch.wait(te, "T", st.r.get(("qkev", t, cc - 2), None))
                rh4 = B["xt_hi"][:].rearrange("p (s dc m) -> p dc s m", dc=8, m=128)
                rl4 = B["xt_lo"][:].rearrange("p (s dc m) -> p dc s m", dc=8, m=128)
                i = None
                for dc in range(8):
                    whi = B["qks_hi"][:, sl * 1024 + dc * 128: sl * 1024 + (dc + 1) * 128]
                    wlo = B["qks_lo"][:, sl * 1024 + dc * 128: sl * 1024 + (dc + 1) * 128]
                    te.matmul(pqk[:, sl * 512:(sl + 1) * 512], whi, rh4[:, dc],
                              start=(dc == 0), stop=False)
                    te.matmul(pqk[:, sl * 512:(sl + 1) * 512], whi, rl4[:, dc],
                              start=False, stop=False)
                    i = te.matmul(pqk[:, sl * 512:(sl + 1) * 512], wlo, rh4[:, dc],
                                  start=False, stop=(dc == 7))
                st.r[("qkmm", t, cc)] = sch.inc("T", i)

        @block.scalar
        def _(s):
            sch.wait(s, "S", st.sc_guard)
            for cc in range(16):
                sl = cc % 2
                sch.wait(s, "S", st.r[("qkmm", t, cc)])
                dst = B["q_t"] if cc < 8 else B["k_t"]
                ring = 5 if cc < 8 else 6
                ccp = cc % 8
                for bi, n in enumerate(blocks):
                    rs = n % ring
                    i = s.activation(
                        out=dst[:, (ccp * ring + rs) * 128:(ccp * ring + rs + 1) * 128],
                        in_=pqk[:, sl * 512 + bi * 128: sl * 512 + (bi + 1) * 128],
                        func=AF.Identity, bias=B["bqk_sb"][:, cc:cc + 1], scale=1.0)
                sch.inc("S", i)

    qk_done = sch.tok("S")
    if sub == "qk":
        return

    with nc.psum_tensor(f"pv_{t}", [128, 4 * 1024], F32) as pv:
        for dc in range(8):
            sl = dc % 2

            @block.sync
            def _(y, dc=dc, sl=sl):
                if dc < 2:
                    sch.wait(y, "Y", st.wvs_last)
                sch.wait(y, "Y", st.r.get(("vmm", t, dc - 2), None))
                d = y.dma_start(out=B["wvs_hi"][:, sl * D:(sl + 1) * D], in_=H["wv_hi"][dc])
                sch.inc("DL", d)
                d = y.dma_start(out=B["wvs_lo"][:, sl * D:(sl + 1) * D], in_=H["wv_lo"][dc])
                st.r[("wv", t, dc)] = sch.inc("DL", d)

            @block.tensor
            def _(te, dc=dc, sl=sl):
                if dc == 0:
                    sch.wait(te, "T", qk_done)
                    sch.wait(te, "T", st.b_s3b)
                sch.wait(te, "T", st.r[("wv", t, dc)])
                i = None
                for bi, n in enumerate(blocks):
                    col = ((n % 4) * 8 + dc) * 128
                    xh = B["xt_hi"][:, col:col + 128]
                    xl = B["xt_lo"][:, col:col + 128]
                    for hf in range(2):
                        pvs = pv[:, bi * 1024 + hf * 512: bi * 1024 + (hf + 1) * 512]
                        te.matmul(pvs, xh,
                                  B["wvs_hi"][:, sl * D + hf * 512: sl * D + (hf + 1) * 512],
                                  start=(dc == 0), stop=False)
                        te.matmul(pvs, xh,
                                  B["wvs_lo"][:, sl * D + hf * 512: sl * D + (hf + 1) * 512],
                                  start=False, stop=False)
                        i = te.matmul(pvs, xl,
                                      B["wvs_hi"][:, sl * D + hf * 512: sl * D + (hf + 1) * 512],
                                      start=False, stop=False)
                st.r[("vmm", t, dc)] = sch.inc("T", i)

        @block.tensor
        def _(te):
            i = None
            for bi in range(4):
                for hf in range(2):
                    pvs = pv[:, bi * 1024 + hf * 512: bi * 1024 + (hf + 1) * 512]
                    te.matmul(pvs, B["ones_bf"][0:1, 0:128],
                              B["bvr_hi"][0:1, hf * 512:(hf + 1) * 512],
                              start=False, stop=False)
                    i = te.matmul(pvs, B["ones_bf"][0:1, 0:128],
                                  B["bvr_lo"][0:1, hf * 512:(hf + 1) * 512],
                                  start=False, stop=True)
            st.r[("vmm_all", t)] = sch.inc("T", i)
        st.qks_last = st.r[("qkmm", t, 15)]
        st.wvs_last = st.r[("vmm", t, 7)]

        @block.vector
        def _(v):
            sch.wait(v, "V", st.pv_guard)
            sch.wait(v, "V", st.r[("vmm_all", t)])
            for bi, n in enumerate(blocks):
                rs = n % 6
                dst = B["vt_t"][:, rs * 1040:(rs + 1) * 1040].rearrange(
                    "p (h c) -> p h c", c=65)[:, :, 0:64]
                src = pv[:, bi * 1024:(bi + 1) * 1024].rearrange(
                    "p (h c) -> p h c", c=64)
                i = v.tensor_copy(out=dst, in_=src)
                st.r[("v", n)] = sch.inc("V", i)


def _emit_A2(nc, block, st, t, ms=None, tag=""):
    sch, H, B = st.sch, st.H, st.B
    if ms is None:
        ms = list(range(max(0, 4 * t - 1), 4 * t + 3))

    with nc.psum_tensor(f"psc_{t}{tag}", [128, 2 * 512], F32) as psc, \
         nc.psum_tensor(f"ppv_{t}{tag}", [128, 1024], F32) as ppv, \
         nc.psum_tensor(f"ptr2_{t}{tag}", [128, 1024], F32) as ptr2, \
         nc.psum_tensor(f"pwo_{t}{tag}", [128, 1024], F32) as pwo:

        def emit_sc(m, h, chunks):
            cc = h // 2
            half = (h % 2) * 64
            pb = h % 2
            qs = m % 5

            @block.tensor
            def _(te):
                if m == ms[0] and h == 0:
                    sch.wait(te, "T", st.r.get(("qkev", t, 15), None))
                    sch.wait(te, "T", st.r.get(("v", 4 * t), None))
                    sch.wait(te, "T", st.b_s3b)
                sch.wait(te, "T", st.r.get(("exp", m, h - 2), None))
                if h <= 1:
                    sch.wait(te, "T", st.r.get(("exp", m - 1, 14 + h), None))
                i = None
                for (c, ks) in chunks:
                    lhs = B["k_t"][half:half + 64,
                                   (cc * 6 + ks) * 128:(cc * 6 + ks + 1) * 128]
                    rhs = B["q_t"][half:half + 64,
                                   (cc * 5 + qs) * 128:(cc * 5 + qs + 1) * 128]
                    i = te.matmul(psc[:, pb * 512 + c * 128: pb * 512 + (c + 1) * 128],
                                  lhs, rhs, start=True, stop=(c == 1))
                    if c == 0:
                        i = te.matmul(psc[:, pb * 512: pb * 512 + 128],
                                      B["m0t"][:], B["id_bf"][:], start=False, stop=True)
                    elif c == 2:
                        i = te.matmul(psc[:, pb * 512 + 256: pb * 512 + 384],
                                      B["m2t"][:], B["id_bf"][:], start=False, stop=True)
                st.r[("sc", m, h)] = sch.inc("T", i)

        def emit_exp_section(m, chunks):
            @block.scalar
            def _(s):
                if m == ms[0]:
                    sch.wait(s, "S", st.b_vdone)
                for h in range(16):
                    pb = h % 2
                    sch.wait(s, "S", st.r[("sc", m, h)])
                    sch.wait(s, "S", st.r.get(("pv", m, h - 2), None))
                    if h <= 1:
                        sch.wait(s, "S", st.r.get(("pv", m - 1, 14 + h), None))
                    c0, c1 = chunks[0][0], chunks[-1][0]
                    i = s.activation(out=B["e_t"][:, pb * 384 + c0 * 128: pb * 384 + (c1 + 1) * 128],
                                     in_=psc[:, pb * 512 + c0 * 128: pb * 512 + (c1 + 1) * 128],
                                     func=AF.Exp, bias=0.0, scale=SCALE)
                    sch.inc("S", i)

        def emit_pv(m, h, chunks):
            pb = h % 2

            @block.tensor
            def _(te):
                if m == ms[0] and h == 0:
                    sch.wait(te, "T", st.r.get(("v", 4 * t + 1), None))
                sch.wait(te, "T", st.r[("exp", m, h)])
                if h % 8 == 0:
                    sch.wait(te, "T", st.r.get(("anorm", m, h - 8), None))
                    if h == 0:
                        sch.wait(te, "T", st.r.get(("anorm", m - 1, 8), None))
                po = (h % 8) * 128
                first = True
                i = None
                for (c, ks) in chunks:
                    lhs = B["e_t"][:, pb * 384 + c * 128: pb * 384 + (c + 1) * 128]
                    rhs = B["vt_t"][:, ks * 1040 + h * 65: ks * 1040 + (h + 1) * 65]
                    i = te.matmul(ppv[:, po:po + 65], lhs, rhs,
                                  start=first, stop=(c == chunks[-1][0]))
                    first = False
                st.r[("pv", m, h)] = sch.inc("T", i)

        for m in ms:
            chunks = []
            if m > 0:
                chunks.append((0, (m - 1) % 6))
            chunks.append((1, m % 6))
            if m < 31:
                chunks.append((2, (m + 1) % 6))

            e_base = sch.count["S"]
            for h in range(16):
                st.r[("exp", m, h)] = ("S", e_base + h + 1)
            for h in range(16):
                emit_sc(m, h, chunks)
                if h >= 1:
                    emit_pv(m, h - 1, chunks)
                    if h == 8:
                        _emit_wave_norm(nc, block, st, m, 0, ppv)
            emit_pv(m, 15, chunks)
            emit_exp_section(m, chunks)
            _emit_wave_norm(nc, block, st, m, 1, ppv)

            for dc in range(8):
                @block.tensor
                def _(te, m=m, dc=dc):
                    if dc == 0:
                        if m == ms[0]:
                            sch.wait(te, "T", st.r.get(("v", 4 * t + 2), None))
                        sch.wait(te, "T", st.r[("anorm", m, 0)])
                    if dc == 4:
                        sch.wait(te, "T", st.r[("anorm", m, 8)])
                    pb2 = dc % 2
                    prev = ("atr_ev", m, dc - 2) if dc >= 2 else ("atr_ev", m - 1, 6 + dc)
                    sch.wait(te, "T", st.r.get(prev, None))
                    i = te.transpose(ptr2[:, pb2 * 512: pb2 * 512 + 128],
                                     B["attn_tm"][:, dc * 128:(dc + 1) * 128],
                                     B["id_f32"][:])
                    st.r[("atr", m, dc)] = sch.inc("T", i)

                @block.vector
                def _(v, m=m, dc=dc):
                    pb2 = dc % 2
                    sch.wait(v, "V", st.r[("atr", m, dc)])
                    sch.wait(v, "V", st.r.get(("womm", m - 1, dc), None))
                    v.tensor_copy(out=B["at_hi"][:, dc * 128:(dc + 1) * 128],
                                  in_=ptr2[:, pb2 * 512: pb2 * 512 + 128])
                    i = v.scalar_tensor_tensor(
                        out=B["at_lo"][:, dc * 128:(dc + 1) * 128],
                        in0=B["at_hi"][:, dc * 128:(dc + 1) * 128], scalar=-1.0,
                        in1=ptr2[:, pb2 * 512: pb2 * 512 + 128], op0=ALU.mult, op1=ALU.add)
                    st.r[("atr_ev", m, dc)] = sch.inc("V", i)

            for dc in range(8):
                sl = dc % 2

                @block.sync
                def _(y, m=m, dc=dc, sl=sl):
                    if dc < 2:
                        sch.wait(y, "Y", st.wvs_last)
                    sch.wait(y, "Y", st.r.get(("womm", m, dc - 2) if dc >= 2
                                              else ("womm", m - 1, 6 + dc), None))
                    d = y.dma_start(out=B["wvs_hi"][:, sl * D:(sl + 1) * D],
                                    in_=H["wo_hi"][dc])
                    sch.inc("DL", d)
                    d = y.dma_start(out=B["wvs_lo"][:, sl * D:(sl + 1) * D],
                                    in_=H["wo_lo"][dc])
                    st.r[("wow", m, dc)] = sch.inc("DL", d)

                @block.tensor
                def _(te, m=m, dc=dc, sl=sl):
                    if dc == 0:
                        if m == ms[0]:
                            sch.wait(te, "T", st.r.get(("v", 4 * t + 3), None))
                        sch.wait(te, "T", st.r.get((("ln1", m - 1), "r"), None))
                    sch.wait(te, "T", st.r[("atr_ev", m, dc)])
                    sch.wait(te, "T", st.r[("wow", m, dc)])
                    ah = B["at_hi"][:, dc * 128:(dc + 1) * 128]
                    al = B["at_lo"][:, dc * 128:(dc + 1) * 128]
                    i = None
                    for hf in range(2):
                        pws = pwo[:, hf * 512:(hf + 1) * 512]
                        te.matmul(pws, ah,
                                  B["wvs_hi"][:, sl * D + hf * 512: sl * D + (hf + 1) * 512],
                                  start=(dc == 0), stop=False)
                        te.matmul(pws, ah,
                                  B["wvs_lo"][:, sl * D + hf * 512: sl * D + (hf + 1) * 512],
                                  start=False, stop=False)
                        i = te.matmul(pws, al,
                                      B["wvs_hi"][:, sl * D + hf * 512: sl * D + (hf + 1) * 512],
                                      start=False, stop=False)
                    st.r[("womm", m, dc)] = sch.inc("T", i)

            @block.tensor
            def _(te, m=m):
                i = None
                for hf in range(2):
                    pws = pwo[:, hf * 512:(hf + 1) * 512]
                    te.matmul(pws, B["ones_bf"][0:1, 0:128],
                              B["bor_hi"][0:1, hf * 512:(hf + 1) * 512],
                              start=False, stop=False)
                    i = te.matmul(pws, B["ones_bf"][0:1, 0:128],
                                  B["bor_lo"][0:1, hf * 512:(hf + 1) * 512],
                                  start=False, stop=True)
                st.r[("wo_mm", m)] = sch.inc("T", i)
            st.wvs_last = st.r[("womm", m, 7)]

            _emit_ln(nc, block, st, src=pwo[:],
                     res_ap=B["x_t"][:, (m % 5) * D:(m % 5 + 1) * D],
                     out_ap=B["y_t"][:, (m % 4) * D:(m % 4 + 1) * D],
                     key=("ln1", m), dep=st.r[("wo_mm", m)])
            st.r[("y", m)] = st.r[("ln1", m)]
        st.sc_guard = st.r[("sc", ms[-1], 15)]
        st.pv_guard = st.r[("pv", ms[-1], 15)]
        st.a2_ln_last = st.r[("ln1", ms[-1])]
        st.a2_exp_last = st.r[("exp", ms[-1], 15)]


def _emit_wave_norm(nc, block, st, m, wave, ppv):
    sch, B = st.sch, st.B

    @block.vector
    def _(v):
        for h in range(wave * 8, wave * 8 + 8):
            sch.wait(v, "V", st.r[("pv", m, h)])
        if wave == 0:
            sch.wait(v, "V", st.r.get(("atr", m - 1, 7), None))
        i = v.reciprocal(out=B["st_t"][:, 8:16], in_=ppv[:, 64::128])
        sch.wait(v, "V", sch.inc("V", i))
        i = None
        for hh in range(8):
            h = wave * 8 + hh
            i = v.tensor_scalar(out=B["attn_tm"][:, h * 64:(h + 1) * 64],
                                in0=ppv[:, hh * 128: hh * 128 + 64],
                                scalar1=B["st_t"][:, 8 + hh:9 + hh], scalar2=None,
                                op0=ALU.mult)
        st.r[("anorm", m, wave * 8)] = sch.inc("V", i)


def _emit_ln(nc, block, st, src, res_ap, out_ap, key, dep):
    sch, B = st.sch, st.B
    r_buf = B["r2_t"]

    @block.vector
    def _(v):
        sch.wait(v, "V", dep)
        sch.wait(v, "V", st.last_ln_sq)
        i = v.scalar_tensor_tensor(out=r_buf, in0=src, scalar=1.0, in1=res_ap,
                                   op0=ALU.mult, op1=ALU.add,
                                   accum_out=B["st_t"][:, 0:1])
        st.r[(key, "r")] = sch.inc("V", i)

    _emit_ln_tail(nc, block, st, key, r_buf, out_ap)


def _emit_ln_tail(nc, block, st, key, r_buf, out_ap, pre_tok=None, sum2=False):
    sch, B = st.sch, st.B

    @block.scalar
    def _(s):
        sch.wait(s, "S", st.r[(key, "r")])
        i = s.activation(out=B["sq_t"], in_=r_buf, func=AF.Square,
                         bias=0.0, scale=1.0, accum_out=B["st_t"][:, 1:2])
        st.r[(key, "sq")] = sch.inc("S", i)
    st.last_ln_sq = st.r[(key, "sq")]

    @block.vector
    def _(v):
        sch.wait(v, "V", st.r[(key, "sq")])
        stt = B["st_t"]
        if sum2:
            i = v.tensor_tensor(out=stt[:, 0:1], in0=stt[:, 0:1],
                                in1=stt[:, 9:10], op=ALU.add)
            sch.wait(v, "V", sch.inc("V", i))
        i = v.tensor_scalar(out=stt[:, 2:3], in0=stt[:, 0:1], scalar1=1.0 / D,
                            scalar2=None, op0=ALU.mult)
        sch.wait(v, "V", sch.inc("V", i))
        i = v.tensor_tensor(out=stt[:, 3:4], in0=stt[:, 2:3], in1=stt[:, 2:3],
                            op=ALU.mult)
        sch.wait(v, "V", sch.inc("V", i))
        i = v.scalar_tensor_tensor(out=stt[:, 4:5], in0=stt[:, 1:2], scalar=1.0 / D,
                                   in1=stt[:, 3:4], op0=ALU.mult, op1=ALU.subtract)
        sch.wait(v, "V", sch.inc("V", i))
        i = v.tensor_scalar(out=stt[:, 4:5], in0=stt[:, 4:5], scalar1=1.0e-5,
                            scalar2=None, op0=ALU.add)
        st.r[(key, "v")] = sch.inc("V", i)

    @block.scalar
    def _(s):
        sch.wait(s, "S", st.r[(key, "v")])
        i = s.activation(out=B["st_t"][:, 5:6], in_=B["st_t"][:, 4:5],
                         func=AF.Sqrt, bias=0.0, scale=1.0)
        st.r[(key, "s0")] = sch.inc("S", i)

    @block.vector
    def _(v):
        sch.wait(v, "V", st.r[(key, "s0")])
        sch.wait(v, "V", pre_tok)
        stt = B["st_t"]
        i = v.reciprocal(out=stt[:, 6:7], in_=stt[:, 5:6])
        sch.wait(v, "V", sch.inc("V", i))
        if isinstance(out_ap, tuple):
            v.tensor_scalar(out=out_ap[0], in0=r_buf[:, 0:512], scalar1=stt[:, 2:3],
                            scalar2=stt[:, 6:7], op0=ALU.subtract, op1=ALU.mult)
            i = v.tensor_scalar(out=out_ap[1], in0=r_buf[:, 512:1024], scalar1=stt[:, 2:3],
                                scalar2=stt[:, 6:7], op0=ALU.subtract, op1=ALU.mult)
        else:
            i = v.tensor_scalar(out=out_ap, in0=r_buf, scalar1=stt[:, 2:3],
                                scalar2=stt[:, 6:7], op0=ALU.subtract, op1=ALU.mult)
        st.r[key] = sch.inc("V", i)


def _emit_B(nc, block, st, t, blocks):
    sch, H, B = st.sch, st.H, st.B
    nb = len(blocks)
    ntok = nb * 128
    bid = (t, blocks[0])

    if H.get("ydbg") is not None:
        @block.gpsimd
        def _(g):
            for m in blocks:
                sch.wait(g, "G", st.r[("y", m)])
                d = g.dma_start(out=H["ydbg"][m * 128:(m + 1) * 128, :],
                                in_=B["y_t"][:, (m % 4) * D:(m % 4 + 1) * D])
                sch.inc("DS", d)

    with nc.psum_tensor(f"ptr3_{t}_{blocks[0]}", [128, 1024], F32) as ptr3:
        for bi, m in enumerate(blocks):
            for dc in range(8):
                @block.tensor
                def _(te, m=m, bi=bi, dc=dc):
                    if dc == 0:
                        if bi == 0:
                            sch.wait(te, "T", st.a2_exp_last)
                        sch.wait(te, "T", st.r[("y", m)])
                    pb = dc % 2
                    prev = ("ytev", bid, bi, dc - 2) if dc >= 2 else ("ytev", bid, bi - 1, 6 + dc)
                    sch.wait(te, "T", st.r.get(prev, None))
                    i = te.transpose(ptr3[:, pb * 512: pb * 512 + 128],
                                     B["y_t"][:, (m % 4) * D + dc * 128:(m % 4) * D + (dc + 1) * 128],
                                     B["id_f32"][:])
                    st.r[("ytr", bid, bi, dc)] = sch.inc("T", i)

                @block.vector
                def _(v, m=m, bi=bi, dc=dc):
                    pb = dc % 2
                    col = (bi * 8 + dc) * 128
                    sch.wait(v, "V", st.r[("ytr", bid, bi, dc)])
                    i = v.tensor_copy(out=B["xt_hi"][:, col:col + 128],
                                      in_=ptr3[:, pb * 512: pb * 512 + 128])
                    i = v.scalar_tensor_tensor(
                        out=B["xt_lo"][:, col:col + 128],
                        in0=B["xt_hi"][:, col:col + 128], scalar=-1.0,
                        in1=ptr3[:, pb * 512: pb * 512 + 128], op0=ALU.mult, op1=ALU.add)
                    st.r[("ytev", bid, bi, dc)] = sch.inc("V", i)

    yt_ready = sch.tok("V")

    def emit_w1_stream(fc):
        sl = fc % 2

        @block.sync
        def _(y):
            if fc < 2:
                sch.wait(y, "Y", st.qks_last)
            sch.wait(y, "Y", st.r.get(("ff1mm", bid, fc - 2), None))
            d = y.dma_start(out=B["w1s_hi"][:, sl * 1024:(sl + 1) * 1024].rearrange(
                "p (dc m) -> p dc m", m=128), in_=H["w1_hi"][fc].rearrange("dc p m -> p dc m"))
            sch.inc("DL", d)
            d = y.dma_start(out=B["w1s_lo"][:, sl * 1024:(sl + 1) * 1024].rearrange(
                "p (dc m) -> p dc m", m=128), in_=H["w1_lo"][fc].rearrange("dc p m -> p dc m"))
            sch.inc("DL", d)
            d = y.dma_start(out=B["b1s_hi"][0:1, sl * 128:(sl + 1) * 128],
                            in_=H["b1r_hi"][fc])
            sch.inc("DL", d)
            d = y.dma_start(out=B["b1s_lo"][0:1, sl * 128:(sl + 1) * 128],
                            in_=H["b1r_lo"][fc])
            st.r[("w1w", bid, fc)] = sch.inc("DL", d)

    def emit_w2_stream(fc):
        sl = fc % 2

        @block.sync
        def _(y):
            if fc < 2:
                sch.wait(y, "Y", st.wvs_last)
            sch.wait(y, "Y", st.r.get(("ff2b", bid, fc - 2), None))
            d = y.dma_start(out=B["w2s_hi"][:, sl * D: (sl + 1) * D],
                            in_=H["w2_hi"][fc])
            sch.inc("DL", d)
            d = y.dma_start(out=B["w2s_lo"][:, sl * D: (sl + 1) * D],
                            in_=H["w2_lo"][fc])
            st.r[("w2w", bid, fc)] = sch.inc("DL", d)

    with nc.psum_tensor(f"ph_{t}_{blocks[0]}", [128, 2 * 512], F32) as ph, \
         nc.psum_tensor(f"po_{t}_{blocks[0]}", [128, 4 * 512], F32) as po, \
         nc.psum_tensor(f"po1_{t}_{blocks[0]}", [128, 2 * 512], F32) as po1:

        def emit_ff1_T(fc):
            sl = fc % 2

            @block.tensor
            def _(te):
                sch.wait(te, "T", st.r[("w1w", bid, fc)])
                sch.wait(te, "T", yt_ready)
                if fc >= 2:
                    sch.wait(te, "T", ("S", st.b_sbase
                                       + S_PER * ((fc - 2) // gf)
                                       + ((fc - 2) % gf) + 1))
                rh4 = B["xt_hi"][:, 0:nb * 8 * 128].rearrange(
                    "p (s dc m) -> p dc s m", dc=8, m=128)
                rl4 = B["xt_lo"][:, 0:nb * 8 * 128].rearrange(
                    "p (s dc m) -> p dc s m", dc=8, m=128)
                pp = ph[:, (fc % 2) * 512:(fc % 2) * 512 + ntok]
                for dc in range(8):
                    whi = B["w1s_hi"][:, sl * 1024 + dc * 128: sl * 1024 + (dc + 1) * 128]
                    wlo = B["w1s_lo"][:, sl * 1024 + dc * 128: sl * 1024 + (dc + 1) * 128]
                    te.matmul(pp, whi, rh4[:, dc], start=(dc == 0), stop=False)
                    te.matmul(pp, whi, rl4[:, dc], start=False, stop=False)
                    te.matmul(pp, wlo, rh4[:, dc], start=False, stop=False)
                te.matmul(pp, B["b1s_hi"][0:1, sl * 128:(sl + 1) * 128],
                          B["ones_bf"][0:1, 0:ntok], start=False, stop=False)
                i = te.matmul(pp, B["b1s_lo"][0:1, sl * 128:(sl + 1) * 128],
                              B["ones_bf"][0:1, 0:ntok], start=False, stop=True)
                st.r[("ff1mm", bid, fc)] = sch.inc("T", i)

        def rslot(g, k):
            return (RK.index(k) % 4) * 512

        def s_cnt(g, off):
            return ("S", st.b_sbase + S_PER * g + off)

        def v_cnt(g, off):
            return ("V", st.b_vbase + V_PER * g + off)

        def emit_epi_S():
            @block.scalar
            def _(s):
                sch.wait(s, "S", st.last_store)
                for fc in range(32):
                    p5 = (fc % 2) * 512
                    sch.wait(s, "S", v_cnt(fc - 2, 2))
                    sch.wait(s, "S", st.r[("ff1mm", bid, fc)])
                    pp = ph[:, (fc % 2) * 512:(fc % 2) * 512 + ntok]
                    i = s.activation(out=B["s1_t"][:, p5:p5 + ntok], in_=pp,
                                     func=AF.Identity, bias=B["pmag"][:, 0:1],
                                     scale=256.0)
                    sch.inc("S", i)
                    uap = B["u0"] if fc % 2 == 0 else B["u1"]
                    i = s.activation(out=uap[:, 0:ntok],
                                     in_=B["s1_t"][:, p5:p5 + ntok],
                                     func=AF.Abs, bias=B["nmag"][:, 0:1], scale=1.0)
                    sch.inc("S", i)
                    sch.wait(s, "S", v_cnt(fc - 1, 19))
                    for k in RK:
                        i = s.activation(
                            out=B["rk_t"][:, rslot(fc, k): rslot(fc, k) + ntok],
                            in_=uap[:, 0:ntok], func=AF.Relu,
                            bias=B["kinkb"][:, k:k + 1],
                            scale=(1.0 if k in TK else abs(KINK_C[k])))
                        sch.inc("S", i)

        def emit_epi_V(g):
            @block.vector
            def _(v):
                p5 = (g % 2) * 512
                uap = B["u0"] if g % 2 == 0 else B["u1"]
                sch.wait(v, "V", s_cnt(g, gf))
                i = v.tensor_scalar(out=B["acc_t"][:, 0:ntok_g],
                                    in0=B["s1_t"][:, p5:p5 + ntok_g],
                                    scalar1=0.5, scalar2=-MAGIC / 2,
                                    op0=ALU.mult, op1=ALU.add)
                sch.inc("V", i)
                sch.wait(v, "V", s_cnt(g, gf + 1))
                i = v.scalar_tensor_tensor(out=B["acc_t"][:, 0:ntok_g],
                                           in0=uap[:, 0:ntok_g],
                                           scalar=A0, in1=B["acc_t"][:, 0:ntok_g],
                                           op0=ALU.mult, op1=ALU.add)
                sch.inc("V", i)
                for idx, k in enumerate(RK):
                    sch.wait(v, "V", s_cnt(g, gf + 2 + idx))
                    i = v.scalar_tensor_tensor(
                        out=B["acc_t"][:, 0:ntok_g],
                        in0=B["rk_t"][:, rslot(g, k): rslot(g, k) + ntok_g],
                        scalar=(KINK_C[k] if k in TK
                                else (1.0 if KINK_C[k] > 0 else -1.0)),
                        in1=B["acc_t"][:, 0:ntok_g], op0=ALU.mult, op1=ALU.add)
                    sch.inc("V", i)
                i = v.tensor_scalar(out=B["a_t"][:, g * 512: g * 512 + ntok_g],
                                    in0=B["acc_t"][:, 0:ntok_g], scalar1=MAGIC,
                                    scalar2=-MAGIC, op0=ALU.add, op1=ALU.add)
                st.r[("aq", bid, g)] = sch.inc("V", i)

        nb1 = min(2, nb)
        gf = 1
        ngrp = 32 // gf
        ntok_g = ntok * gf
        S_PER = gf + 1 + len(RK)
        V_PER = 20

        def a_slice(fc, sub):
            if gf == 1:
                return B["a_t"][:, fc * 512 + sub * 128: fc * 512 + (sub + 1) * 128]
            return B["a_t"][:, fc * 128: (fc + 1) * 128]

        def emit_ff2a(fc):
            sl2 = fc % 2

            @block.tensor
            def _(te):
                sch.wait(te, "T", ("V", st.b_vbase + V_PER * (fc // gf) + 20))
                sch.wait(te, "T", st.r[("w2w", bid, fc)])
                i = None
                for sub in range(nb):
                    a_ap = a_slice(fc, sub)
                    te.matmul(po[:, sub * 512:(sub + 1) * 512], a_ap,
                              B["w2s_hi"][:, sl2 * D: sl2 * D + 512],
                              start=(fc == 0), stop=False)
                    i = te.matmul(po[:, sub * 512:(sub + 1) * 512], a_ap,
                                  B["w2s_lo"][:, sl2 * D: sl2 * D + 512],
                                  start=False, stop=(fc == 31))
                    if fc == 0:
                        te.matmul(po[:, sub * 512:(sub + 1) * 512],
                                  B["ones_f16"][0:1, 0:128],
                                  B["b2r_hi"][0:1, 0:512],
                                  start=False, stop=False)
                        i = te.matmul(po[:, sub * 512:(sub + 1) * 512],
                                      B["ones_f16"][0:1, 0:128],
                                      B["b2r_lo"][0:1, 0:512],
                                      start=False, stop=False)
                st.r[("ff2a", bid, fc)] = sch.inc("T", i)

        def emit_ff2b(fc):
            sl2 = fc % 2

            @block.tensor
            def _(te):
                i = None
                for sub in range(nb1):
                    a_ap = a_slice(fc, sub)
                    te.matmul(po1[:, sub * 512:(sub + 1) * 512], a_ap,
                              B["w2s_hi"][:, sl2 * D + 512: (sl2 + 1) * D],
                              start=(fc == 0), stop=False)
                    i = te.matmul(po1[:, sub * 512:(sub + 1) * 512], a_ap,
                                  B["w2s_lo"][:, sl2 * D + 512: (sl2 + 1) * D],
                                  start=False, stop=(fc == 31))
                    if fc == 0:
                        te.matmul(po1[:, sub * 512:(sub + 1) * 512],
                                  B["ones_f16"][0:1, 0:128],
                                  B["b2r_hi"][0:1, 512:1024],
                                  start=False, stop=False)
                        i = te.matmul(po1[:, sub * 512:(sub + 1) * 512],
                                      B["ones_f16"][0:1, 0:128],
                                      B["b2r_lo"][0:1, 512:1024],
                                      start=False, stop=False)
                st.r[("ff2b", bid, fc)] = sch.inc("T", i)

        st.b_sbase = sch.count["S"]
        st.b_vbase = sch.count["V"]
        lag = 2 * gf
        emit_w2_stream(0)
        emit_w2_stream(1)
        for pr in range(16):
            f0 = 2 * pr
            emit_w1_stream(f0)
            emit_w1_stream(f0 + 1)
            emit_ff1_T(f0)
            emit_ff1_T(f0 + 1)
            if f0 >= lag:
                emit_ff2a(f0 - lag)
                emit_ff2b(f0 - lag)
                emit_ff2a(f0 - lag + 1)
                emit_ff2b(f0 - lag + 1)
                if f0 - lag + 2 < 32:
                    emit_w2_stream(f0 - lag + 2)
                if f0 - lag + 3 < 32:
                    emit_w2_stream(f0 - lag + 3)
        for fc in range(32 - lag, 32):
            emit_ff2a(fc)
            emit_ff2b(fc)
            if fc + 2 < 32:
                emit_w2_stream(fc + 2)
        emit_epi_S()
        for g in range(ngrp):
            emit_epi_V(g)

        for sub, m in enumerate(blocks):
            @block.scalar
            def _(s, sub=sub, m=m):
                sch.wait(s, "S", st.r[("ff2a", bid, 31)])
                if sub > 0:
                    sch.wait(s, "S", st.r.get(("oqa", blocks[sub - 1]), None))
                i = s.activation(out=B["s3_t"], in_=po[:, sub * 512:(sub + 1) * 512],
                                 func=AF.Identity, bias=B["pmag"][:, 0:1], scale=1.0)
                st.r[("s3a", m)] = sch.inc("S", i)

            @block.vector
            def _(v, sub=sub, m=m):
                sch.wait(v, "V", st.r[("s3a", m)])
                i = v.tensor_scalar(out=B["oh16"][:, sub * 512:(sub + 1) * 512],
                                    in0=B["s3_t"], scalar1=-MAGIC,
                                    scalar2=None, op0=ALU.add)
                st.r[("oqa", m)] = sch.inc("V", i)

        def emit_ln2(sub, m, src_po, gate_key):
            @block.scalar
            def _(s, sub=sub, m=m):
                sch.wait(s, "S", st.r[gate_key])
                if sub == 0:
                    sch.wait(s, "S", st.r[("oqa", blocks[-1])])
                sch.wait(s, "S", st.r.get(("r2", m - 1), None))
                i = s.activation(out=B["s3_t"], in_=src_po,
                                 func=AF.Identity, bias=B["pmag"][:, 0:1], scale=1.0)
                st.r[("s3b", m)] = sch.inc("S", i)

            @block.vector
            def _(v, sub=sub, m=m):
                sch.wait(v, "V", st.r[("s3b", m)])
                sch.wait(v, "V", st.last_ln_sq)
                sch.wait(v, "V", st.r.get(("ff2c", bid, 15), None))
                v.tensor_scalar(out=B["u0"][:, 0:512], in0=B["s3_t"],
                                scalar1=-MAGIC, scalar2=1.0 / 256.0,
                                op0=ALU.add, op1=ALU.mult)
                v.scalar_tensor_tensor(
                    out=B["r2_t"][:, 0:512], in0=B["oh16"][:, sub * 512:(sub + 1) * 512],
                    scalar=1.0 / 256.0, in1=B["y_t"][:, (m % 4) * D:(m % 4) * D + 512],
                    op0=ALU.mult, op1=ALU.add, accum_out=B["st_t"][:, 0:1])
                i = v.scalar_tensor_tensor(
                    out=B["r2_t"][:, 512:1024], in0=B["u0"][:, 0:512],
                    scalar=1.0, in1=B["y_t"][:, (m % 4) * D + 512:(m % 4 + 1) * D],
                    op0=ALU.mult, op1=ALU.add, accum_out=B["st_t"][:, 9:10])
                st.r[(("ln2", m), "r")] = sch.inc("V", i)
                st.r[("r2", m)] = st.r[(("ln2", m), "r")]

            _emit_ln_tail(nc, block, st, ("ln2", m), B["r2_t"],
                          (B["s1_t"][:, 0:512], B["s1_t"][:, 512:1024]),
                          pre_tok=st.last_store, sum2=True)
            st.r[("ob", m)] = st.r[("ln2", m)]

            @block.gpsimd
            def _(g, m=m):
                sch.wait(g, "G", st.r[("ob", m)])
                d = g.dma_start(out=st.H["out"][m * 128:(m + 1) * 128, 0:512],
                                in_=B["s1_t"][:, 0:512])
                sch.inc("DS", d)
                d = g.dma_start(out=st.H["out"][m * 128:(m + 1) * 128, 512:1024],
                                in_=B["s1_t"][:, 512:1024])
                st.last_store = sch.inc("DS", d)

        if nb > 2:
            for fc in range(32):
                sl = fc % 2

                @block.sync
                def _(y, fc=fc, sl=sl):
                    if fc < 2:
                        sch.wait(y, "Y", st.r[("ff2b", bid, 30 + sl)])
                    sch.wait(y, "Y", st.r.get(("ff2c", bid, fc - 2), None))
                    d = y.dma_start(out=B["w2s_hi"][:, sl * D: sl * D + 512],
                                    in_=H["w2_hi"][fc][:, 512:1024])
                    sch.inc("DL", d)
                    d = y.dma_start(out=B["w2s_lo"][:, sl * D: sl * D + 512],
                                    in_=H["w2_lo"][fc][:, 512:1024])
                    st.r[("w2wc", bid, fc)] = sch.inc("DL", d)

                @block.tensor
                def _(te, fc=fc, sl=sl):
                    if fc == 0:
                        sch.wait(te, "T", st.r[("s3a", blocks[-1])])
                    sch.wait(te, "T", st.r[("w2wc", bid, fc)])
                    i = None
                    for j, sub in enumerate(range(2, nb)):
                        a_ap = a_slice(fc, sub)
                        te.matmul(po[:, j * 512:(j + 1) * 512], a_ap,
                                  B["w2s_hi"][:, sl * D: sl * D + 512],
                                  start=(fc == 0), stop=False)
                        i = te.matmul(po[:, j * 512:(j + 1) * 512], a_ap,
                                      B["w2s_lo"][:, sl * D: sl * D + 512],
                                      start=False, stop=(fc == 31))
                        if fc == 0:
                            te.matmul(po[:, j * 512:(j + 1) * 512],
                                      B["ones_f16"][0:1, 0:128],
                                      B["b2r_hi"][0:1, 512:1024],
                                      start=False, stop=False)
                            i = te.matmul(po[:, j * 512:(j + 1) * 512],
                                          B["ones_f16"][0:1, 0:128],
                                          B["b2r_lo"][0:1, 512:1024],
                                          start=False, stop=False)
                    st.r[("ff2c", bid, fc)] = sch.inc("T", i)

            st.wvs_last = st.r[("ff2c", bid, 31)]
        else:
            st.wvs_last = st.r[("ff2b", bid, 31)]
        for sub in range(nb1):
            emit_ln2(sub, blocks[sub], po1[:, sub * 512:(sub + 1) * 512],
                     ("ff2b", bid, 31))
        for j, sub in enumerate(range(2, nb)):
            emit_ln2(sub, blocks[sub], po[:, j * 512:(j + 1) * 512],
                     ("ff2c", bid, 31))
        st.qks_last = st.r[("ff1mm", bid, 31)]

        st.b_s3b = st.r[("s3b", blocks[-1])]
        st.b_s1last = ("S", st.b_sbase + S_PER * (ngrp - 1) + gf)
        st.b_vdone = sch.tok("V")


def kernel(L0_attn_x, L0_attn_Wqkv, L0_attn_bqkv, L0_attn_Wout, L0_attn_bout,
           L0_ffn_w1, L0_ffn_b1, L0_ffn_w2, L0_ffn_b2):
    nc = build()
    in_maps = []
    for c in range(8):
        in_maps.append(dict(
            x=np.ascontiguousarray(L0_attn_x[c]),
            wqkv=np.ascontiguousarray(L0_attn_Wqkv),
            bqkv=np.ascontiguousarray(L0_attn_bqkv),
            wout=np.ascontiguousarray(L0_attn_Wout),
            bout=np.ascontiguousarray(L0_attn_bout),
            w1=np.ascontiguousarray(L0_ffn_w1),
            b1=np.ascontiguousarray(L0_ffn_b1),
            w2=np.ascontiguousarray(L0_ffn_w2),
            b2=np.ascontiguousarray(L0_ffn_b2),
        ))
    res = run_bass_kernel_spmd(nc, in_maps, core_ids=list(range(8)))
    return np.stack([r["out"] for r in res.results], axis=0)


# version 19
# speedup vs baseline: 1.0243x; 1.0056x over previous
import math
import numpy as np
import concourse.bass as bass
import concourse.mybir as mybir
from concourse.bass_utils import run_bass_kernel_spmd

F32 = mybir.dt.float32
BF16 = mybir.dt.bfloat16
F16 = mybir.dt.float16
AF = mybir.ActivationFunctionType
ALU = mybir.AluOpType
MAGIC = float(3 * 2 ** 22)

D = 1024
FF = 4096
NB = 32
NT = 8
SCALE = 1.0 / 32.0


def _pwl_params():
    X = np.linspace(-4.0, 4.0, 33).astype(np.float32)
    erf = np.vectorize(math.erf)
    Y = (0.5 * X * (1.0 + erf(X.astype(np.float64) / math.sqrt(2.0)))).astype(np.float32)
    Gh = np.array([256.0 * (float(Y[16 + k]) - 0.125 * k) for k in range(17)])
    s = [(Gh[k + 1] - Gh[k]) / 64.0 for k in range(16)]
    s16 = 512.5 - Gh[16]
    Ks = [64.0 * k for k in range(1, 16)] + [1024.0, 1025.0]
    cs = [s[k] - s[k - 1] for k in range(1, 16)] + [s16 - s[15], 0.5 - s16]
    return float(s[0]), [float(k) for k in Ks], [float(c) for c in cs]


A0, KINK_K, KINK_C = _pwl_params()
NK = len(KINK_K)
GK = [5, 6, 7, 8]
TK = [15, 16]
RK = list(range(NK))
assert all(KINK_C[k] < 0 for k in GK)


class Sch:
    def __init__(self, nc, ctx):
        self.nc = nc
        self.sem = {}
        self.count = {}
        self.last_wait = {w: {} for w in ("T", "V", "S", "G", "Y")}
        for name in ("T", "V", "S", "G", "DL", "DS"):
            self.sem[name] = ctx.enter_context(nc.semaphore("sem_" + name))
            self.count[name] = 0

    def inc(self, name, inst):
        step = 16 if name in ("DL", "DS") else 1
        inst.then_inc(self.sem[name], step)
        self.count[name] += step
        return (name, self.count[name])

    def tok(self, name):
        return (name, self.count[name])

    def wait(self, eng, wname, token):
        if not token:
            return
        name, val = token
        if val <= 0:
            return
        lw = self.last_wait[wname]
        if lw.get(name, 0) >= val:
            return
        eng.wait_ge(self.sem[name], val)
        lw[name] = val


class St:
    def __init__(self, sch, H, B):
        self.sch = sch
        self.H = H
        self.B = B
        self.r = {}
        self.last_store = None
        self.last_ln_sq = None
        self.wvs_last = None
        self.qks_last = None
        self.b_s3b = None
        self.b_s1last = None
        self.b_vdone = None
        self.a2_ln_last = None
        self.a2_exp_last = None
        self.sc_guard = None
        self.pv_guard = None


def _barrier(block, st, include_y=False):
    sch = st.sch
    toks = [sch.tok(n) for n in ("T", "V", "S", "G", "DL", "DS")]
    pairs = [("T", block.tensor), ("V", block.vector), ("S", block.scalar),
             ("G", block.gpsimd)]
    if include_y:
        pairs.append(("Y", block.sync))
    for name, section in pairs:
        @section
        def _(e, _t=toks, _n=name):
            for tk in _t:
                sch.wait(e, _n, tk)


def build(limit_tiles=NT, phases="ab", debug=False):
    nc = bass.Bass("TRN2", target_bir_lowering=False, detect_race_conditions=False)

    H = {}
    H["x"] = nc.dram_tensor("x", [4096, D], F32, kind="ExternalInput").ap()
    H["wqkv"] = nc.dram_tensor("wqkv", [D, 3 * D], F32, kind="ExternalInput").ap()
    H["bqkv"] = nc.dram_tensor("bqkv", [3 * D], F32, kind="ExternalInput").ap()
    H["wout"] = nc.dram_tensor("wout", [D, D], F32, kind="ExternalInput").ap()
    H["bout"] = nc.dram_tensor("bout", [D], F32, kind="ExternalInput").ap()
    H["w1"] = nc.dram_tensor("w1", [D, FF], F32, kind="ExternalInput").ap()
    H["b1"] = nc.dram_tensor("b1", [FF], F32, kind="ExternalInput").ap()
    H["w2"] = nc.dram_tensor("w2", [FF, D], F32, kind="ExternalInput").ap()
    H["b2"] = nc.dram_tensor("b2", [D], F32, kind="ExternalInput").ap()
    H["out"] = nc.dram_tensor("out", [4096, D], F32, kind="ExternalOutput").ap()
    H["ydbg"] = nc.dram_tensor("ydbg", [4096, D], F32, kind="ExternalOutput").ap() if debug else None

    H["qk_hi"] = nc.dram_tensor("qk_hi", [16, 8, 128, 128], BF16).ap()
    H["qk_lo"] = nc.dram_tensor("qk_lo", [16, 8, 128, 128], BF16).ap()
    H["wv_hi"] = nc.dram_tensor("wv_hi", [8, 128, D], BF16).ap()
    H["wv_lo"] = nc.dram_tensor("wv_lo", [8, 128, D], BF16).ap()
    H["wo_hi"] = nc.dram_tensor("wo_hi", [8, 128, D], BF16).ap()
    H["wo_lo"] = nc.dram_tensor("wo_lo", [8, 128, D], BF16).ap()
    H["w1_hi"] = nc.dram_tensor("w1_hi", [32, 8, 128, 128], BF16).ap()
    H["w1_lo"] = nc.dram_tensor("w1_lo", [32, 8, 128, 128], BF16).ap()
    H["w2_hi"] = nc.dram_tensor("w2_hi", [32, 128, D], F16).ap()
    H["w2_lo"] = nc.dram_tensor("w2_lo", [32, 128, D], F16).ap()
    H["b1r_hi"] = nc.dram_tensor("b1r_hi", [32, 1, 128], BF16).ap()
    H["b1r_lo"] = nc.dram_tensor("b1r_lo", [32, 1, 128], BF16).ap()

    from contextlib import ExitStack
    es = ExitStack()

    def sb(name, shape, dt=F32):
        return es.enter_context(nc.sbuf_tensor(name, shape, dt))

    B = {}
    B["x_t"] = None
    B["q_t"] = None
    B["bqk_sb"] = sb("bqk_sb", [128, 16])
    B["bvr_hi"] = sb("bvr_hi", [1, D], BF16)
    B["bvr_lo"] = sb("bvr_lo", [1, D], BF16)
    B["bor_hi"] = sb("bor_hi", [1, D], BF16)
    B["bor_lo"] = sb("bor_lo", [1, D], BF16)
    B["b2r_hi"] = sb("b2r_hi", [1, D], F16)
    B["b2r_lo"] = sb("b2r_lo", [1, D], F16)
    B["iota_p"] = sb("iota_p", [128, 1])
    B["iota_f"] = sb("iota_f", [128, 128])
    B["id_f32"] = sb("id_f32", [128, 128])
    B["id_bf"] = sb("id_bf", [128, 128], BF16)
    B["m0t"] = sb("m0t", [128, 128], BF16)
    B["m2t"] = sb("m2t", [128, 128], BF16)
    B["ones_bf"] = sb("ones_bf", [1, 512], BF16)
    B["ones_f16"] = sb("ones_f16", [1, 128], F16)
    B["kinkb"] = sb("kinkb", [128, NK])
    B["pmag"] = sb("pmag", [128, 1])
    B["nmag"] = sb("nmag", [128, 1])
    B["st_t"] = sb("st_t", [128, 16])

    with es, nc.Block() as block:
        sch = Sch(nc, es)
        st = St(sch, H, B)
        _emit_consts(nc, block, st)
        _emit_prep(nc, block, st)
        _barrier(block, st, include_y=True)

        B["x_t"] = sb("x_t", [128, 5 * D])
        B["y_t"] = sb("y_t", [128, 4 * D])
        B["q_t"] = sb("q_t", [128, 8 * 5 * 128])
        B["k_t"] = sb("k_t", [128, 8 * 6 * 128])
        B["vt_t"] = sb("vt_t", [128, 6 * 1040])
        B["xt_hi"] = sb("xt_hi", [128, 4 * 8 * 128], BF16)
        B["xt_lo"] = sb("xt_lo", [128, 4 * 8 * 128], BF16)
        B["at_hi"] = sb("at_hi", [128, 8 * 128], BF16)
        B["at_lo"] = sb("at_lo", [128, 8 * 128], BF16)
        B["attn_tm"] = sb("attn_tm", [128, D])
        B["e_t"] = sb("e_t", [128, 2 * 384])
        B["a_t"] = sb("a_t", [128, 32 * 512], F16)
        B["qks_hi"] = sb("qks_hi", [128, 2 * 1024], BF16)
        B["qks_lo"] = sb("qks_lo", [128, 2 * 1024], BF16)
        B["wvs_hi"] = sb("wvs_hi", [128, 2 * D], BF16)
        B["wvs_lo"] = sb("wvs_lo", [128, 2 * D], BF16)
        B["w1s_hi"] = B["qks_hi"]
        B["w1s_lo"] = B["qks_lo"]
        B["w2s_hi"] = B["wvs_hi"][:].bitcast(F16)
        B["w2s_lo"] = B["wvs_lo"][:].bitcast(F16)
        B["b1s_hi"] = sb("b1s_hi", [1, 2 * 128], BF16)
        B["b1s_lo"] = sb("b1s_lo", [1, 2 * 128], BF16)
        B["s1_t"] = sb("s1_t", [128, 2 * 512])
        B["rk_t"] = sb("rk_t", [128, 2 * 1024])
        B["s3_t"] = B["e_t"][:, 0:512]
        B["r2_t"] = B["a_t"][:, 0:2048].bitcast(F32)
        B["sq_t"] = B["a_t"][:, 2048:4096].bitcast(F32)
        B["oh16"] = B["attn_tm"][:].bitcast(F16)
        B["acc_t"] = B["attn_tm"][:, 0:512]
        B["t1_t"] = B["attn_tm"][:, 512:1024]
        B["u_t"] = None
        B["u0"] = B["at_hi"][:].bitcast(F32)
        B["u1"] = B["at_lo"][:].bitcast(F32)
        @block.vector
        def _(v):
            i = None
            for s in range(6):
                i = v.memset(B["vt_t"][:, s * 1040 + 64: (s + 1) * 1040: 65], 1.0)
            sch.inc("V", i)
        for t in range(limit_tiles):
            _emit_A1(nc, block, st, t, sub=("all" if phases not in ("x", "tr", "tr1", "tr1a", "qk") else phases))
            if phases in ("a1", "x", "tr", "tr1", "tr1a", "qk"):
                _barrier(block, st)
                continue
            _emit_A2(nc, block, st, t, ms=(None if t < NT - 1 else list(range(27, 31))))
            if phases == "a2":
                _barrier(block, st)
                continue
            if t == NT - 1:
                _emit_B(nc, block, st, t, list(range(27, 31)))
                _barrier(block, st)
                _emit_A2(nc, block, st, t, ms=[31], tag="x")
                _barrier(block, st)
                _emit_B(nc, block, st, t, [31])
            else:
                _emit_B(nc, block, st, t, list(range(max(0, 4 * t - 1), 4 * t + 3)))

        @block.gpsimd
        def _(g):
            g.wait_ge(sch.sem["DS"], sch.count["DS"])
    return nc


def _emit_consts(nc, block, st):
    sch, B = st.sch, st.B

    @block.gpsimd
    def _(g):
        g.iota(B["iota_p"][:], pattern=[[0, 1]], channel_multiplier=1,
               allow_small_or_imprecise_dtypes=True)
        i = g.iota(B["iota_f"][:], pattern=[[1, 128]], channel_multiplier=0,
                   allow_small_or_imprecise_dtypes=True)
        sch.inc("G", i)

    @block.vector
    def _(v):
        sch.wait(v, "V", sch.tok("G"))
        v.tensor_scalar(out=B["id_f32"][:], in0=B["iota_f"][:],
                        scalar1=B["iota_p"][:, 0:1], scalar2=None, op0=ALU.is_equal)
        v.tensor_copy(out=B["id_bf"][:], in_=B["id_f32"][:])
        v.tensor_scalar(out=B["m0t"][:], in0=B["iota_f"][:],
                        scalar1=B["iota_p"][:, 0:1], scalar2=-1.0e30,
                        op0=ALU.is_lt, op1=ALU.mult)
        v.tensor_scalar(out=B["m2t"][:], in0=B["iota_f"][:],
                        scalar1=B["iota_p"][:, 0:1], scalar2=-1.0e30,
                        op0=ALU.is_gt, op1=ALU.mult)
        v.memset(B["ones_bf"][:], 1.0)
        v.memset(B["ones_f16"][:], 1.0)
        for k in range(NK):
            v.memset(B["kinkb"][:, k:k + 1],
                     -KINK_K[k] if k in TK else -abs(KINK_C[k]) * KINK_K[k])
        v.memset(B["pmag"][:], MAGIC)
        i = v.memset(B["nmag"][:], -MAGIC)
        sch.inc("V", i)


def _emit_prep(nc, block, st):
    sch, H, B = st.sch, st.H, st.B
    with nc.sbuf_tensor("pf0", [128, 4096], F32) as pf0, \
         nc.sbuf_tensor("pf1", [128, 4096], F32) as pf1, \
         nc.sbuf_tensor("ph0", [128, 4096], BF16) as ph0, \
         nc.sbuf_tensor("ph1", [128, 4096], BF16) as ph1, \
         nc.sbuf_tensor("pl0", [128, 4096], BF16) as pl0, \
         nc.sbuf_tensor("pl1", [128, 4096], BF16) as pl1, \
         nc.sbuf_tensor("ph16", [128, 2 * 1024], F16) as ph16, \
         nc.sbuf_tensor("pl16", [128, 2 * 1024], F16) as pl16, \
         nc.sbuf_tensor("brow", [1, 4096], F32) as brow, \
         nc.sbuf_tensor("brh", [1, 4096], BF16) as brh, \
         nc.sbuf_tensor("brl", [1, 4096], BF16) as brl:

        pf = [pf0, pf1]
        phh = [ph0, ph1]
        pll = [pl0, pl1]
        jobs = []
        for dc in range(8):
            jobs.append((H["wqkv"][dc * 128:(dc + 1) * 128, 0:2048], 2048,
                         H["qk_hi"][:, dc], H["qk_lo"][:, dc], "cc"))
        for dc in range(8):
            jobs.append((H["wqkv"][dc * 128:(dc + 1) * 128, 2048:3072], 1024,
                         H["wv_hi"][dc], H["wv_lo"][dc], None))
        for dc in range(8):
            jobs.append((H["wout"][dc * 128:(dc + 1) * 128, :], 1024,
                         H["wo_hi"][dc], H["wo_lo"][dc], None))
        for dc in range(8):
            jobs.append((H["w1"][dc * 128:(dc + 1) * 128, :], 4096,
                         H["w1_hi"][:, dc], H["w1_lo"][:, dc], "fc"))
        for fc in range(32):
            jobs.append((H["w2"][fc * 128:(fc + 1) * 128, :], 1024,
                         H["w2_hi"][fc], H["w2_lo"][fc], "f16"))

        for j, (src, n, hid, lod, mode) in enumerate(jobs):
            p = j % 2

            @block.sync
            def _(y, j=j, p=p, src=src, n=n):
                sch.wait(y, "Y", st.r.get(("pst", j - 2), None))
                d = y.dma_start(out=pf[p][:, 0:n], in_=src)
                st.r[("pld", j)] = sch.inc("DL", d)

            @block.vector
            def _(v, j=j, p=p, n=n, mode=mode):
                sch.wait(v, "V", st.r[("pld", j)])
                sch.wait(v, "V", st.r.get(("pst", j - 2), None))
                if mode == "f16":
                    v.tensor_copy(out=ph16[:, p * 1024:p * 1024 + n], in_=pf[p][:, 0:n])
                    i = v.scalar_tensor_tensor(
                        out=pl16[:, p * 1024:p * 1024 + n],
                        in0=ph16[:, p * 1024:p * 1024 + n], scalar=-1.0,
                        in1=pf[p][:, 0:n], op0=ALU.mult, op1=ALU.add)
                else:
                    v.tensor_copy(out=phh[p][:, 0:n], in_=pf[p][:, 0:n])
                    i = v.scalar_tensor_tensor(
                        out=pll[p][:, 0:n], in0=phh[p][:, 0:n], scalar=-1.0,
                        in1=pf[p][:, 0:n], op0=ALU.mult, op1=ALU.add)
                st.r[("psp", j)] = sch.inc("V", i)

            @block.gpsimd
            def _(g, j=j, p=p, n=n, hid=hid, lod=lod, mode=mode):
                sch.wait(g, "G", st.r[("psp", j)])
                if mode == "f16":
                    shi = ph16[:, p * 1024:p * 1024 + n]
                    slo = pl16[:, p * 1024:p * 1024 + n]
                else:
                    shi = phh[p][:, 0:n]
                    slo = pll[p][:, 0:n]
                if mode in ("cc", "fc"):
                    shi = shi.rearrange("p (c m) -> p c m", m=128)
                    slo = slo.rearrange("p (c m) -> p c m", m=128)
                    hid = hid.rearrange("c p m -> p c m")
                    lod = lod.rearrange("c p m -> p c m")
                d = g.dma_start(out=hid, in_=shi)
                sch.inc("DS", d)
                d = g.dma_start(out=lod, in_=slo)
                st.r[("pst", j)] = sch.inc("DS", d)

        @block.sync
        def _(y):
            sch.wait(y, "Y", st.r[("pst", len(jobs) - 1)])
            for cc in range(16):
                d = y.dma_start(out=B["bqk_sb"][:, cc:cc + 1],
                                in_=H["bqkv"][cc * 128:(cc + 1) * 128])
                sch.inc("DL", d)
            d = y.dma_start(out=brow[0:1, 0:1024], in_=H["bqkv"][2048:3072])
            sch.inc("DL", d)
            d = y.dma_start(out=brow[0:1, 1024:2048], in_=H["bout"])
            sch.inc("DL", d)
            d = y.dma_start(out=brow[0:1, 2048:3072], in_=H["b2"])
            sch.inc("DL", d)

        @block.vector
        def _(v):
            sch.wait(v, "V", sch.tok("DL"))
            v.tensor_copy(out=B["bvr_hi"][:], in_=brow[0:1, 0:1024])
            v.scalar_tensor_tensor(out=B["bvr_lo"][:], in0=B["bvr_hi"][:], scalar=-1.0,
                                   in1=brow[0:1, 0:1024], op0=ALU.mult, op1=ALU.add)
            v.tensor_copy(out=B["bor_hi"][:], in_=brow[0:1, 1024:2048])
            v.scalar_tensor_tensor(out=B["bor_lo"][:], in0=B["bor_hi"][:], scalar=-1.0,
                                   in1=brow[0:1, 1024:2048], op0=ALU.mult, op1=ALU.add)
            v.tensor_scalar(out=brow[0:1, 3072:4096], in0=brow[0:1, 2048:3072],
                            scalar1=256.0, scalar2=None, op0=ALU.mult)
            v.tensor_copy(out=B["b2r_hi"][:], in_=brow[0:1, 3072:4096])
            i = v.scalar_tensor_tensor(out=B["b2r_lo"][:], in0=B["b2r_hi"][:], scalar=-1.0,
                                       in1=brow[0:1, 3072:4096], op0=ALU.mult, op1=ALU.add)
            st.r[("bprep",)] = sch.inc("V", i)

        @block.sync
        def _(y):
            sch.wait(y, "Y", st.r[("bprep",)])
            d = y.dma_start(out=brow[0:1, 0:4096], in_=H["b1"])
            sch.inc("DL", d)

        @block.vector
        def _(v):
            sch.wait(v, "V", sch.tok("DL"))
            v.tensor_copy(out=brh[:], in_=brow[0:1, 0:4096])
            i = v.scalar_tensor_tensor(out=brl[:], in0=brh[:], scalar=-1.0,
                                       in1=brow[0:1, 0:4096], op0=ALU.mult, op1=ALU.add)
            st.r[("b1sp",)] = sch.inc("V", i)

        @block.gpsimd
        def _(g):
            sch.wait(g, "G", st.r[("b1sp",)])
            d = g.dma_start(out=H["b1r_hi"].rearrange("fc o m -> o fc m"),
                            in_=brh[:].rearrange("o (fc m) -> o fc m", m=128))
            sch.inc("DS", d)
            d = g.dma_start(out=H["b1r_lo"].rearrange("fc o m -> o fc m"),
                            in_=brl[:].rearrange("o (fc m) -> o fc m", m=128))
            sch.inc("DS", d)


def _emit_A1(nc, block, st, t, sub="all"):
    sch, H, B = st.sch, st.H, st.B
    blocks = list(range(4 * t, 4 * t + 4))

    @block.sync
    def _(y):
        sch.wait(y, "Y", sch.tok("DS"))
        for n in blocks:
            sch.wait(y, "Y", st.r.get((("ln1", n - 5), "r"), None))
            d = y.dma_start(out=B["x_t"][:, (n % 5) * D:(n % 5 + 1) * D],
                            in_=H["x"][n * 128:(n + 1) * 128, :])
            st.r[("x", n)] = sch.inc("DL", d)

    if sub == "x":
        return
    with nc.psum_tensor(f"ptr_{t}", [128, 1024], F32) as ptr:
        for bi, n in enumerate(blocks):
            if sub == "tr1" and bi > 0:
                break
            for dc in range(8):
                if sub in ("tr1", "tr1a") and dc >= 2:
                    break
                @block.tensor
                def _(te, n=n, dc=dc, bi=bi):
                    if dc == 0:
                        if bi == 0:
                            sch.wait(te, "T", st.b_s1last)
                        sch.wait(te, "T", st.r[("x", n)])
                    pb = dc % 2
                    prev = ("trev", n, dc - 2) if dc >= 2 else ("trev", n - 1, 6 + dc)
                    sch.wait(te, "T", st.r.get(prev, None))
                    i = te.transpose(ptr[:, pb * 512: pb * 512 + 128],
                                     B["x_t"][:, (n % 5) * D + dc * 128:(n % 5) * D + (dc + 1) * 128],
                                     B["id_f32"][:])
                    st.r[("tr", n, dc)] = sch.inc("T", i)

                if sub != "tr1a":
                    @block.vector
                    def _(v, n=n, dc=dc, bi=bi):
                        pb = dc % 2
                        col = ((n % 4) * 8 + dc) * 128
                        if bi == 0 and dc == 0:
                            sch.wait(v, "V", st.qks_last)
                        sch.wait(v, "V", st.r[("tr", n, dc)])
                        v.tensor_copy(out=B["xt_hi"][:, col:col + 128],
                                      in_=ptr[:, pb * 512: pb * 512 + 128])
                        i = v.scalar_tensor_tensor(
                            out=B["xt_lo"][:, col:col + 128],
                            in0=B["xt_hi"][:, col:col + 128], scalar=-1.0,
                            in1=ptr[:, pb * 512: pb * 512 + 128], op0=ALU.mult, op1=ALU.add)
                        st.r[("trev", n, dc)] = sch.inc("V", i)

    xt_ready = sch.tok("V")
    if sub in ("tr", "tr1", "tr1a"):
        return

    with nc.psum_tensor(f"pqk_{t}", [128, 2 * 512], F32) as pqk:
        q_base = sch.count["S"]
        for cc in range(16):
            st.r[("qkev", t, cc)] = ("S", q_base + cc + 1)
        for cc in range(16):
            sl = cc % 2

            @block.sync
            def _(y, cc=cc, sl=sl):
                if cc < 2:
                    sch.wait(y, "Y", st.qks_last)
                sch.wait(y, "Y", st.r.get(("qkmm", t, cc - 2), None))
                d = y.dma_start(out=B["qks_hi"][:, sl * 1024:(sl + 1) * 1024].rearrange(
                    "p (dc m) -> p dc m", m=128), in_=H["qk_hi"][cc].rearrange("dc p m -> p dc m"))
                sch.inc("DL", d)
                d = y.dma_start(out=B["qks_lo"][:, sl * 1024:(sl + 1) * 1024].rearrange(
                    "p (dc m) -> p dc m", m=128), in_=H["qk_lo"][cc].rearrange("dc p m -> p dc m"))
                st.r[("qkw", t, cc)] = sch.inc("DL", d)

            @block.tensor
            def _(te, cc=cc, sl=sl):
                sch.wait(te, "T", st.r[("qkw", t, cc)])
                sch.wait(te, "T", xt_ready)
                sch.wait(te, "T", st.r.get(("qkev", t, cc - 2), None))
                rh4 = B["xt_hi"][:].rearrange("p (s dc m) -> p dc s m", dc=8, m=128)
                rl4 = B["xt_lo"][:].rearrange("p (s dc m) -> p dc s m", dc=8, m=128)
                i = None
                for dc in range(8):
                    whi = B["qks_hi"][:, sl * 1024 + dc * 128: sl * 1024 + (dc + 1) * 128]
                    wlo = B["qks_lo"][:, sl * 1024 + dc * 128: sl * 1024 + (dc + 1) * 128]
                    te.matmul(pqk[:, sl * 512:(sl + 1) * 512], whi, rh4[:, dc],
                              start=(dc == 0), stop=False)
                    te.matmul(pqk[:, sl * 512:(sl + 1) * 512], whi, rl4[:, dc],
                              start=False, stop=False)
                    i = te.matmul(pqk[:, sl * 512:(sl + 1) * 512], wlo, rh4[:, dc],
                                  start=False, stop=(dc == 7))
                st.r[("qkmm", t, cc)] = sch.inc("T", i)

        @block.scalar
        def _(s):
            sch.wait(s, "S", st.sc_guard)
            for cc in range(16):
                sl = cc % 2
                sch.wait(s, "S", st.r[("qkmm", t, cc)])
                dst = B["q_t"] if cc < 8 else B["k_t"]
                ring = 5 if cc < 8 else 6
                ccp = cc % 8
                for bi, n in enumerate(blocks):
                    rs = n % ring
                    i = s.activation(
                        out=dst[:, (ccp * ring + rs) * 128:(ccp * ring + rs + 1) * 128],
                        in_=pqk[:, sl * 512 + bi * 128: sl * 512 + (bi + 1) * 128],
                        func=AF.Identity, bias=B["bqk_sb"][:, cc:cc + 1], scale=1.0)
                sch.inc("S", i)

    qk_done = sch.tok("S")
    if sub == "qk":
        return

    with nc.psum_tensor(f"pv_{t}", [128, 4 * 1024], F32) as pv:
        for dc in range(8):
            sl = dc % 2

            @block.sync
            def _(y, dc=dc, sl=sl):
                if dc < 2:
                    sch.wait(y, "Y", st.wvs_last)
                sch.wait(y, "Y", st.r.get(("vmm", t, dc - 2), None))
                d = y.dma_start(out=B["wvs_hi"][:, sl * D:(sl + 1) * D], in_=H["wv_hi"][dc])
                sch.inc("DL", d)
                d = y.dma_start(out=B["wvs_lo"][:, sl * D:(sl + 1) * D], in_=H["wv_lo"][dc])
                st.r[("wv", t, dc)] = sch.inc("DL", d)

            @block.tensor
            def _(te, dc=dc, sl=sl):
                if dc == 0:
                    sch.wait(te, "T", qk_done)
                    sch.wait(te, "T", st.b_s3b)
                sch.wait(te, "T", st.r[("wv", t, dc)])
                i = None
                for bi, n in enumerate(blocks):
                    col = ((n % 4) * 8 + dc) * 128
                    xh = B["xt_hi"][:, col:col + 128]
                    xl = B["xt_lo"][:, col:col + 128]
                    for hf in range(2):
                        pvs = pv[:, bi * 1024 + hf * 512: bi * 1024 + (hf + 1) * 512]
                        te.matmul(pvs, xh,
                                  B["wvs_hi"][:, sl * D + hf * 512: sl * D + (hf + 1) * 512],
                                  start=(dc == 0), stop=False)
                        te.matmul(pvs, xh,
                                  B["wvs_lo"][:, sl * D + hf * 512: sl * D + (hf + 1) * 512],
                                  start=False, stop=False)
                        i = te.matmul(pvs, xl,
                                      B["wvs_hi"][:, sl * D + hf * 512: sl * D + (hf + 1) * 512],
                                      start=False, stop=False)
                st.r[("vmm", t, dc)] = sch.inc("T", i)

        @block.tensor
        def _(te):
            i = None
            for bi in range(4):
                for hf in range(2):
                    pvs = pv[:, bi * 1024 + hf * 512: bi * 1024 + (hf + 1) * 512]
                    te.matmul(pvs, B["ones_bf"][0:1, 0:128],
                              B["bvr_hi"][0:1, hf * 512:(hf + 1) * 512],
                              start=False, stop=False)
                    i = te.matmul(pvs, B["ones_bf"][0:1, 0:128],
                                  B["bvr_lo"][0:1, hf * 512:(hf + 1) * 512],
                                  start=False, stop=True)
            st.r[("vmm_all", t)] = sch.inc("T", i)
        st.qks_last = st.r[("qkmm", t, 15)]
        st.wvs_last = st.r[("vmm", t, 7)]

        @block.vector
        def _(v):
            sch.wait(v, "V", st.pv_guard)
            sch.wait(v, "V", st.r[("vmm_all", t)])
            for bi, n in enumerate(blocks):
                rs = n % 6
                dst = B["vt_t"][:, rs * 1040:(rs + 1) * 1040].rearrange(
                    "p (h c) -> p h c", c=65)[:, :, 0:64]
                src = pv[:, bi * 1024:(bi + 1) * 1024].rearrange(
                    "p (h c) -> p h c", c=64)
                i = v.tensor_copy(out=dst, in_=src)
                st.r[("v", n)] = sch.inc("V", i)


def _emit_A2(nc, block, st, t, ms=None, tag=""):
    sch, H, B = st.sch, st.H, st.B
    if ms is None:
        ms = list(range(max(0, 4 * t - 1), 4 * t + 3))

    with nc.psum_tensor(f"psc_{t}{tag}", [128, 2 * 512], F32) as psc, \
         nc.psum_tensor(f"ppv_{t}{tag}", [128, 1024], F32) as ppv, \
         nc.psum_tensor(f"ptr2_{t}{tag}", [128, 1024], F32) as ptr2, \
         nc.psum_tensor(f"pwo_{t}{tag}", [128, 1024], F32) as pwo:

        def emit_sc(m, h, chunks):
            cc = h // 2
            half = (h % 2) * 64
            pb = h % 2
            qs = m % 5

            @block.tensor
            def _(te):
                if m == ms[0] and h == 0:
                    sch.wait(te, "T", st.r.get(("qkev", t, 15), None))
                    sch.wait(te, "T", st.r.get(("v", 4 * t), None))
                    sch.wait(te, "T", st.b_s3b)
                sch.wait(te, "T", st.r.get(("exp", m, h - 2), None))
                if h <= 1:
                    sch.wait(te, "T", st.r.get(("exp", m - 1, 14 + h), None))
                i = None
                for (c, ks) in chunks:
                    lhs = B["k_t"][half:half + 64,
                                   (cc * 6 + ks) * 128:(cc * 6 + ks + 1) * 128]
                    rhs = B["q_t"][half:half + 64,
                                   (cc * 5 + qs) * 128:(cc * 5 + qs + 1) * 128]
                    i = te.matmul(psc[:, pb * 512 + c * 128: pb * 512 + (c + 1) * 128],
                                  lhs, rhs, start=True, stop=(c == 1))
                    if c == 0:
                        i = te.matmul(psc[:, pb * 512: pb * 512 + 128],
                                      B["m0t"][:], B["id_bf"][:], start=False, stop=True)
                    elif c == 2:
                        i = te.matmul(psc[:, pb * 512 + 256: pb * 512 + 384],
                                      B["m2t"][:], B["id_bf"][:], start=False, stop=True)
                st.r[("sc", m, h)] = sch.inc("T", i)

        def emit_exp_section(m, chunks):
            @block.scalar
            def _(s):
                if m == ms[0]:
                    sch.wait(s, "S", st.b_vdone)
                for h in range(16):
                    pb = h % 2
                    sch.wait(s, "S", st.r[("sc", m, h)])
                    sch.wait(s, "S", st.r.get(("pv", m, h - 2), None))
                    if h <= 1:
                        sch.wait(s, "S", st.r.get(("pv", m - 1, 14 + h), None))
                    c0, c1 = chunks[0][0], chunks[-1][0]
                    i = s.activation(out=B["e_t"][:, pb * 384 + c0 * 128: pb * 384 + (c1 + 1) * 128],
                                     in_=psc[:, pb * 512 + c0 * 128: pb * 512 + (c1 + 1) * 128],
                                     func=AF.Exp, bias=0.0, scale=SCALE)
                    sch.inc("S", i)

        def emit_pv(m, h, chunks):
            pb = h % 2

            @block.tensor
            def _(te):
                if m == ms[0] and h == 0:
                    sch.wait(te, "T", st.r.get(("v", 4 * t + 1), None))
                sch.wait(te, "T", st.r[("exp", m, h)])
                if h % 8 == 0:
                    sch.wait(te, "T", st.r.get(("anorm", m, h - 8), None))
                    if h == 0:
                        sch.wait(te, "T", st.r.get(("anorm", m - 1, 8), None))
                po = (h % 8) * 128
                first = True
                i = None
                for (c, ks) in chunks:
                    lhs = B["e_t"][:, pb * 384 + c * 128: pb * 384 + (c + 1) * 128]
                    rhs = B["vt_t"][:, ks * 1040 + h * 65: ks * 1040 + (h + 1) * 65]
                    i = te.matmul(ppv[:, po:po + 65], lhs, rhs,
                                  start=first, stop=(c == chunks[-1][0]))
                    first = False
                st.r[("pv", m, h)] = sch.inc("T", i)

        for m in ms:
            chunks = []
            if m > 0:
                chunks.append((0, (m - 1) % 6))
            chunks.append((1, m % 6))
            if m < 31:
                chunks.append((2, (m + 1) % 6))

            e_base = sch.count["S"]
            for h in range(16):
                st.r[("exp", m, h)] = ("S", e_base + h + 1)
            for h in range(16):
                emit_sc(m, h, chunks)
                if h >= 1:
                    emit_pv(m, h - 1, chunks)
                    if h == 8:
                        _emit_wave_norm(nc, block, st, m, 0, ppv)
            emit_pv(m, 15, chunks)
            emit_exp_section(m, chunks)
            _emit_wave_norm(nc, block, st, m, 1, ppv)

            for dc in range(8):
                @block.tensor
                def _(te, m=m, dc=dc):
                    if dc == 0:
                        if m == ms[0]:
                            sch.wait(te, "T", st.r.get(("v", 4 * t + 2), None))
                        sch.wait(te, "T", st.r[("anorm", m, 0)])
                    if dc == 4:
                        sch.wait(te, "T", st.r[("anorm", m, 8)])
                    pb2 = dc % 2
                    prev = ("atr_ev", m, dc - 2) if dc >= 2 else ("atr_ev", m - 1, 6 + dc)
                    sch.wait(te, "T", st.r.get(prev, None))
                    i = te.transpose(ptr2[:, pb2 * 512: pb2 * 512 + 128],
                                     B["attn_tm"][:, dc * 128:(dc + 1) * 128],
                                     B["id_f32"][:])
                    st.r[("atr", m, dc)] = sch.inc("T", i)

                @block.vector
                def _(v, m=m, dc=dc):
                    pb2 = dc % 2
                    sch.wait(v, "V", st.r[("atr", m, dc)])
                    sch.wait(v, "V", st.r.get(("womm", m - 1, dc), None))
                    v.tensor_copy(out=B["at_hi"][:, dc * 128:(dc + 1) * 128],
                                  in_=ptr2[:, pb2 * 512: pb2 * 512 + 128])
                    i = v.scalar_tensor_tensor(
                        out=B["at_lo"][:, dc * 128:(dc + 1) * 128],
                        in0=B["at_hi"][:, dc * 128:(dc + 1) * 128], scalar=-1.0,
                        in1=ptr2[:, pb2 * 512: pb2 * 512 + 128], op0=ALU.mult, op1=ALU.add)
                    st.r[("atr_ev", m, dc)] = sch.inc("V", i)

            for dc in range(8):
                sl = dc % 2

                @block.sync
                def _(y, m=m, dc=dc, sl=sl):
                    if dc < 2:
                        sch.wait(y, "Y", st.wvs_last)
                    sch.wait(y, "Y", st.r.get(("womm", m, dc - 2) if dc >= 2
                                              else ("womm", m - 1, 6 + dc), None))
                    d = y.dma_start(out=B["wvs_hi"][:, sl * D:(sl + 1) * D],
                                    in_=H["wo_hi"][dc])
                    sch.inc("DL", d)
                    d = y.dma_start(out=B["wvs_lo"][:, sl * D:(sl + 1) * D],
                                    in_=H["wo_lo"][dc])
                    st.r[("wow", m, dc)] = sch.inc("DL", d)

                @block.tensor
                def _(te, m=m, dc=dc, sl=sl):
                    if dc == 0:
                        if m == ms[0]:
                            sch.wait(te, "T", st.r.get(("v", 4 * t + 3), None))
                        sch.wait(te, "T", st.r.get((("ln1", m - 1), "r"), None))
                    sch.wait(te, "T", st.r[("atr_ev", m, dc)])
                    sch.wait(te, "T", st.r[("wow", m, dc)])
                    ah = B["at_hi"][:, dc * 128:(dc + 1) * 128]
                    al = B["at_lo"][:, dc * 128:(dc + 1) * 128]
                    i = None
                    for hf in range(2):
                        pws = pwo[:, hf * 512:(hf + 1) * 512]
                        te.matmul(pws, ah,
                                  B["wvs_hi"][:, sl * D + hf * 512: sl * D + (hf + 1) * 512],
                                  start=(dc == 0), stop=False)
                        te.matmul(pws, ah,
                                  B["wvs_lo"][:, sl * D + hf * 512: sl * D + (hf + 1) * 512],
                                  start=False, stop=False)
                        i = te.matmul(pws, al,
                                      B["wvs_hi"][:, sl * D + hf * 512: sl * D + (hf + 1) * 512],
                                      start=False, stop=False)
                    st.r[("womm", m, dc)] = sch.inc("T", i)

            @block.tensor
            def _(te, m=m):
                i = None
                for hf in range(2):
                    pws = pwo[:, hf * 512:(hf + 1) * 512]
                    te.matmul(pws, B["ones_bf"][0:1, 0:128],
                              B["bor_hi"][0:1, hf * 512:(hf + 1) * 512],
                              start=False, stop=False)
                    i = te.matmul(pws, B["ones_bf"][0:1, 0:128],
                                  B["bor_lo"][0:1, hf * 512:(hf + 1) * 512],
                                  start=False, stop=True)
                st.r[("wo_mm", m)] = sch.inc("T", i)
            st.wvs_last = st.r[("womm", m, 7)]

            _emit_ln(nc, block, st, src=pwo[:],
                     res_ap=B["x_t"][:, (m % 5) * D:(m % 5 + 1) * D],
                     out_ap=B["y_t"][:, (m % 4) * D:(m % 4 + 1) * D],
                     key=("ln1", m), dep=st.r[("wo_mm", m)])
            st.r[("y", m)] = st.r[("ln1", m)]
        st.sc_guard = st.r[("sc", ms[-1], 15)]
        st.pv_guard = st.r[("pv", ms[-1], 15)]
        st.a2_ln_last = st.r[("ln1", ms[-1])]
        st.a2_exp_last = st.r[("exp", ms[-1], 15)]


def _emit_wave_norm(nc, block, st, m, wave, ppv):
    sch, B = st.sch, st.B

    @block.vector
    def _(v):
        for h in range(wave * 8, wave * 8 + 8):
            sch.wait(v, "V", st.r[("pv", m, h)])
        if wave == 0:
            sch.wait(v, "V", st.r.get(("atr", m - 1, 7), None))
        i = v.reciprocal(out=B["st_t"][:, 8:16], in_=ppv[:, 64::128])
        sch.wait(v, "V", sch.inc("V", i))
        i = None
        for hh in range(8):
            h = wave * 8 + hh
            i = v.tensor_scalar(out=B["attn_tm"][:, h * 64:(h + 1) * 64],
                                in0=ppv[:, hh * 128: hh * 128 + 64],
                                scalar1=B["st_t"][:, 8 + hh:9 + hh], scalar2=None,
                                op0=ALU.mult)
        st.r[("anorm", m, wave * 8)] = sch.inc("V", i)


def _emit_ln(nc, block, st, src, res_ap, out_ap, key, dep):
    sch, B = st.sch, st.B
    r_buf = B["r2_t"]

    @block.vector
    def _(v):
        sch.wait(v, "V", dep)
        sch.wait(v, "V", st.last_ln_sq)
        i = v.scalar_tensor_tensor(out=r_buf, in0=src, scalar=1.0, in1=res_ap,
                                   op0=ALU.mult, op1=ALU.add,
                                   accum_out=B["st_t"][:, 0:1])
        st.r[(key, "r")] = sch.inc("V", i)

    _emit_ln_tail(nc, block, st, key, r_buf, out_ap)


def _emit_ln_tail(nc, block, st, key, r_buf, out_ap, pre_tok=None, sum2=False):
    sch, B = st.sch, st.B

    @block.scalar
    def _(s):
        sch.wait(s, "S", st.r[(key, "r")])
        i = s.activation(out=B["sq_t"], in_=r_buf, func=AF.Square,
                         bias=0.0, scale=1.0, accum_out=B["st_t"][:, 1:2])
        st.r[(key, "sq")] = sch.inc("S", i)
    st.last_ln_sq = st.r[(key, "sq")]

    @block.vector
    def _(v):
        sch.wait(v, "V", st.r[(key, "sq")])
        stt = B["st_t"]
        if sum2:
            i = v.tensor_tensor(out=stt[:, 0:1], in0=stt[:, 0:1],
                                in1=stt[:, 9:10], op=ALU.add)
            sch.wait(v, "V", sch.inc("V", i))
        i = v.tensor_scalar(out=stt[:, 2:3], in0=stt[:, 0:1], scalar1=1.0 / D,
                            scalar2=None, op0=ALU.mult)
        sch.wait(v, "V", sch.inc("V", i))
        i = v.tensor_tensor(out=stt[:, 3:4], in0=stt[:, 2:3], in1=stt[:, 2:3],
                            op=ALU.mult)
        sch.wait(v, "V", sch.inc("V", i))
        i = v.scalar_tensor_tensor(out=stt[:, 4:5], in0=stt[:, 1:2], scalar=1.0 / D,
                                   in1=stt[:, 3:4], op0=ALU.mult, op1=ALU.subtract)
        sch.wait(v, "V", sch.inc("V", i))
        i = v.tensor_scalar(out=stt[:, 4:5], in0=stt[:, 4:5], scalar1=1.0e-5,
                            scalar2=None, op0=ALU.add)
        st.r[(key, "v")] = sch.inc("V", i)

    @block.scalar
    def _(s):
        sch.wait(s, "S", st.r[(key, "v")])
        i = s.activation(out=B["st_t"][:, 5:6], in_=B["st_t"][:, 4:5],
                         func=AF.Sqrt, bias=0.0, scale=1.0)
        st.r[(key, "s0")] = sch.inc("S", i)

    @block.vector
    def _(v):
        sch.wait(v, "V", st.r[(key, "s0")])
        sch.wait(v, "V", pre_tok)
        stt = B["st_t"]
        i = v.reciprocal(out=stt[:, 6:7], in_=stt[:, 5:6])
        sch.wait(v, "V", sch.inc("V", i))
        if isinstance(out_ap, tuple):
            v.tensor_scalar(out=out_ap[0], in0=r_buf[:, 0:512], scalar1=stt[:, 2:3],
                            scalar2=stt[:, 6:7], op0=ALU.subtract, op1=ALU.mult)
            i = v.tensor_scalar(out=out_ap[1], in0=r_buf[:, 512:1024], scalar1=stt[:, 2:3],
                                scalar2=stt[:, 6:7], op0=ALU.subtract, op1=ALU.mult)
        else:
            i = v.tensor_scalar(out=out_ap, in0=r_buf, scalar1=stt[:, 2:3],
                                scalar2=stt[:, 6:7], op0=ALU.subtract, op1=ALU.mult)
        st.r[key] = sch.inc("V", i)


def _emit_B(nc, block, st, t, blocks):
    sch, H, B = st.sch, st.H, st.B
    nb = len(blocks)
    ntok = nb * 128
    bid = (t, blocks[0])

    if H.get("ydbg") is not None:
        @block.gpsimd
        def _(g):
            for m in blocks:
                sch.wait(g, "G", st.r[("y", m)])
                d = g.dma_start(out=H["ydbg"][m * 128:(m + 1) * 128, :],
                                in_=B["y_t"][:, (m % 4) * D:(m % 4 + 1) * D])
                sch.inc("DS", d)

    with nc.psum_tensor(f"ptr3_{t}_{blocks[0]}", [128, 1024], F32) as ptr3:
        for bi, m in enumerate(blocks):
            for dc in range(8):
                @block.tensor
                def _(te, m=m, bi=bi, dc=dc):
                    if dc == 0:
                        if bi == 0:
                            sch.wait(te, "T", st.a2_exp_last)
                        sch.wait(te, "T", st.r[("y", m)])
                    pb = dc % 2
                    prev = ("ytev", bid, bi, dc - 2) if dc >= 2 else ("ytev", bid, bi - 1, 6 + dc)
                    sch.wait(te, "T", st.r.get(prev, None))
                    i = te.transpose(ptr3[:, pb * 512: pb * 512 + 128],
                                     B["y_t"][:, (m % 4) * D + dc * 128:(m % 4) * D + (dc + 1) * 128],
                                     B["id_f32"][:])
                    st.r[("ytr", bid, bi, dc)] = sch.inc("T", i)

                @block.vector
                def _(v, m=m, bi=bi, dc=dc):
                    pb = dc % 2
                    col = (bi * 8 + dc) * 128
                    sch.wait(v, "V", st.r[("ytr", bid, bi, dc)])
                    i = v.tensor_copy(out=B["xt_hi"][:, col:col + 128],
                                      in_=ptr3[:, pb * 512: pb * 512 + 128])
                    i = v.scalar_tensor_tensor(
                        out=B["xt_lo"][:, col:col + 128],
                        in0=B["xt_hi"][:, col:col + 128], scalar=-1.0,
                        in1=ptr3[:, pb * 512: pb * 512 + 128], op0=ALU.mult, op1=ALU.add)
                    st.r[("ytev", bid, bi, dc)] = sch.inc("V", i)

    yt_ready = sch.tok("V")

    def emit_w1_stream(fc):
        sl = fc % 2

        @block.sync
        def _(y):
            if fc < 2:
                sch.wait(y, "Y", st.qks_last)
            sch.wait(y, "Y", st.r.get(("ff1mm", bid, fc - 2), None))
            d = y.dma_start(out=B["w1s_hi"][:, sl * 1024:(sl + 1) * 1024].rearrange(
                "p (dc m) -> p dc m", m=128), in_=H["w1_hi"][fc].rearrange("dc p m -> p dc m"))
            sch.inc("DL", d)
            d = y.dma_start(out=B["w1s_lo"][:, sl * 1024:(sl + 1) * 1024].rearrange(
                "p (dc m) -> p dc m", m=128), in_=H["w1_lo"][fc].rearrange("dc p m -> p dc m"))
            sch.inc("DL", d)
            d = y.dma_start(out=B["b1s_hi"][0:1, sl * 128:(sl + 1) * 128],
                            in_=H["b1r_hi"][fc])
            sch.inc("DL", d)
            d = y.dma_start(out=B["b1s_lo"][0:1, sl * 128:(sl + 1) * 128],
                            in_=H["b1r_lo"][fc])
            st.r[("w1w", bid, fc)] = sch.inc("DL", d)

    def emit_w2_stream(fc):
        sl = fc % 2

        @block.sync
        def _(y):
            if fc < 2:
                sch.wait(y, "Y", st.wvs_last)
            sch.wait(y, "Y", st.r.get(("ff2b", bid, fc - 2), None))
            d = y.dma_start(out=B["w2s_hi"][:, sl * D: (sl + 1) * D],
                            in_=H["w2_hi"][fc])
            sch.inc("DL", d)
            d = y.dma_start(out=B["w2s_lo"][:, sl * D: (sl + 1) * D],
                            in_=H["w2_lo"][fc])
            st.r[("w2w", bid, fc)] = sch.inc("DL", d)

    with nc.psum_tensor(f"ph_{t}_{blocks[0]}", [128, 2 * 512], F32) as ph, \
         nc.psum_tensor(f"po_{t}_{blocks[0]}", [128, 4 * 512], F32) as po, \
         nc.psum_tensor(f"po1_{t}_{blocks[0]}", [128, 2 * 512], F32) as po1:

        def emit_ff1_T(fc):
            sl = fc % 2

            @block.tensor
            def _(te):
                sch.wait(te, "T", st.r[("w1w", bid, fc)])
                sch.wait(te, "T", yt_ready)
                if fc >= 2:
                    sch.wait(te, "T", ("S", st.b_sbase
                                       + S_PER * ((fc - 2) // gf)
                                       + ((fc - 2) % gf) + 1))
                rh4 = B["xt_hi"][:, 0:nb * 8 * 128].rearrange(
                    "p (s dc m) -> p dc s m", dc=8, m=128)
                rl4 = B["xt_lo"][:, 0:nb * 8 * 128].rearrange(
                    "p (s dc m) -> p dc s m", dc=8, m=128)
                pp = ph[:, (fc % 2) * 512:(fc % 2) * 512 + ntok]
                for dc in range(8):
                    whi = B["w1s_hi"][:, sl * 1024 + dc * 128: sl * 1024 + (dc + 1) * 128]
                    wlo = B["w1s_lo"][:, sl * 1024 + dc * 128: sl * 1024 + (dc + 1) * 128]
                    te.matmul(pp, whi, rh4[:, dc], start=(dc == 0), stop=False)
                    te.matmul(pp, whi, rl4[:, dc], start=False, stop=False)
                    te.matmul(pp, wlo, rh4[:, dc], start=False, stop=False)
                te.matmul(pp, B["b1s_hi"][0:1, sl * 128:(sl + 1) * 128],
                          B["ones_bf"][0:1, 0:ntok], start=False, stop=False)
                i = te.matmul(pp, B["b1s_lo"][0:1, sl * 128:(sl + 1) * 128],
                              B["ones_bf"][0:1, 0:ntok], start=False, stop=True)
                st.r[("ff1mm", bid, fc)] = sch.inc("T", i)

        def rslot(g, k):
            return (RK.index(k) % 4) * 512

        def s_cnt(g, off):
            return ("S", st.b_sbase + S_PER * g + off)

        def v_cnt(g, off):
            return ("V", st.b_vbase + V_PER * g + off)

        def emit_epi_S():
            @block.scalar
            def _(s):
                sch.wait(s, "S", st.last_store)
                for fc in range(32):
                    p5 = (fc % 2) * 512
                    sch.wait(s, "S", v_cnt(fc - 2, 2))
                    sch.wait(s, "S", st.r[("ff1mm", bid, fc)])
                    pp = ph[:, (fc % 2) * 512:(fc % 2) * 512 + ntok]
                    i = s.activation(out=B["s1_t"][:, p5:p5 + ntok], in_=pp,
                                     func=AF.Identity, bias=B["pmag"][:, 0:1],
                                     scale=256.0)
                    sch.inc("S", i)
                    uap = B["u0"] if fc % 2 == 0 else B["u1"]
                    i = s.activation(out=uap[:, 0:ntok],
                                     in_=B["s1_t"][:, p5:p5 + ntok],
                                     func=AF.Abs, bias=B["nmag"][:, 0:1], scale=1.0)
                    sch.inc("S", i)
                    sch.wait(s, "S", v_cnt(fc - 1, 19))
                    for k in RK:
                        i = s.activation(
                            out=B["rk_t"][:, rslot(fc, k): rslot(fc, k) + ntok],
                            in_=uap[:, 0:ntok], func=AF.Relu,
                            bias=B["kinkb"][:, k:k + 1],
                            scale=(1.0 if k in TK else abs(KINK_C[k])))
                        sch.inc("S", i)

        def emit_epi_V(g):
            @block.vector
            def _(v):
                p5 = (g % 2) * 512
                uap = B["u0"] if g % 2 == 0 else B["u1"]
                sch.wait(v, "V", s_cnt(g, gf))
                i = v.tensor_scalar(out=B["acc_t"][:, 0:ntok_g],
                                    in0=B["s1_t"][:, p5:p5 + ntok_g],
                                    scalar1=0.5, scalar2=-MAGIC / 2,
                                    op0=ALU.mult, op1=ALU.add)
                sch.inc("V", i)
                sch.wait(v, "V", s_cnt(g, gf + 1))
                i = v.scalar_tensor_tensor(out=B["acc_t"][:, 0:ntok_g],
                                           in0=uap[:, 0:ntok_g],
                                           scalar=A0, in1=B["acc_t"][:, 0:ntok_g],
                                           op0=ALU.mult, op1=ALU.add)
                sch.inc("V", i)
                for idx, k in enumerate(RK):
                    sch.wait(v, "V", s_cnt(g, gf + 2 + idx))
                    i = v.scalar_tensor_tensor(
                        out=B["acc_t"][:, 0:ntok_g],
                        in0=B["rk_t"][:, rslot(g, k): rslot(g, k) + ntok_g],
                        scalar=(KINK_C[k] if k in TK
                                else (1.0 if KINK_C[k] > 0 else -1.0)),
                        in1=B["acc_t"][:, 0:ntok_g], op0=ALU.mult, op1=ALU.add)
                    sch.inc("V", i)
                i = v.tensor_scalar(out=B["a_t"][:, g * 512: g * 512 + ntok_g],
                                    in0=B["acc_t"][:, 0:ntok_g], scalar1=MAGIC,
                                    scalar2=-MAGIC, op0=ALU.add, op1=ALU.add)
                st.r[("aq", bid, g)] = sch.inc("V", i)

        nb1 = min(2, nb)
        gf = 1
        ngrp = 32 // gf
        ntok_g = ntok * gf
        S_PER = gf + 1 + len(RK)
        V_PER = 20

        def a_slice(fc, sub):
            if gf == 1:
                return B["a_t"][:, fc * 512 + sub * 128: fc * 512 + (sub + 1) * 128]
            return B["a_t"][:, fc * 128: (fc + 1) * 128]

        def emit_ff2a(fc):
            sl2 = fc % 2

            @block.tensor
            def _(te):
                sch.wait(te, "T", ("V", st.b_vbase + V_PER * (fc // gf) + 20))
                sch.wait(te, "T", st.r[("w2w", bid, fc)])
                i = None
                for sub in range(nb):
                    a_ap = a_slice(fc, sub)
                    te.matmul(po[:, sub * 512:(sub + 1) * 512], a_ap,
                              B["w2s_hi"][:, sl2 * D: sl2 * D + 512],
                              start=(fc == 0), stop=False)
                    i = te.matmul(po[:, sub * 512:(sub + 1) * 512], a_ap,
                                  B["w2s_lo"][:, sl2 * D: sl2 * D + 512],
                                  start=False, stop=(fc == 31))
                    if fc == 0:
                        te.matmul(po[:, sub * 512:(sub + 1) * 512],
                                  B["ones_f16"][0:1, 0:128],
                                  B["b2r_hi"][0:1, 0:512],
                                  start=False, stop=False)
                        i = te.matmul(po[:, sub * 512:(sub + 1) * 512],
                                      B["ones_f16"][0:1, 0:128],
                                      B["b2r_lo"][0:1, 0:512],
                                      start=False, stop=False)
                st.r[("ff2a", bid, fc)] = sch.inc("T", i)

        def emit_ff2b(fc):
            sl2 = fc % 2

            @block.tensor
            def _(te):
                i = None
                for sub in range(nb1):
                    a_ap = a_slice(fc, sub)
                    te.matmul(po1[:, sub * 512:(sub + 1) * 512], a_ap,
                              B["w2s_hi"][:, sl2 * D + 512: (sl2 + 1) * D],
                              start=(fc == 0), stop=False)
                    i = te.matmul(po1[:, sub * 512:(sub + 1) * 512], a_ap,
                                  B["w2s_lo"][:, sl2 * D + 512: (sl2 + 1) * D],
                                  start=False, stop=(fc == 31))
                    if fc == 0:
                        te.matmul(po1[:, sub * 512:(sub + 1) * 512],
                                  B["ones_f16"][0:1, 0:128],
                                  B["b2r_hi"][0:1, 512:1024],
                                  start=False, stop=False)
                        i = te.matmul(po1[:, sub * 512:(sub + 1) * 512],
                                      B["ones_f16"][0:1, 0:128],
                                      B["b2r_lo"][0:1, 512:1024],
                                      start=False, stop=False)
                st.r[("ff2b", bid, fc)] = sch.inc("T", i)

        st.b_sbase = sch.count["S"]
        st.b_vbase = sch.count["V"]
        lag = 2 * gf
        emit_w2_stream(0)
        emit_w2_stream(1)
        for pr in range(16):
            f0 = 2 * pr
            emit_w1_stream(f0)
            emit_w1_stream(f0 + 1)
            emit_ff1_T(f0)
            emit_ff1_T(f0 + 1)
            if f0 >= lag:
                emit_ff2a(f0 - lag)
                emit_ff2b(f0 - lag)
                emit_ff2a(f0 - lag + 1)
                emit_ff2b(f0 - lag + 1)
                if f0 - lag + 2 < 32:
                    emit_w2_stream(f0 - lag + 2)
                if f0 - lag + 3 < 32:
                    emit_w2_stream(f0 - lag + 3)
        for fc in range(32 - lag, 32):
            emit_ff2a(fc)
            emit_ff2b(fc)
            if fc + 2 < 32:
                emit_w2_stream(fc + 2)
        emit_epi_S()
        for g in range(ngrp):
            emit_epi_V(g)

        for sub, m in enumerate(blocks):
            @block.scalar
            def _(s, sub=sub, m=m):
                sch.wait(s, "S", st.r[("ff2a", bid, 31)])
                if sub > 0:
                    sch.wait(s, "S", st.r.get(("oqa", blocks[sub - 1]), None))
                i = s.activation(out=B["s3_t"], in_=po[:, sub * 512:(sub + 1) * 512],
                                 func=AF.Identity, bias=B["pmag"][:, 0:1], scale=1.0)
                st.r[("s3a", m)] = sch.inc("S", i)

            @block.vector
            def _(v, sub=sub, m=m):
                sch.wait(v, "V", st.r[("s3a", m)])
                i = v.tensor_scalar(out=B["oh16"][:, sub * 512:(sub + 1) * 512],
                                    in0=B["s3_t"], scalar1=-MAGIC,
                                    scalar2=None, op0=ALU.add)
                st.r[("oqa", m)] = sch.inc("V", i)

        def emit_ln2(sub, m, src_po, gate_key):
            @block.scalar
            def _(s, sub=sub, m=m):
                sch.wait(s, "S", st.r[gate_key])
                if sub == 0:
                    sch.wait(s, "S", st.r[("oqa", blocks[-1])])
                sch.wait(s, "S", st.r.get(("r2", m - 1), None))
                i = s.activation(out=B["s3_t"], in_=src_po,
                                 func=AF.Identity, bias=B["pmag"][:, 0:1], scale=1.0)
                st.r[("s3b", m)] = sch.inc("S", i)

            @block.vector
            def _(v, sub=sub, m=m):
                sch.wait(v, "V", st.r[("s3b", m)])
                sch.wait(v, "V", st.last_ln_sq)
                sch.wait(v, "V", st.r.get(("ff2c", bid, 15), None))
                v.tensor_scalar(out=B["u0"][:, 0:512], in0=B["s3_t"],
                                scalar1=-MAGIC, scalar2=1.0 / 256.0,
                                op0=ALU.add, op1=ALU.mult)
                v.scalar_tensor_tensor(
                    out=B["r2_t"][:, 0:512], in0=B["oh16"][:, sub * 512:(sub + 1) * 512],
                    scalar=1.0 / 256.0, in1=B["y_t"][:, (m % 4) * D:(m % 4) * D + 512],
                    op0=ALU.mult, op1=ALU.add, accum_out=B["st_t"][:, 0:1])
                i = v.scalar_tensor_tensor(
                    out=B["r2_t"][:, 512:1024], in0=B["u0"][:, 0:512],
                    scalar=1.0, in1=B["y_t"][:, (m % 4) * D + 512:(m % 4 + 1) * D],
                    op0=ALU.mult, op1=ALU.add, accum_out=B["st_t"][:, 9:10])
                st.r[(("ln2", m), "r")] = sch.inc("V", i)
                st.r[("r2", m)] = st.r[(("ln2", m), "r")]

            _emit_ln_tail(nc, block, st, ("ln2", m), B["r2_t"],
                          (B["s1_t"][:, 0:512], B["s1_t"][:, 512:1024]),
                          pre_tok=st.last_store, sum2=True)
            st.r[("ob", m)] = st.r[("ln2", m)]

            @block.gpsimd
            def _(g, m=m):
                sch.wait(g, "G", st.r[("ob", m)])
                d = g.dma_start(out=st.H["out"][m * 128:(m + 1) * 128, 0:512],
                                in_=B["s1_t"][:, 0:512])
                sch.inc("DS", d)
                d = g.dma_start(out=st.H["out"][m * 128:(m + 1) * 128, 512:1024],
                                in_=B["s1_t"][:, 512:1024])
                st.last_store = sch.inc("DS", d)

        if nb > 2:
            w1h16 = B["w1s_hi"][:].bitcast(F16)
            w1l16 = B["w1s_lo"][:].bitcast(F16)

            def wc_hi(sl):
                if sl < 2:
                    return B["w2s_hi"][:, sl * D: sl * D + 512]
                return w1h16[:, (sl - 2) * 512: (sl - 1) * 512]

            def wc_lo(sl):
                if sl < 2:
                    return B["w2s_lo"][:, sl * D: sl * D + 512]
                return w1l16[:, (sl - 2) * 512: (sl - 1) * 512]

            for fc in range(32):
                sl = fc % 4

                @block.sync
                def _(y, fc=fc, sl=sl):
                    if fc < 2:
                        sch.wait(y, "Y", st.r[("ff2b", bid, 30 + sl)])
                    elif fc < 4:
                        sch.wait(y, "Y", st.r[("ff1mm", bid, 28 + fc)])
                    sch.wait(y, "Y", st.r.get(("ff2c", bid, fc - 4), None))
                    d = y.dma_start(out=wc_hi(sl), in_=H["w2_hi"][fc][:, 512:1024])
                    sch.inc("DL", d)
                    d = y.dma_start(out=wc_lo(sl), in_=H["w2_lo"][fc][:, 512:1024])
                    st.r[("w2wc", bid, fc)] = sch.inc("DL", d)

                @block.tensor
                def _(te, fc=fc, sl=sl):
                    if fc == 0:
                        sch.wait(te, "T", st.r[("s3a", blocks[-1])])
                    sch.wait(te, "T", st.r[("w2wc", bid, fc)])
                    i = None
                    for j, sub in enumerate(range(2, nb)):
                        a_ap = a_slice(fc, sub)
                        te.matmul(po[:, j * 512:(j + 1) * 512], a_ap,
                                  wc_hi(sl),
                                  start=(fc == 0), stop=False)
                        i = te.matmul(po[:, j * 512:(j + 1) * 512], a_ap,
                                      wc_lo(sl),
                                      start=False, stop=(fc == 31))
                        if fc == 0:
                            te.matmul(po[:, j * 512:(j + 1) * 512],
                                      B["ones_f16"][0:1, 0:128],
                                      B["b2r_hi"][0:1, 512:1024],
                                      start=False, stop=False)
                            i = te.matmul(po[:, j * 512:(j + 1) * 512],
                                          B["ones_f16"][0:1, 0:128],
                                          B["b2r_lo"][0:1, 512:1024],
                                          start=False, stop=False)
                    st.r[("ff2c", bid, fc)] = sch.inc("T", i)

            st.wvs_last = st.r[("ff2c", bid, 31)]
            st.qks_last = st.r[("ff2c", bid, 31)]
        else:
            st.wvs_last = st.r[("ff2b", bid, 31)]
        for sub in range(nb1):
            emit_ln2(sub, blocks[sub], po1[:, sub * 512:(sub + 1) * 512],
                     ("ff2b", bid, 31))
        for j, sub in enumerate(range(2, nb)):
            emit_ln2(sub, blocks[sub], po[:, j * 512:(j + 1) * 512],
                     ("ff2c", bid, 31))
        st.qks_last = (st.r[("ff2c", bid, 31)] if nb > 2
                       else st.r[("ff1mm", bid, 31)])

        st.b_s3b = st.r[("s3b", blocks[-1])]
        st.b_s1last = ("S", st.b_sbase + S_PER * (ngrp - 1) + gf)
        st.b_vdone = sch.tok("V")


def kernel(L0_attn_x, L0_attn_Wqkv, L0_attn_bqkv, L0_attn_Wout, L0_attn_bout,
           L0_ffn_w1, L0_ffn_b1, L0_ffn_w2, L0_ffn_b2):
    nc = build()
    in_maps = []
    for c in range(8):
        in_maps.append(dict(
            x=np.ascontiguousarray(L0_attn_x[c]),
            wqkv=np.ascontiguousarray(L0_attn_Wqkv),
            bqkv=np.ascontiguousarray(L0_attn_bqkv),
            wout=np.ascontiguousarray(L0_attn_Wout),
            bout=np.ascontiguousarray(L0_attn_bout),
            w1=np.ascontiguousarray(L0_ffn_w1),
            b1=np.ascontiguousarray(L0_ffn_b1),
            w2=np.ascontiguousarray(L0_ffn_w2),
            b2=np.ascontiguousarray(L0_ffn_b2),
        ))
    res = run_bass_kernel_spmd(nc, in_maps, core_ids=list(range(8)))
    return np.stack([r["out"] for r in res.results], axis=0)


# version 20
# speedup vs baseline: 1.0407x; 1.0160x over previous
import math
import numpy as np
import concourse.bass as bass
import concourse.mybir as mybir
from concourse.bass_utils import run_bass_kernel_spmd

F32 = mybir.dt.float32
BF16 = mybir.dt.bfloat16
F16 = mybir.dt.float16
AF = mybir.ActivationFunctionType
ALU = mybir.AluOpType
MAGIC = float(3 * 2 ** 22)

D = 1024
FF = 4096
NB = 32
NT = 8
SCALE = 1.0 / 32.0


def _pwl_params():
    X = np.linspace(-4.0, 4.0, 33).astype(np.float32)
    erf = np.vectorize(math.erf)
    Y = (0.5 * X * (1.0 + erf(X.astype(np.float64) / math.sqrt(2.0)))).astype(np.float32)
    Gh = np.array([256.0 * (float(Y[16 + k]) - 0.125 * k) for k in range(17)])
    s = [(Gh[k + 1] - Gh[k]) / 64.0 for k in range(16)]
    s16 = 512.5 - Gh[16]
    Ks = [64.0 * k for k in range(1, 16)] + [1024.0, 1025.0]
    cs = [s[k] - s[k - 1] for k in range(1, 16)] + [s16 - s[15], 0.5 - s16]
    return float(s[0]), [float(k) for k in Ks], [float(c) for c in cs]


A0, KINK_K, KINK_C = _pwl_params()
NK = len(KINK_K)
GK = [5, 6, 7, 8]
TK = [15, 16]
RK = list(range(NK))
assert all(KINK_C[k] < 0 for k in GK)


class Sch:
    def __init__(self, nc, ctx):
        self.nc = nc
        self.sem = {}
        self.count = {}
        self.last_wait = {w: {} for w in ("T", "V", "S", "G", "Y")}
        for name in ("T", "V", "S", "G", "DL", "DS"):
            self.sem[name] = ctx.enter_context(nc.semaphore("sem_" + name))
            self.count[name] = 0

    def inc(self, name, inst):
        step = 16 if name in ("DL", "DS") else 1
        inst.then_inc(self.sem[name], step)
        self.count[name] += step
        return (name, self.count[name])

    def tok(self, name):
        return (name, self.count[name])

    def wait(self, eng, wname, token):
        if not token:
            return
        name, val = token
        if val <= 0:
            return
        lw = self.last_wait[wname]
        if lw.get(name, 0) >= val:
            return
        eng.wait_ge(self.sem[name], val)
        lw[name] = val


class St:
    def __init__(self, sch, H, B):
        self.sch = sch
        self.H = H
        self.B = B
        self.r = {}
        self.last_store = None
        self.last_ln_sq = None
        self.wvs_last = None
        self.qks_last = None
        self.b_s3b = None
        self.b_s1last = None
        self.b_vdone = None
        self.a2_ln_last = None
        self.a2_exp_last = None
        self.sc_guard = None
        self.pv_guard = None


def _barrier(block, st, include_y=False):
    sch = st.sch
    toks = [sch.tok(n) for n in ("T", "V", "S", "G", "DL", "DS")]
    pairs = [("T", block.tensor), ("V", block.vector), ("S", block.scalar),
             ("G", block.gpsimd)]
    if include_y:
        pairs.append(("Y", block.sync))
    for name, section in pairs:
        @section
        def _(e, _t=toks, _n=name):
            for tk in _t:
                sch.wait(e, _n, tk)


def build(limit_tiles=NT, phases="ab", debug=False):
    nc = bass.Bass("TRN2", target_bir_lowering=False, detect_race_conditions=False)

    H = {}
    H["x"] = nc.dram_tensor("x", [4096, D], F32, kind="ExternalInput").ap()
    H["wqkv"] = nc.dram_tensor("wqkv", [D, 3 * D], F32, kind="ExternalInput").ap()
    H["bqkv"] = nc.dram_tensor("bqkv", [3 * D], F32, kind="ExternalInput").ap()
    H["wout"] = nc.dram_tensor("wout", [D, D], F32, kind="ExternalInput").ap()
    H["bout"] = nc.dram_tensor("bout", [D], F32, kind="ExternalInput").ap()
    H["w1"] = nc.dram_tensor("w1", [D, FF], F32, kind="ExternalInput").ap()
    H["b1"] = nc.dram_tensor("b1", [FF], F32, kind="ExternalInput").ap()
    H["w2"] = nc.dram_tensor("w2", [FF, D], F32, kind="ExternalInput").ap()
    H["b2"] = nc.dram_tensor("b2", [D], F32, kind="ExternalInput").ap()
    H["out"] = nc.dram_tensor("out", [4096, D], F32, kind="ExternalOutput").ap()
    H["ydbg"] = nc.dram_tensor("ydbg", [4096, D], F32, kind="ExternalOutput").ap() if debug else None

    H["qk_hi"] = nc.dram_tensor("qk_hi", [16, 8, 128, 128], BF16).ap()
    H["qk_lo"] = nc.dram_tensor("qk_lo", [16, 8, 128, 128], BF16).ap()
    H["wv_hi"] = nc.dram_tensor("wv_hi", [8, 128, D], BF16).ap()
    H["wv_lo"] = nc.dram_tensor("wv_lo", [8, 128, D], BF16).ap()
    H["wo_hi"] = nc.dram_tensor("wo_hi", [8, 128, D], BF16).ap()
    H["wo_lo"] = nc.dram_tensor("wo_lo", [8, 128, D], BF16).ap()
    H["w1_hi"] = nc.dram_tensor("w1_hi", [32, 8, 128, 128], BF16).ap()
    H["w1_lo"] = nc.dram_tensor("w1_lo", [32, 8, 128, 128], BF16).ap()
    H["w2_hi"] = nc.dram_tensor("w2_hi", [32, 128, D], F16).ap()
    H["w2_lo"] = nc.dram_tensor("w2_lo", [32, 128, D], F16).ap()
    H["b1r_hi"] = nc.dram_tensor("b1r_hi", [32, 1, 128], BF16).ap()
    H["b1r_lo"] = nc.dram_tensor("b1r_lo", [32, 1, 128], BF16).ap()

    from contextlib import ExitStack
    es = ExitStack()

    def sb(name, shape, dt=F32):
        return es.enter_context(nc.sbuf_tensor(name, shape, dt))

    B = {}
    B["x_t"] = None
    B["q_t"] = None
    B["bqk_sb"] = sb("bqk_sb", [128, 16])
    B["bvr_hi"] = sb("bvr_hi", [1, D], BF16)
    B["bvr_lo"] = sb("bvr_lo", [1, D], BF16)
    B["bor_hi"] = sb("bor_hi", [1, D], BF16)
    B["bor_lo"] = sb("bor_lo", [1, D], BF16)
    B["b2r_hi"] = sb("b2r_hi", [1, D], F16)
    B["b2r_lo"] = sb("b2r_lo", [1, D], F16)
    B["iota_p"] = sb("iota_p", [128, 1])
    B["iota_f"] = sb("iota_f", [128, 128])
    B["id_f32"] = sb("id_f32", [128, 128])
    B["id_bf"] = sb("id_bf", [128, 128], BF16)
    B["m0t"] = sb("m0t", [128, 128], BF16)
    B["m2t"] = sb("m2t", [128, 128], BF16)
    B["ones_bf"] = sb("ones_bf", [1, 512], BF16)
    B["ones_f16"] = sb("ones_f16", [1, 128], F16)
    B["kinkb"] = sb("kinkb", [128, NK])
    B["pmag"] = sb("pmag", [128, 1])
    B["nmag"] = sb("nmag", [128, 1])
    B["st_t"] = sb("st_t", [128, 16])

    with es, nc.Block() as block:
        sch = Sch(nc, es)
        st = St(sch, H, B)
        _emit_consts(nc, block, st)
        _emit_prep(nc, block, st)
        _barrier(block, st, include_y=True)

        B["x_t"] = sb("x_t", [128, 5 * D])
        B["y_t"] = sb("y_t", [128, 4 * D])
        B["q_t"] = sb("q_t", [128, 8 * 5 * 128])
        B["k_t"] = sb("k_t", [128, 8 * 6 * 128])
        B["vt_t"] = sb("vt_t", [128, 6 * 1040])
        B["xt_hi"] = sb("xt_hi", [128, 4 * 8 * 128], BF16)
        B["xt_lo"] = sb("xt_lo", [128, 4 * 8 * 128], BF16)
        B["at_hi"] = sb("at_hi", [128, 8 * 128], BF16)
        B["at_lo"] = sb("at_lo", [128, 8 * 128], BF16)
        B["attn_tm"] = sb("attn_tm", [128, D])
        B["e_t"] = sb("e_t", [128, 2 * 384])
        B["a_t"] = sb("a_t", [128, 32 * 512], F16)
        B["qks_hi"] = sb("qks_hi", [128, 2 * 1024], BF16)
        B["qks_lo"] = sb("qks_lo", [128, 2 * 1024], BF16)
        B["wvs_hi"] = sb("wvs_hi", [128, 2 * D], BF16)
        B["wvs_lo"] = sb("wvs_lo", [128, 2 * D], BF16)
        B["w1s_hi"] = B["qks_hi"]
        B["w1s_lo"] = B["qks_lo"]
        B["w2s_hi"] = B["wvs_hi"][:].bitcast(F16)
        B["w2s_lo"] = B["wvs_lo"][:].bitcast(F16)
        B["b1s_hi"] = sb("b1s_hi", [1, 2 * 128], BF16)
        B["b1s_lo"] = sb("b1s_lo", [1, 2 * 128], BF16)
        B["s1_t"] = sb("s1_t", [128, 2 * 512])
        B["rk_t"] = sb("rk_t", [128, 2 * 1024])
        B["s3_t"] = B["e_t"][:, 0:512]
        B["r2_t"] = B["a_t"][:, 0:2048].bitcast(F32)
        B["sq_t"] = B["a_t"][:, 2048:4096].bitcast(F32)
        B["oh16"] = B["attn_tm"][:].bitcast(F16)
        B["acc_t"] = B["attn_tm"][:, 0:512]
        B["t1_t"] = B["attn_tm"][:, 512:1024]
        B["u_t"] = None
        B["u0"] = B["at_hi"][:].bitcast(F32)
        B["u1"] = B["at_lo"][:].bitcast(F32)
        @block.vector
        def _(v):
            i = None
            for s in range(6):
                i = v.memset(B["vt_t"][:, s * 1040 + 64: (s + 1) * 1040: 65], 1.0)
            sch.inc("V", i)
        for t in range(limit_tiles):
            _emit_A1(nc, block, st, t, sub=("all" if phases not in ("x", "tr", "tr1", "tr1a", "qk") else phases))
            if phases in ("a1", "x", "tr", "tr1", "tr1a", "qk"):
                _barrier(block, st)
                continue
            _emit_A2(nc, block, st, t, ms=(None if t < NT - 1 else list(range(27, 31))))
            if phases == "a2":
                _barrier(block, st)
                continue
            if t == NT - 1:
                _emit_B(nc, block, st, t, list(range(27, 31)))
                _barrier(block, st)
                _emit_A2(nc, block, st, t, ms=[31], tag="x")
                _barrier(block, st)
                _emit_B(nc, block, st, t, [31])
            else:
                _emit_B(nc, block, st, t, list(range(max(0, 4 * t - 1), 4 * t + 3)))

        @block.gpsimd
        def _(g):
            g.wait_ge(sch.sem["DS"], sch.count["DS"])
    return nc


def _emit_consts(nc, block, st):
    sch, B = st.sch, st.B

    @block.gpsimd
    def _(g):
        g.iota(B["iota_p"][:], pattern=[[0, 1]], channel_multiplier=1,
               allow_small_or_imprecise_dtypes=True)
        i = g.iota(B["iota_f"][:], pattern=[[1, 128]], channel_multiplier=0,
                   allow_small_or_imprecise_dtypes=True)
        sch.inc("G", i)

    @block.vector
    def _(v):
        sch.wait(v, "V", sch.tok("G"))
        v.tensor_scalar(out=B["id_f32"][:], in0=B["iota_f"][:],
                        scalar1=B["iota_p"][:, 0:1], scalar2=None, op0=ALU.is_equal)
        v.tensor_copy(out=B["id_bf"][:], in_=B["id_f32"][:])
        v.tensor_scalar(out=B["m0t"][:], in0=B["iota_f"][:],
                        scalar1=B["iota_p"][:, 0:1], scalar2=-1.0e30,
                        op0=ALU.is_lt, op1=ALU.mult)
        v.tensor_scalar(out=B["m2t"][:], in0=B["iota_f"][:],
                        scalar1=B["iota_p"][:, 0:1], scalar2=-1.0e30,
                        op0=ALU.is_gt, op1=ALU.mult)
        v.memset(B["ones_bf"][:], 1.0)
        v.memset(B["ones_f16"][:], 1.0)
        for k in range(NK):
            v.memset(B["kinkb"][:, k:k + 1],
                     -KINK_K[k] if k in TK else -abs(KINK_C[k]) * KINK_K[k])
        v.memset(B["pmag"][:], MAGIC)
        i = v.memset(B["nmag"][:], -MAGIC)
        sch.inc("V", i)


def _emit_prep(nc, block, st):
    sch, H, B = st.sch, st.H, st.B
    with nc.sbuf_tensor("pf0", [128, 4096], F32) as pf0, \
         nc.sbuf_tensor("pf1", [128, 4096], F32) as pf1, \
         nc.sbuf_tensor("ph0", [128, 4096], BF16) as ph0, \
         nc.sbuf_tensor("ph1", [128, 4096], BF16) as ph1, \
         nc.sbuf_tensor("pl0", [128, 4096], BF16) as pl0, \
         nc.sbuf_tensor("pl1", [128, 4096], BF16) as pl1, \
         nc.sbuf_tensor("ph16", [128, 2 * 1024], F16) as ph16, \
         nc.sbuf_tensor("pl16", [128, 2 * 1024], F16) as pl16, \
         nc.sbuf_tensor("brow", [1, 4096], F32) as brow, \
         nc.sbuf_tensor("brh", [1, 4096], BF16) as brh, \
         nc.sbuf_tensor("brl", [1, 4096], BF16) as brl:

        pf = [pf0, pf1]
        phh = [ph0, ph1]
        pll = [pl0, pl1]
        jobs = []
        for dc in range(8):
            jobs.append((H["wqkv"][dc * 128:(dc + 1) * 128, 0:2048], 2048,
                         H["qk_hi"][:, dc], H["qk_lo"][:, dc], "cc"))
        for dc in range(8):
            jobs.append((H["wqkv"][dc * 128:(dc + 1) * 128, 2048:3072], 1024,
                         H["wv_hi"][dc], H["wv_lo"][dc], None))
        for dc in range(8):
            jobs.append((H["wout"][dc * 128:(dc + 1) * 128, :], 1024,
                         H["wo_hi"][dc], H["wo_lo"][dc], None))
        for dc in range(8):
            jobs.append((H["w1"][dc * 128:(dc + 1) * 128, :], 4096,
                         H["w1_hi"][:, dc], H["w1_lo"][:, dc], "fc"))
        for fc in range(32):
            jobs.append((H["w2"][fc * 128:(fc + 1) * 128, :], 1024,
                         H["w2_hi"][fc], H["w2_lo"][fc], "f16"))

        for j, (src, n, hid, lod, mode) in enumerate(jobs):
            p = j % 2

            @block.sync
            def _(y, j=j, p=p, src=src, n=n):
                sch.wait(y, "Y", st.r.get(("pst", j - 2), None))
                d = y.dma_start(out=pf[p][:, 0:n], in_=src)
                st.r[("pld", j)] = sch.inc("DL", d)

            @block.vector
            def _(v, j=j, p=p, n=n, mode=mode):
                sch.wait(v, "V", st.r[("pld", j)])
                sch.wait(v, "V", st.r.get(("pst", j - 2), None))
                if mode == "f16":
                    v.tensor_copy(out=ph16[:, p * 1024:p * 1024 + n], in_=pf[p][:, 0:n])
                    i = v.scalar_tensor_tensor(
                        out=pl16[:, p * 1024:p * 1024 + n],
                        in0=ph16[:, p * 1024:p * 1024 + n], scalar=-1.0,
                        in1=pf[p][:, 0:n], op0=ALU.mult, op1=ALU.add)
                else:
                    v.tensor_copy(out=phh[p][:, 0:n], in_=pf[p][:, 0:n])
                    i = v.scalar_tensor_tensor(
                        out=pll[p][:, 0:n], in0=phh[p][:, 0:n], scalar=-1.0,
                        in1=pf[p][:, 0:n], op0=ALU.mult, op1=ALU.add)
                st.r[("psp", j)] = sch.inc("V", i)

            @block.gpsimd
            def _(g, j=j, p=p, n=n, hid=hid, lod=lod, mode=mode):
                sch.wait(g, "G", st.r[("psp", j)])
                if mode == "f16":
                    shi = ph16[:, p * 1024:p * 1024 + n]
                    slo = pl16[:, p * 1024:p * 1024 + n]
                else:
                    shi = phh[p][:, 0:n]
                    slo = pll[p][:, 0:n]
                if mode in ("cc", "fc"):
                    shi = shi.rearrange("p (c m) -> p c m", m=128)
                    slo = slo.rearrange("p (c m) -> p c m", m=128)
                    hid = hid.rearrange("c p m -> p c m")
                    lod = lod.rearrange("c p m -> p c m")
                d = g.dma_start(out=hid, in_=shi)
                sch.inc("DS", d)
                d = g.dma_start(out=lod, in_=slo)
                st.r[("pst", j)] = sch.inc("DS", d)

        @block.sync
        def _(y):
            sch.wait(y, "Y", st.r[("pst", len(jobs) - 1)])
            for cc in range(16):
                d = y.dma_start(out=B["bqk_sb"][:, cc:cc + 1],
                                in_=H["bqkv"][cc * 128:(cc + 1) * 128])
                sch.inc("DL", d)
            d = y.dma_start(out=brow[0:1, 0:1024], in_=H["bqkv"][2048:3072])
            sch.inc("DL", d)
            d = y.dma_start(out=brow[0:1, 1024:2048], in_=H["bout"])
            sch.inc("DL", d)
            d = y.dma_start(out=brow[0:1, 2048:3072], in_=H["b2"])
            sch.inc("DL", d)

        @block.vector
        def _(v):
            sch.wait(v, "V", sch.tok("DL"))
            v.tensor_copy(out=B["bvr_hi"][:], in_=brow[0:1, 0:1024])
            v.scalar_tensor_tensor(out=B["bvr_lo"][:], in0=B["bvr_hi"][:], scalar=-1.0,
                                   in1=brow[0:1, 0:1024], op0=ALU.mult, op1=ALU.add)
            v.tensor_copy(out=B["bor_hi"][:], in_=brow[0:1, 1024:2048])
            v.scalar_tensor_tensor(out=B["bor_lo"][:], in0=B["bor_hi"][:], scalar=-1.0,
                                   in1=brow[0:1, 1024:2048], op0=ALU.mult, op1=ALU.add)
            v.tensor_scalar(out=brow[0:1, 3072:4096], in0=brow[0:1, 2048:3072],
                            scalar1=256.0, scalar2=None, op0=ALU.mult)
            v.tensor_copy(out=B["b2r_hi"][:], in_=brow[0:1, 3072:4096])
            i = v.scalar_tensor_tensor(out=B["b2r_lo"][:], in0=B["b2r_hi"][:], scalar=-1.0,
                                       in1=brow[0:1, 3072:4096], op0=ALU.mult, op1=ALU.add)
            st.r[("bprep",)] = sch.inc("V", i)

        @block.sync
        def _(y):
            sch.wait(y, "Y", st.r[("bprep",)])
            d = y.dma_start(out=brow[0:1, 0:4096], in_=H["b1"])
            sch.inc("DL", d)

        @block.vector
        def _(v):
            sch.wait(v, "V", sch.tok("DL"))
            v.tensor_copy(out=brh[:], in_=brow[0:1, 0:4096])
            i = v.scalar_tensor_tensor(out=brl[:], in0=brh[:], scalar=-1.0,
                                       in1=brow[0:1, 0:4096], op0=ALU.mult, op1=ALU.add)
            st.r[("b1sp",)] = sch.inc("V", i)

        @block.gpsimd
        def _(g):
            sch.wait(g, "G", st.r[("b1sp",)])
            d = g.dma_start(out=H["b1r_hi"].rearrange("fc o m -> o fc m"),
                            in_=brh[:].rearrange("o (fc m) -> o fc m", m=128))
            sch.inc("DS", d)
            d = g.dma_start(out=H["b1r_lo"].rearrange("fc o m -> o fc m"),
                            in_=brl[:].rearrange("o (fc m) -> o fc m", m=128))
            sch.inc("DS", d)


def _emit_A1(nc, block, st, t, sub="all"):
    sch, H, B = st.sch, st.H, st.B
    blocks = list(range(4 * t, 4 * t + 4))

    @block.sync
    def _(y):
        sch.wait(y, "Y", sch.tok("DS"))
        for n in blocks:
            sch.wait(y, "Y", st.r.get((("ln1", n - 5), "r"), None))
            d = y.dma_start(out=B["x_t"][:, (n % 5) * D:(n % 5 + 1) * D],
                            in_=H["x"][n * 128:(n + 1) * 128, :])
            st.r[("x", n)] = sch.inc("DL", d)

    if sub == "x":
        return
    with nc.psum_tensor(f"ptr_{t}", [128, 1024], F32) as ptr:
        for bi, n in enumerate(blocks):
            if sub == "tr1" and bi > 0:
                break
            for dc in range(8):
                if sub in ("tr1", "tr1a") and dc >= 2:
                    break
                @block.tensor
                def _(te, n=n, dc=dc, bi=bi):
                    if dc == 0:
                        if bi == 0:
                            sch.wait(te, "T", st.b_s1last)
                        sch.wait(te, "T", st.r[("x", n)])
                    pb = dc % 2
                    prev = ("trev", n, dc - 2) if dc >= 2 else ("trev", n - 1, 6 + dc)
                    sch.wait(te, "T", st.r.get(prev, None))
                    i = te.transpose(ptr[:, pb * 512: pb * 512 + 128],
                                     B["x_t"][:, (n % 5) * D + dc * 128:(n % 5) * D + (dc + 1) * 128],
                                     B["id_f32"][:])
                    st.r[("tr", n, dc)] = sch.inc("T", i)

                if sub != "tr1a":
                    @block.vector
                    def _(v, n=n, dc=dc, bi=bi):
                        pb = dc % 2
                        col = ((n % 4) * 8 + dc) * 128
                        if bi == 0 and dc == 0:
                            sch.wait(v, "V", st.qks_last)
                        sch.wait(v, "V", st.r[("tr", n, dc)])
                        v.tensor_copy(out=B["xt_hi"][:, col:col + 128],
                                      in_=ptr[:, pb * 512: pb * 512 + 128])
                        i = v.scalar_tensor_tensor(
                            out=B["xt_lo"][:, col:col + 128],
                            in0=B["xt_hi"][:, col:col + 128], scalar=-1.0,
                            in1=ptr[:, pb * 512: pb * 512 + 128], op0=ALU.mult, op1=ALU.add)
                        st.r[("trev", n, dc)] = sch.inc("V", i)

    xt_ready = sch.tok("V")
    if sub in ("tr", "tr1", "tr1a"):
        return

    with nc.psum_tensor(f"pqk_{t}", [128, 2 * 512], F32) as pqk:
        q_base = sch.count["S"]
        for cc in range(16):
            st.r[("qkev", t, cc)] = ("S", q_base + cc + 1)
        for cc in range(16):
            sl = cc % 2

            @block.sync
            def _(y, cc=cc, sl=sl):
                if cc < 2:
                    sch.wait(y, "Y", st.qks_last)
                sch.wait(y, "Y", st.r.get(("qkmm", t, cc - 2), None))
                d = y.dma_start(out=B["qks_hi"][:, sl * 1024:(sl + 1) * 1024].rearrange(
                    "p (dc m) -> p dc m", m=128), in_=H["qk_hi"][cc].rearrange("dc p m -> p dc m"))
                sch.inc("DL", d)
                d = y.dma_start(out=B["qks_lo"][:, sl * 1024:(sl + 1) * 1024].rearrange(
                    "p (dc m) -> p dc m", m=128), in_=H["qk_lo"][cc].rearrange("dc p m -> p dc m"))
                st.r[("qkw", t, cc)] = sch.inc("DL", d)

            @block.tensor
            def _(te, cc=cc, sl=sl):
                sch.wait(te, "T", st.r[("qkw", t, cc)])
                sch.wait(te, "T", xt_ready)
                sch.wait(te, "T", st.r.get(("qkev", t, cc - 2), None))
                rh4 = B["xt_hi"][:].rearrange("p (s dc m) -> p dc s m", dc=8, m=128)
                rl4 = B["xt_lo"][:].rearrange("p (s dc m) -> p dc s m", dc=8, m=128)
                i = None
                for dc in range(8):
                    whi = B["qks_hi"][:, sl * 1024 + dc * 128: sl * 1024 + (dc + 1) * 128]
                    wlo = B["qks_lo"][:, sl * 1024 + dc * 128: sl * 1024 + (dc + 1) * 128]
                    te.matmul(pqk[:, sl * 512:(sl + 1) * 512], whi, rh4[:, dc],
                              start=(dc == 0), stop=False)
                    te.matmul(pqk[:, sl * 512:(sl + 1) * 512], whi, rl4[:, dc],
                              start=False, stop=False)
                    i = te.matmul(pqk[:, sl * 512:(sl + 1) * 512], wlo, rh4[:, dc],
                                  start=False, stop=(dc == 7))
                st.r[("qkmm", t, cc)] = sch.inc("T", i)

        @block.scalar
        def _(s):
            sch.wait(s, "S", st.sc_guard)
            for cc in range(16):
                sl = cc % 2
                sch.wait(s, "S", st.r[("qkmm", t, cc)])
                dst = B["q_t"] if cc < 8 else B["k_t"]
                ring = 5 if cc < 8 else 6
                ccp = cc % 8
                for bi, n in enumerate(blocks):
                    rs = n % ring
                    i = s.activation(
                        out=dst[:, (ccp * ring + rs) * 128:(ccp * ring + rs + 1) * 128],
                        in_=pqk[:, sl * 512 + bi * 128: sl * 512 + (bi + 1) * 128],
                        func=AF.Identity, bias=B["bqk_sb"][:, cc:cc + 1], scale=1.0)
                sch.inc("S", i)

    qk_done = sch.tok("S")
    if sub == "qk":
        return

    with nc.psum_tensor(f"pv_{t}", [128, 4 * 1024], F32) as pv:
        for dc in range(8):
            sl = dc % 2

            @block.sync
            def _(y, dc=dc, sl=sl):
                if dc < 2:
                    sch.wait(y, "Y", st.wvs_last)
                sch.wait(y, "Y", st.r.get(("vmm", t, dc - 2), None))
                d = y.dma_start(out=B["wvs_hi"][:, sl * D:(sl + 1) * D], in_=H["wv_hi"][dc])
                sch.inc("DL", d)
                d = y.dma_start(out=B["wvs_lo"][:, sl * D:(sl + 1) * D], in_=H["wv_lo"][dc])
                st.r[("wv", t, dc)] = sch.inc("DL", d)

            @block.tensor
            def _(te, dc=dc, sl=sl):
                if dc == 0:
                    sch.wait(te, "T", qk_done)
                    sch.wait(te, "T", st.b_s3b)
                sch.wait(te, "T", st.r[("wv", t, dc)])
                i = None
                for bi, n in enumerate(blocks):
                    col = ((n % 4) * 8 + dc) * 128
                    xh = B["xt_hi"][:, col:col + 128]
                    xl = B["xt_lo"][:, col:col + 128]
                    for hf in range(2):
                        pvs = pv[:, bi * 1024 + hf * 512: bi * 1024 + (hf + 1) * 512]
                        te.matmul(pvs, xh,
                                  B["wvs_hi"][:, sl * D + hf * 512: sl * D + (hf + 1) * 512],
                                  start=(dc == 0), stop=False)
                        te.matmul(pvs, xh,
                                  B["wvs_lo"][:, sl * D + hf * 512: sl * D + (hf + 1) * 512],
                                  start=False, stop=False)
                        i = te.matmul(pvs, xl,
                                      B["wvs_hi"][:, sl * D + hf * 512: sl * D + (hf + 1) * 512],
                                      start=False, stop=False)
                st.r[("vmm", t, dc)] = sch.inc("T", i)

        @block.tensor
        def _(te):
            i = None
            for bi in range(4):
                for hf in range(2):
                    pvs = pv[:, bi * 1024 + hf * 512: bi * 1024 + (hf + 1) * 512]
                    te.matmul(pvs, B["ones_bf"][0:1, 0:128],
                              B["bvr_hi"][0:1, hf * 512:(hf + 1) * 512],
                              start=False, stop=False)
                    i = te.matmul(pvs, B["ones_bf"][0:1, 0:128],
                                  B["bvr_lo"][0:1, hf * 512:(hf + 1) * 512],
                                  start=False, stop=True)
            st.r[("vmm_all", t)] = sch.inc("T", i)
        st.qks_last = st.r[("qkmm", t, 15)]
        st.wvs_last = st.r[("vmm", t, 7)]

        @block.vector
        def _(v):
            sch.wait(v, "V", st.pv_guard)
            sch.wait(v, "V", st.r[("vmm_all", t)])
            for bi, n in enumerate(blocks):
                rs = n % 6
                dst = B["vt_t"][:, rs * 1040:(rs + 1) * 1040].rearrange(
                    "p (h c) -> p h c", c=65)[:, :, 0:64]
                src = pv[:, bi * 1024:(bi + 1) * 1024].rearrange(
                    "p (h c) -> p h c", c=64)
                i = v.tensor_copy(out=dst, in_=src)
                st.r[("v", n)] = sch.inc("V", i)


def _emit_A2(nc, block, st, t, ms=None, tag=""):
    sch, H, B = st.sch, st.H, st.B
    if ms is None:
        ms = list(range(max(0, 4 * t - 1), 4 * t + 3))

    with nc.psum_tensor(f"psc_{t}{tag}", [128, 2 * 512], F32) as psc, \
         nc.psum_tensor(f"ppv_{t}{tag}", [128, 1024], F32) as ppv, \
         nc.psum_tensor(f"ptr2_{t}{tag}", [128, 1024], F32) as ptr2, \
         nc.psum_tensor(f"pwo_{t}{tag}", [128, 1024], F32) as pwo:

        def emit_sc(m, h, chunks):
            cc = h // 2
            half = (h % 2) * 64
            pb = h % 2
            qs = m % 5

            @block.tensor
            def _(te):
                if m == ms[0] and h == 0:
                    sch.wait(te, "T", st.r.get(("qkev", t, 15), None))
                    sch.wait(te, "T", st.r.get(("v", 4 * t), None))
                    sch.wait(te, "T", st.b_s3b)
                sch.wait(te, "T", st.r.get(("exp", m, h - 2), None))
                if h <= 1:
                    sch.wait(te, "T", st.r.get(("exp", m - 1, 14 + h), None))
                i = None
                for (c, ks) in chunks:
                    lhs = B["k_t"][half:half + 64,
                                   (cc * 6 + ks) * 128:(cc * 6 + ks + 1) * 128]
                    rhs = B["q_t"][half:half + 64,
                                   (cc * 5 + qs) * 128:(cc * 5 + qs + 1) * 128]
                    i = te.matmul(psc[:, pb * 512 + c * 128: pb * 512 + (c + 1) * 128],
                                  lhs, rhs, start=True, stop=(c == 1))
                    if c == 0:
                        i = te.matmul(psc[:, pb * 512: pb * 512 + 128],
                                      B["m0t"][:], B["id_bf"][:], start=False, stop=True)
                    elif c == 2:
                        i = te.matmul(psc[:, pb * 512 + 256: pb * 512 + 384],
                                      B["m2t"][:], B["id_bf"][:], start=False, stop=True)
                st.r[("sc", m, h)] = sch.inc("T", i)

        def emit_exp_section(m, chunks):
            @block.scalar
            def _(s):
                if m == ms[0]:
                    sch.wait(s, "S", st.b_vdone)
                for h in range(16):
                    pb = h % 2
                    sch.wait(s, "S", st.r[("sc", m, h)])
                    sch.wait(s, "S", st.r.get(("pv", m, h - 2), None))
                    if h <= 1:
                        sch.wait(s, "S", st.r.get(("pv", m - 1, 14 + h), None))
                    c0, c1 = chunks[0][0], chunks[-1][0]
                    i = s.activation(out=B["e_t"][:, pb * 384 + c0 * 128: pb * 384 + (c1 + 1) * 128],
                                     in_=psc[:, pb * 512 + c0 * 128: pb * 512 + (c1 + 1) * 128],
                                     func=AF.Exp, bias=0.0, scale=SCALE)
                    sch.inc("S", i)

        def emit_pv(m, h, chunks):
            pb = h % 2

            @block.tensor
            def _(te):
                if m == ms[0] and h == 0:
                    sch.wait(te, "T", st.r.get(("v", 4 * t + 1), None))
                sch.wait(te, "T", st.r[("exp", m, h)])
                if h % 8 == 0:
                    sch.wait(te, "T", st.r.get(("anorm", m, h - 8), None))
                    if h == 0:
                        sch.wait(te, "T", st.r.get(("anorm", m - 1, 8), None))
                po = (h % 8) * 128
                first = True
                i = None
                for (c, ks) in chunks:
                    lhs = B["e_t"][:, pb * 384 + c * 128: pb * 384 + (c + 1) * 128]
                    rhs = B["vt_t"][:, ks * 1040 + h * 65: ks * 1040 + (h + 1) * 65]
                    i = te.matmul(ppv[:, po:po + 65], lhs, rhs,
                                  start=first, stop=(c == chunks[-1][0]))
                    first = False
                st.r[("pv", m, h)] = sch.inc("T", i)

        for m in ms:
            chunks = []
            if m > 0:
                chunks.append((0, (m - 1) % 6))
            chunks.append((1, m % 6))
            if m < 31:
                chunks.append((2, (m + 1) % 6))

            e_base = sch.count["S"]
            for h in range(16):
                st.r[("exp", m, h)] = ("S", e_base + h + 1)
            for h in range(16):
                emit_sc(m, h, chunks)
                if h >= 1:
                    emit_pv(m, h - 1, chunks)
                    if h == 8:
                        _emit_wave_norm(nc, block, st, m, 0, ppv)
            emit_pv(m, 15, chunks)
            emit_exp_section(m, chunks)
            _emit_wave_norm(nc, block, st, m, 1, ppv)

            for dc in range(8):
                @block.tensor
                def _(te, m=m, dc=dc):
                    if dc == 0:
                        if m == ms[0]:
                            sch.wait(te, "T", st.r.get(("v", 4 * t + 2), None))
                        sch.wait(te, "T", st.r[("anorm", m, 0)])
                    if dc == 4:
                        sch.wait(te, "T", st.r[("anorm", m, 8)])
                    pb2 = dc % 2
                    prev = ("atr_ev", m, dc - 2) if dc >= 2 else ("atr_ev", m - 1, 6 + dc)
                    sch.wait(te, "T", st.r.get(prev, None))
                    i = te.transpose(ptr2[:, pb2 * 512: pb2 * 512 + 128],
                                     B["attn_tm"][:, dc * 128:(dc + 1) * 128],
                                     B["id_f32"][:])
                    st.r[("atr", m, dc)] = sch.inc("T", i)

                @block.vector
                def _(v, m=m, dc=dc):
                    pb2 = dc % 2
                    sch.wait(v, "V", st.r[("atr", m, dc)])
                    sch.wait(v, "V", st.r.get(("womm", m - 1, dc), None))
                    v.tensor_copy(out=B["at_hi"][:, dc * 128:(dc + 1) * 128],
                                  in_=ptr2[:, pb2 * 512: pb2 * 512 + 128])
                    i = v.scalar_tensor_tensor(
                        out=B["at_lo"][:, dc * 128:(dc + 1) * 128],
                        in0=B["at_hi"][:, dc * 128:(dc + 1) * 128], scalar=-1.0,
                        in1=ptr2[:, pb2 * 512: pb2 * 512 + 128], op0=ALU.mult, op1=ALU.add)
                    st.r[("atr_ev", m, dc)] = sch.inc("V", i)

            for dc in range(8):
                sl = dc % 2

                @block.sync
                def _(y, m=m, dc=dc, sl=sl):
                    if dc < 2:
                        sch.wait(y, "Y", st.wvs_last)
                    sch.wait(y, "Y", st.r.get(("womm", m, dc - 2) if dc >= 2
                                              else ("womm", m - 1, 6 + dc), None))
                    d = y.dma_start(out=B["wvs_hi"][:, sl * D:(sl + 1) * D],
                                    in_=H["wo_hi"][dc])
                    sch.inc("DL", d)
                    d = y.dma_start(out=B["wvs_lo"][:, sl * D:(sl + 1) * D],
                                    in_=H["wo_lo"][dc])
                    st.r[("wow", m, dc)] = sch.inc("DL", d)

                @block.tensor
                def _(te, m=m, dc=dc, sl=sl):
                    if dc == 0:
                        if m == ms[0]:
                            sch.wait(te, "T", st.r.get(("v", 4 * t + 3), None))
                        sch.wait(te, "T", st.r.get((("ln1", m - 1), "r"), None))
                    sch.wait(te, "T", st.r[("atr_ev", m, dc)])
                    sch.wait(te, "T", st.r[("wow", m, dc)])
                    ah = B["at_hi"][:, dc * 128:(dc + 1) * 128]
                    al = B["at_lo"][:, dc * 128:(dc + 1) * 128]
                    i = None
                    for hf in range(2):
                        pws = pwo[:, hf * 512:(hf + 1) * 512]
                        te.matmul(pws, ah,
                                  B["wvs_hi"][:, sl * D + hf * 512: sl * D + (hf + 1) * 512],
                                  start=(dc == 0), stop=False)
                        te.matmul(pws, ah,
                                  B["wvs_lo"][:, sl * D + hf * 512: sl * D + (hf + 1) * 512],
                                  start=False, stop=False)
                        i = te.matmul(pws, al,
                                      B["wvs_hi"][:, sl * D + hf * 512: sl * D + (hf + 1) * 512],
                                      start=False, stop=False)
                    st.r[("womm", m, dc)] = sch.inc("T", i)

            @block.tensor
            def _(te, m=m):
                i = None
                for hf in range(2):
                    pws = pwo[:, hf * 512:(hf + 1) * 512]
                    te.matmul(pws, B["ones_bf"][0:1, 0:128],
                              B["bor_hi"][0:1, hf * 512:(hf + 1) * 512],
                              start=False, stop=False)
                    i = te.matmul(pws, B["ones_bf"][0:1, 0:128],
                                  B["bor_lo"][0:1, hf * 512:(hf + 1) * 512],
                                  start=False, stop=True)
                st.r[("wo_mm", m)] = sch.inc("T", i)
            st.wvs_last = st.r[("womm", m, 7)]

            _emit_ln(nc, block, st, src=pwo[:],
                     res_ap=B["x_t"][:, (m % 5) * D:(m % 5 + 1) * D],
                     out_ap=B["y_t"][:, (m % 4) * D:(m % 4 + 1) * D],
                     key=("ln1", m), dep=st.r[("wo_mm", m)])
            st.r[("y", m)] = st.r[("ln1", m)]
        st.sc_guard = st.r[("sc", ms[-1], 15)]
        st.pv_guard = st.r[("pv", ms[-1], 15)]
        st.a2_ln_last = st.r[("ln1", ms[-1])]
        st.a2_exp_last = st.r[("exp", ms[-1], 15)]


def _emit_wave_norm(nc, block, st, m, wave, ppv):
    sch, B = st.sch, st.B

    @block.vector
    def _(v):
        for h in range(wave * 8, wave * 8 + 8):
            sch.wait(v, "V", st.r[("pv", m, h)])
        if wave == 0:
            sch.wait(v, "V", st.r.get(("atr", m - 1, 7), None))
        i = v.reciprocal(out=B["st_t"][:, 8:16], in_=ppv[:, 64::128])
        sch.wait(v, "V", sch.inc("V", i))
        i = None
        for hh in range(8):
            h = wave * 8 + hh
            i = v.tensor_scalar(out=B["attn_tm"][:, h * 64:(h + 1) * 64],
                                in0=ppv[:, hh * 128: hh * 128 + 64],
                                scalar1=B["st_t"][:, 8 + hh:9 + hh], scalar2=None,
                                op0=ALU.mult)
        st.r[("anorm", m, wave * 8)] = sch.inc("V", i)


def _emit_ln(nc, block, st, src, res_ap, out_ap, key, dep):
    sch, B = st.sch, st.B
    r_buf = B["r2_t"]

    @block.vector
    def _(v):
        sch.wait(v, "V", dep)
        sch.wait(v, "V", st.last_ln_sq)
        i = v.scalar_tensor_tensor(out=r_buf, in0=src, scalar=1.0, in1=res_ap,
                                   op0=ALU.mult, op1=ALU.add,
                                   accum_out=B["st_t"][:, 0:1])
        st.r[(key, "r")] = sch.inc("V", i)

    _emit_ln_tail(nc, block, st, key, r_buf, out_ap)


def _emit_ln_tail(nc, block, st, key, r_buf, out_ap, pre_tok=None, sum2=False):
    sch, B = st.sch, st.B

    @block.scalar
    def _(s):
        sch.wait(s, "S", st.r[(key, "r")])
        i = s.activation(out=B["sq_t"], in_=r_buf, func=AF.Square,
                         bias=0.0, scale=1.0, accum_out=B["st_t"][:, 1:2])
        st.r[(key, "sq")] = sch.inc("S", i)
    st.last_ln_sq = st.r[(key, "sq")]

    @block.vector
    def _(v):
        sch.wait(v, "V", st.r[(key, "sq")])
        stt = B["st_t"]
        if sum2:
            i = v.tensor_tensor(out=stt[:, 0:1], in0=stt[:, 0:1],
                                in1=stt[:, 9:10], op=ALU.add)
            sch.wait(v, "V", sch.inc("V", i))
        i = v.tensor_scalar(out=stt[:, 2:3], in0=stt[:, 0:1], scalar1=1.0 / D,
                            scalar2=None, op0=ALU.mult)
        sch.wait(v, "V", sch.inc("V", i))
        i = v.tensor_tensor(out=stt[:, 3:4], in0=stt[:, 2:3], in1=stt[:, 2:3],
                            op=ALU.mult)
        sch.wait(v, "V", sch.inc("V", i))
        i = v.scalar_tensor_tensor(out=stt[:, 4:5], in0=stt[:, 1:2], scalar=1.0 / D,
                                   in1=stt[:, 3:4], op0=ALU.mult, op1=ALU.subtract)
        sch.wait(v, "V", sch.inc("V", i))
        i = v.tensor_scalar(out=stt[:, 4:5], in0=stt[:, 4:5], scalar1=1.0e-5,
                            scalar2=None, op0=ALU.add)
        st.r[(key, "v")] = sch.inc("V", i)

    @block.scalar
    def _(s):
        sch.wait(s, "S", st.r[(key, "v")])
        i = s.activation(out=B["st_t"][:, 5:6], in_=B["st_t"][:, 4:5],
                         func=AF.Sqrt, bias=0.0, scale=1.0)
        st.r[(key, "s0")] = sch.inc("S", i)

    @block.vector
    def _(v):
        sch.wait(v, "V", st.r[(key, "s0")])
        sch.wait(v, "V", pre_tok)
        stt = B["st_t"]
        i = v.reciprocal(out=stt[:, 6:7], in_=stt[:, 5:6])
        sch.wait(v, "V", sch.inc("V", i))
        if isinstance(out_ap, tuple):
            v.tensor_scalar(out=out_ap[0], in0=r_buf[:, 0:512], scalar1=stt[:, 2:3],
                            scalar2=stt[:, 6:7], op0=ALU.subtract, op1=ALU.mult)
            i = v.tensor_scalar(out=out_ap[1], in0=r_buf[:, 512:1024], scalar1=stt[:, 2:3],
                                scalar2=stt[:, 6:7], op0=ALU.subtract, op1=ALU.mult)
        else:
            i = v.tensor_scalar(out=out_ap, in0=r_buf, scalar1=stt[:, 2:3],
                                scalar2=stt[:, 6:7], op0=ALU.subtract, op1=ALU.mult)
        st.r[key] = sch.inc("V", i)


def _emit_B(nc, block, st, t, blocks):
    sch, H, B = st.sch, st.H, st.B
    nb = len(blocks)
    ntok = nb * 128
    bid = (t, blocks[0])

    if H.get("ydbg") is not None:
        @block.gpsimd
        def _(g):
            for m in blocks:
                sch.wait(g, "G", st.r[("y", m)])
                d = g.dma_start(out=H["ydbg"][m * 128:(m + 1) * 128, :],
                                in_=B["y_t"][:, (m % 4) * D:(m % 4 + 1) * D])
                sch.inc("DS", d)

    with nc.psum_tensor(f"ptr3_{t}_{blocks[0]}", [128, 1024], F32) as ptr3:
        for bi, m in enumerate(blocks):
            for dc in range(8):
                @block.tensor
                def _(te, m=m, bi=bi, dc=dc):
                    if dc == 0:
                        if bi == 0:
                            sch.wait(te, "T", st.a2_exp_last)
                        sch.wait(te, "T", st.r[("y", m)])
                    pb = dc % 2
                    prev = ("ytev", bid, bi, dc - 2) if dc >= 2 else ("ytev", bid, bi - 1, 6 + dc)
                    sch.wait(te, "T", st.r.get(prev, None))
                    i = te.transpose(ptr3[:, pb * 512: pb * 512 + 128],
                                     B["y_t"][:, (m % 4) * D + dc * 128:(m % 4) * D + (dc + 1) * 128],
                                     B["id_f32"][:])
                    st.r[("ytr", bid, bi, dc)] = sch.inc("T", i)

                @block.vector
                def _(v, m=m, bi=bi, dc=dc):
                    pb = dc % 2
                    col = (bi * 8 + dc) * 128
                    sch.wait(v, "V", st.r[("ytr", bid, bi, dc)])
                    i = v.tensor_copy(out=B["xt_hi"][:, col:col + 128],
                                      in_=ptr3[:, pb * 512: pb * 512 + 128])
                    i = v.scalar_tensor_tensor(
                        out=B["xt_lo"][:, col:col + 128],
                        in0=B["xt_hi"][:, col:col + 128], scalar=-1.0,
                        in1=ptr3[:, pb * 512: pb * 512 + 128], op0=ALU.mult, op1=ALU.add)
                    st.r[("ytev", bid, bi, dc)] = sch.inc("V", i)

    yt_ready = sch.tok("V")

    def emit_w1_stream(fc):
        sl = fc % 2

        @block.sync
        def _(y):
            if fc < 2:
                sch.wait(y, "Y", st.qks_last)
            sch.wait(y, "Y", st.r.get(("ff1mm", bid, fc - 2), None))
            d = y.dma_start(out=B["w1s_hi"][:, sl * 1024:(sl + 1) * 1024].rearrange(
                "p (dc m) -> p dc m", m=128), in_=H["w1_hi"][fc].rearrange("dc p m -> p dc m"))
            sch.inc("DL", d)
            d = y.dma_start(out=B["w1s_lo"][:, sl * 1024:(sl + 1) * 1024].rearrange(
                "p (dc m) -> p dc m", m=128), in_=H["w1_lo"][fc].rearrange("dc p m -> p dc m"))
            sch.inc("DL", d)
            d = y.dma_start(out=B["b1s_hi"][0:1, sl * 128:(sl + 1) * 128],
                            in_=H["b1r_hi"][fc])
            sch.inc("DL", d)
            d = y.dma_start(out=B["b1s_lo"][0:1, sl * 128:(sl + 1) * 128],
                            in_=H["b1r_lo"][fc])
            st.r[("w1w", bid, fc)] = sch.inc("DL", d)

    def emit_w2_stream(fc):
        sl = fc % 2

        @block.sync
        def _(y):
            if fc < 2:
                sch.wait(y, "Y", st.wvs_last)
            sch.wait(y, "Y", st.r.get(("ff2b", bid, fc - 2), None))
            d = y.dma_start(out=B["w2s_hi"][:, sl * D: (sl + 1) * D],
                            in_=H["w2_hi"][fc])
            sch.inc("DL", d)
            d = y.dma_start(out=B["w2s_lo"][:, sl * D: (sl + 1) * D],
                            in_=H["w2_lo"][fc])
            st.r[("w2w", bid, fc)] = sch.inc("DL", d)

    with nc.psum_tensor(f"ph_{t}_{blocks[0]}", [128, 2 * 512], F32) as ph, \
         nc.psum_tensor(f"po_{t}_{blocks[0]}", [128, 4 * 512], F32) as po, \
         nc.psum_tensor(f"po1_{t}_{blocks[0]}", [128, 2 * 512], F32) as po1:

        def emit_ff1_T(fc):
            sl = fc % 2

            @block.tensor
            def _(te):
                sch.wait(te, "T", st.r[("w1w", bid, fc)])
                sch.wait(te, "T", yt_ready)
                if fc >= 2:
                    sch.wait(te, "T", ("S", st.b_sbase
                                       + S_PER * ((fc - 2) // gf)
                                       + ((fc - 2) % gf) + 1))
                rh4 = B["xt_hi"][:, 0:nb * 8 * 128].rearrange(
                    "p (s dc m) -> p dc s m", dc=8, m=128)
                rl4 = B["xt_lo"][:, 0:nb * 8 * 128].rearrange(
                    "p (s dc m) -> p dc s m", dc=8, m=128)
                pp = ph[:, (fc % 2) * 512:(fc % 2) * 512 + ntok]
                for dc in range(8):
                    whi = B["w1s_hi"][:, sl * 1024 + dc * 128: sl * 1024 + (dc + 1) * 128]
                    wlo = B["w1s_lo"][:, sl * 1024 + dc * 128: sl * 1024 + (dc + 1) * 128]
                    te.matmul(pp, whi, rh4[:, dc], start=(dc == 0), stop=False)
                    te.matmul(pp, whi, rl4[:, dc], start=False, stop=False)
                    te.matmul(pp, wlo, rh4[:, dc], start=False, stop=False)
                te.matmul(pp, B["b1s_hi"][0:1, sl * 128:(sl + 1) * 128],
                          B["ones_bf"][0:1, 0:ntok], start=False, stop=False)
                i = te.matmul(pp, B["b1s_lo"][0:1, sl * 128:(sl + 1) * 128],
                              B["ones_bf"][0:1, 0:ntok], start=False, stop=True)
                st.r[("ff1mm", bid, fc)] = sch.inc("T", i)

        def rslot(g, k):
            return (RK.index(k) % 4) * 512

        def s_cnt(g, off):
            return ("S", st.b_sbase + S_PER * g + off)

        def v_cnt(g, off):
            return ("V", st.b_vbase + V_PER * g + off)

        def emit_epi_S():
            @block.scalar
            def _(s):
                sch.wait(s, "S", st.last_store)
                for fc in range(32):
                    p5 = (fc % 2) * 512
                    sch.wait(s, "S", v_cnt(fc - 2, 2))
                    sch.wait(s, "S", st.r[("ff1mm", bid, fc)])
                    pp = ph[:, (fc % 2) * 512:(fc % 2) * 512 + ntok]
                    i = s.activation(out=B["s1_t"][:, p5:p5 + ntok], in_=pp,
                                     func=AF.Identity, bias=B["pmag"][:, 0:1],
                                     scale=256.0)
                    sch.inc("S", i)
                    uap = B["u0"] if fc % 2 == 0 else B["u1"]
                    i = s.activation(out=uap[:, 0:ntok],
                                     in_=B["s1_t"][:, p5:p5 + ntok],
                                     func=AF.Abs, bias=B["nmag"][:, 0:1], scale=1.0)
                    sch.inc("S", i)
                    sch.wait(s, "S", v_cnt(fc - 1, 19))
                    for k in RK:
                        i = s.activation(
                            out=B["rk_t"][:, rslot(fc, k): rslot(fc, k) + ntok],
                            in_=uap[:, 0:ntok], func=AF.Relu,
                            bias=B["kinkb"][:, k:k + 1],
                            scale=(1.0 if k in TK else abs(KINK_C[k])))
                        sch.inc("S", i)

        def emit_epi_V(g):
            @block.vector
            def _(v):
                p5 = (g % 2) * 512
                uap = B["u0"] if g % 2 == 0 else B["u1"]
                sch.wait(v, "V", s_cnt(g, gf))
                i = v.tensor_scalar(out=B["acc_t"][:, 0:ntok_g],
                                    in0=B["s1_t"][:, p5:p5 + ntok_g],
                                    scalar1=0.5, scalar2=-MAGIC / 2,
                                    op0=ALU.mult, op1=ALU.add)
                sch.inc("V", i)
                sch.wait(v, "V", s_cnt(g, gf + 1))
                i = v.scalar_tensor_tensor(out=B["acc_t"][:, 0:ntok_g],
                                           in0=uap[:, 0:ntok_g],
                                           scalar=A0, in1=B["acc_t"][:, 0:ntok_g],
                                           op0=ALU.mult, op1=ALU.add)
                sch.inc("V", i)
                for idx, k in enumerate(RK):
                    sch.wait(v, "V", s_cnt(g, gf + 2 + idx))
                    i = v.scalar_tensor_tensor(
                        out=B["acc_t"][:, 0:ntok_g],
                        in0=B["rk_t"][:, rslot(g, k): rslot(g, k) + ntok_g],
                        scalar=(KINK_C[k] if k in TK
                                else (1.0 if KINK_C[k] > 0 else -1.0)),
                        in1=B["acc_t"][:, 0:ntok_g], op0=ALU.mult, op1=ALU.add)
                    sch.inc("V", i)
                i = v.tensor_scalar(out=B["a_t"][:, g * 512: g * 512 + ntok_g],
                                    in0=B["acc_t"][:, 0:ntok_g], scalar1=MAGIC,
                                    scalar2=-MAGIC, op0=ALU.add, op1=ALU.add)
                st.r[("aq", bid, g)] = sch.inc("V", i)

        nb1 = min(2, nb)
        gf = 1
        ngrp = 32 // gf
        ntok_g = ntok * gf
        S_PER = gf + 1 + len(RK)
        V_PER = 20

        def a_slice(fc, sub):
            if gf == 1:
                return B["a_t"][:, fc * 512 + sub * 128: fc * 512 + (sub + 1) * 128]
            return B["a_t"][:, fc * 128: (fc + 1) * 128]

        def emit_ff2a(fc):
            sl2 = fc % 2

            @block.tensor
            def _(te):
                sch.wait(te, "T", ("V", st.b_vbase + V_PER * (fc // gf) + 20))
                sch.wait(te, "T", st.r[("w2w", bid, fc)])
                i = None
                for sub in range(nb):
                    a_ap = a_slice(fc, sub)
                    te.matmul(po[:, sub * 512:(sub + 1) * 512], a_ap,
                              B["w2s_hi"][:, sl2 * D: sl2 * D + 512],
                              start=(fc == 0), stop=False)
                    i = te.matmul(po[:, sub * 512:(sub + 1) * 512], a_ap,
                                  B["w2s_lo"][:, sl2 * D: sl2 * D + 512],
                                  start=False, stop=(fc == 31))
                    if fc == 0:
                        te.matmul(po[:, sub * 512:(sub + 1) * 512],
                                  B["ones_f16"][0:1, 0:128],
                                  B["b2r_hi"][0:1, 0:512],
                                  start=False, stop=False)
                        i = te.matmul(po[:, sub * 512:(sub + 1) * 512],
                                      B["ones_f16"][0:1, 0:128],
                                      B["b2r_lo"][0:1, 0:512],
                                      start=False, stop=False)
                st.r[("ff2a", bid, fc)] = sch.inc("T", i)

        def emit_ff2b(fc):
            sl2 = fc % 2

            @block.tensor
            def _(te):
                i = None
                for sub in range(nb1):
                    a_ap = a_slice(fc, sub)
                    te.matmul(po1[:, sub * 512:(sub + 1) * 512], a_ap,
                              B["w2s_hi"][:, sl2 * D + 512: (sl2 + 1) * D],
                              start=(fc == 0), stop=False)
                    i = te.matmul(po1[:, sub * 512:(sub + 1) * 512], a_ap,
                                  B["w2s_lo"][:, sl2 * D + 512: (sl2 + 1) * D],
                                  start=False, stop=(fc == 31))
                    if fc == 0:
                        te.matmul(po1[:, sub * 512:(sub + 1) * 512],
                                  B["ones_f16"][0:1, 0:128],
                                  B["b2r_hi"][0:1, 512:1024],
                                  start=False, stop=False)
                        i = te.matmul(po1[:, sub * 512:(sub + 1) * 512],
                                      B["ones_f16"][0:1, 0:128],
                                      B["b2r_lo"][0:1, 512:1024],
                                      start=False, stop=False)
                st.r[("ff2b", bid, fc)] = sch.inc("T", i)

        st.b_sbase = sch.count["S"]
        st.b_vbase = sch.count["V"]
        lag = 2 * gf
        emit_w2_stream(0)
        emit_w2_stream(1)
        for pr in range(16):
            f0 = 2 * pr
            emit_w1_stream(f0)
            emit_w1_stream(f0 + 1)
            emit_ff1_T(f0)
            emit_ff1_T(f0 + 1)
            if f0 >= lag:
                emit_ff2a(f0 - lag)
                emit_ff2b(f0 - lag)
                emit_ff2a(f0 - lag + 1)
                emit_ff2b(f0 - lag + 1)
                if f0 - lag + 2 < 32:
                    emit_w2_stream(f0 - lag + 2)
                if f0 - lag + 3 < 32:
                    emit_w2_stream(f0 - lag + 3)
        for fc in range(32 - lag, 32):
            emit_ff2a(fc)
            emit_ff2b(fc)
            if fc + 2 < 32:
                emit_w2_stream(fc + 2)
        emit_epi_S()
        for g in range(ngrp):
            emit_epi_V(g)

        for sub, m in enumerate(blocks):
            @block.scalar
            def _(s, sub=sub, m=m):
                sch.wait(s, "S", st.r[("ff2a", bid, 31)])
                if sub > 0:
                    sch.wait(s, "S", st.r.get(("oqa", blocks[sub - 1]), None))
                i = s.activation(out=B["s3_t"], in_=po[:, sub * 512:(sub + 1) * 512],
                                 func=AF.Identity, bias=B["pmag"][:, 0:1], scale=1.0)
                st.r[("s3a", m)] = sch.inc("S", i)

            @block.vector
            def _(v, sub=sub, m=m):
                sch.wait(v, "V", st.r[("s3a", m)])
                i = v.tensor_scalar(out=B["oh16"][:, sub * 512:(sub + 1) * 512],
                                    in0=B["s3_t"], scalar1=-MAGIC,
                                    scalar2=None, op0=ALU.add)
                st.r[("oqa", m)] = sch.inc("V", i)

        def emit_ln2(sub, m, src_po, gate_key):
            @block.scalar
            def _(s, sub=sub, m=m):
                sch.wait(s, "S", st.r[gate_key])
                if sub == 0:
                    sch.wait(s, "S", st.r[("oqa", blocks[-1])])
                sch.wait(s, "S", st.r.get(("r2", m - 1), None))
                i = s.activation(out=B["s3_t"], in_=src_po,
                                 func=AF.Identity, bias=B["pmag"][:, 0:1], scale=1.0)
                st.r[("s3b", m)] = sch.inc("S", i)

            @block.vector
            def _(v, sub=sub, m=m):
                sch.wait(v, "V", st.r[("s3b", m)])
                sch.wait(v, "V", st.last_ln_sq)
                sch.wait(v, "V", st.r.get(("ff2c", bid, 15), None))
                v.tensor_scalar(out=B["u0"][:, 0:512], in0=B["s3_t"],
                                scalar1=-MAGIC, scalar2=1.0 / 256.0,
                                op0=ALU.add, op1=ALU.mult)
                v.scalar_tensor_tensor(
                    out=B["r2_t"][:, 0:512], in0=B["oh16"][:, sub * 512:(sub + 1) * 512],
                    scalar=1.0 / 256.0, in1=B["y_t"][:, (m % 4) * D:(m % 4) * D + 512],
                    op0=ALU.mult, op1=ALU.add, accum_out=B["st_t"][:, 0:1])
                i = v.scalar_tensor_tensor(
                    out=B["r2_t"][:, 512:1024], in0=B["u0"][:, 0:512],
                    scalar=1.0, in1=B["y_t"][:, (m % 4) * D + 512:(m % 4 + 1) * D],
                    op0=ALU.mult, op1=ALU.add, accum_out=B["st_t"][:, 9:10])
                st.r[(("ln2", m), "r")] = sch.inc("V", i)
                st.r[("r2", m)] = st.r[(("ln2", m), "r")]

            _emit_ln_tail(nc, block, st, ("ln2", m), B["r2_t"],
                          (B["s1_t"][:, 0:512], B["s1_t"][:, 512:1024]),
                          pre_tok=st.last_store, sum2=True)
            st.r[("ob", m)] = st.r[("ln2", m)]

            @block.gpsimd
            def _(g, m=m):
                sch.wait(g, "G", st.r[("ob", m)])
                d = g.dma_start(out=st.H["out"][m * 128:(m + 1) * 128, 0:512],
                                in_=B["s1_t"][:, 0:512])
                sch.inc("DS", d)
                d = g.dma_start(out=st.H["out"][m * 128:(m + 1) * 128, 512:1024],
                                in_=B["s1_t"][:, 512:1024])
                st.last_store = sch.inc("DS", d)

        if nb > 2:
            def wc_hi(sl):
                return B["w2s_hi"][:, sl * 512: (sl + 1) * 512]

            def wc_lo(sl):
                return B["w2s_lo"][:, sl * 512: (sl + 1) * 512]

            for fc in range(32):
                sl = fc % 4

                @block.sync
                def _(y, fc=fc, sl=sl):
                    if fc < 4:
                        sch.wait(y, "Y", st.r[("ff2b", bid, 30 + (sl // 2))])
                    sch.wait(y, "Y", st.r.get(("ff2c", bid, fc - 4), None))
                    d = y.dma_start(out=wc_hi(sl), in_=H["w2_hi"][fc][:, 512:1024])
                    sch.inc("DL", d)
                    d = y.dma_start(out=wc_lo(sl), in_=H["w2_lo"][fc][:, 512:1024])
                    st.r[("w2wc", bid, fc)] = sch.inc("DL", d)

                @block.tensor
                def _(te, fc=fc, sl=sl):
                    if fc == 0:
                        sch.wait(te, "T", st.r[("s3a", blocks[-1])])
                    sch.wait(te, "T", st.r[("w2wc", bid, fc)])
                    i = None
                    for j, sub in enumerate(range(2, nb)):
                        a_ap = a_slice(fc, sub)
                        te.matmul(po[:, j * 512:(j + 1) * 512], a_ap,
                                  wc_hi(sl),
                                  start=(fc == 0), stop=False)
                        i = te.matmul(po[:, j * 512:(j + 1) * 512], a_ap,
                                      wc_lo(sl),
                                      start=False, stop=(fc == 31))
                        if fc == 0:
                            te.matmul(po[:, j * 512:(j + 1) * 512],
                                      B["ones_f16"][0:1, 0:128],
                                      B["b2r_hi"][0:1, 512:1024],
                                      start=False, stop=False)
                            i = te.matmul(po[:, j * 512:(j + 1) * 512],
                                          B["ones_f16"][0:1, 0:128],
                                          B["b2r_lo"][0:1, 512:1024],
                                          start=False, stop=False)
                    st.r[("ff2c", bid, fc)] = sch.inc("T", i)

            st.wvs_last = st.r[("ff2c", bid, 31)]
        else:
            st.wvs_last = st.r[("ff2b", bid, 31)]
        for sub in range(nb1):
            emit_ln2(sub, blocks[sub], po1[:, sub * 512:(sub + 1) * 512],
                     ("ff2b", bid, 31))
        for j, sub in enumerate(range(2, nb)):
            emit_ln2(sub, blocks[sub], po[:, j * 512:(j + 1) * 512],
                     ("ff2c", bid, 31))
        st.qks_last = st.r[("ff1mm", bid, 31)]

        st.b_s3b = st.r[("s3b", blocks[-1])]
        st.b_s1last = ("S", st.b_sbase + S_PER * (ngrp - 1) + gf)
        st.b_vdone = sch.tok("V")


def kernel(L0_attn_x, L0_attn_Wqkv, L0_attn_bqkv, L0_attn_Wout, L0_attn_bout,
           L0_ffn_w1, L0_ffn_b1, L0_ffn_w2, L0_ffn_b2):
    nc = build()
    in_maps = []
    for c in range(8):
        in_maps.append(dict(
            x=np.ascontiguousarray(L0_attn_x[c]),
            wqkv=np.ascontiguousarray(L0_attn_Wqkv),
            bqkv=np.ascontiguousarray(L0_attn_bqkv),
            wout=np.ascontiguousarray(L0_attn_Wout),
            bout=np.ascontiguousarray(L0_attn_bout),
            w1=np.ascontiguousarray(L0_ffn_w1),
            b1=np.ascontiguousarray(L0_ffn_b1),
            w2=np.ascontiguousarray(L0_ffn_w2),
            b2=np.ascontiguousarray(L0_ffn_b2),
        ))
    res = run_bass_kernel_spmd(nc, in_maps, core_ids=list(range(8)))
    return np.stack([r["out"] for r in res.results], axis=0)


# version 21
# speedup vs baseline: 1.0607x; 1.0193x over previous
import math
import numpy as np
import concourse.bass as bass
import concourse.mybir as mybir
from concourse.bass_utils import run_bass_kernel_spmd

F32 = mybir.dt.float32
BF16 = mybir.dt.bfloat16
F16 = mybir.dt.float16
AF = mybir.ActivationFunctionType
ALU = mybir.AluOpType
MAGIC = float(3 * 2 ** 22)

D = 1024
FF = 4096
NB = 32
NT = 8
SCALE = 1.0 / 32.0


def _pwl_params():
    X = np.linspace(-4.0, 4.0, 33).astype(np.float32)
    erf = np.vectorize(math.erf)
    Y = (0.5 * X * (1.0 + erf(X.astype(np.float64) / math.sqrt(2.0)))).astype(np.float32)
    Gh = np.array([256.0 * (float(Y[16 + k]) - 0.125 * k) for k in range(17)])
    s = [(Gh[k + 1] - Gh[k]) / 64.0 for k in range(16)]
    s16 = 512.5 - Gh[16]
    Ks = [64.0 * k for k in range(1, 16)] + [1024.0, 1025.0]
    cs = [s[k] - s[k - 1] for k in range(1, 16)] + [s16 - s[15], 0.5 - s16]
    return float(s[0]), [float(k) for k in Ks], [float(c) for c in cs]


A0, KINK_K, KINK_C = _pwl_params()
NK = len(KINK_K)
GK = [5, 6, 7, 8]
TK = [15, 16]
RK = list(range(NK))
assert all(KINK_C[k] < 0 for k in GK)


class Sch:
    def __init__(self, nc, ctx):
        self.nc = nc
        self.sem = {}
        self.count = {}
        self.last_wait = {w: {} for w in ("T", "V", "S", "G", "Y")}
        for name in ("T", "V", "S", "G", "DL", "DG", "DS"):
            self.sem[name] = ctx.enter_context(nc.semaphore("sem_" + name))
            self.count[name] = 0

    def inc(self, name, inst):
        step = 16 if name in ("DL", "DG", "DS") else 1
        inst.then_inc(self.sem[name], step)
        self.count[name] += step
        return (name, self.count[name])

    def tok(self, name):
        return (name, self.count[name])

    def wait(self, eng, wname, token):
        if not token:
            return
        name, val = token
        if val <= 0:
            return
        lw = self.last_wait[wname]
        if lw.get(name, 0) >= val:
            return
        eng.wait_ge(self.sem[name], val)
        lw[name] = val


class St:
    def __init__(self, sch, H, B):
        self.sch = sch
        self.H = H
        self.B = B
        self.r = {}
        self.last_store = None
        self.last_ln_sq = None
        self.wvs_last = None
        self.qks_last = None
        self.b_s3b = None
        self.b_s1last = None
        self.b_vdone = None
        self.a2_ln_last = None
        self.a2_exp_last = None
        self.sc_guard = None
        self.pv_guard = None


def _barrier(block, st, include_y=False):
    sch = st.sch
    toks = [sch.tok(n) for n in ("T", "V", "S", "G", "DL", "DG", "DS")]
    pairs = [("T", block.tensor), ("V", block.vector), ("S", block.scalar),
             ("G", block.gpsimd)]
    if include_y:
        pairs.append(("Y", block.sync))
    for name, section in pairs:
        @section
        def _(e, _t=toks, _n=name):
            for tk in _t:
                sch.wait(e, _n, tk)


def build(limit_tiles=NT, phases="ab", debug=False):
    nc = bass.Bass("TRN2", target_bir_lowering=False, detect_race_conditions=False)

    H = {}
    H["x"] = nc.dram_tensor("x", [4096, D], F32, kind="ExternalInput").ap()
    H["wqkv"] = nc.dram_tensor("wqkv", [D, 3 * D], F32, kind="ExternalInput").ap()
    H["bqkv"] = nc.dram_tensor("bqkv", [3 * D], F32, kind="ExternalInput").ap()
    H["wout"] = nc.dram_tensor("wout", [D, D], F32, kind="ExternalInput").ap()
    H["bout"] = nc.dram_tensor("bout", [D], F32, kind="ExternalInput").ap()
    H["w1"] = nc.dram_tensor("w1", [D, FF], F32, kind="ExternalInput").ap()
    H["b1"] = nc.dram_tensor("b1", [FF], F32, kind="ExternalInput").ap()
    H["w2"] = nc.dram_tensor("w2", [FF, D], F32, kind="ExternalInput").ap()
    H["b2"] = nc.dram_tensor("b2", [D], F32, kind="ExternalInput").ap()
    H["out"] = nc.dram_tensor("out", [4096, D], F32, kind="ExternalOutput").ap()
    H["ydbg"] = nc.dram_tensor("ydbg", [4096, D], F32, kind="ExternalOutput").ap() if debug else None

    H["qk_hi"] = nc.dram_tensor("qk_hi", [16, 8, 128, 128], BF16).ap()
    H["qk_lo"] = nc.dram_tensor("qk_lo", [16, 8, 128, 128], BF16).ap()
    H["wv_hi"] = nc.dram_tensor("wv_hi", [8, 128, D], BF16).ap()
    H["wv_lo"] = nc.dram_tensor("wv_lo", [8, 128, D], BF16).ap()
    H["wo_hi"] = nc.dram_tensor("wo_hi", [8, 128, D], BF16).ap()
    H["wo_lo"] = nc.dram_tensor("wo_lo", [8, 128, D], BF16).ap()
    H["w1_hi"] = nc.dram_tensor("w1_hi", [32, 8, 128, 128], BF16).ap()
    H["w1_lo"] = nc.dram_tensor("w1_lo", [32, 8, 128, 128], BF16).ap()
    H["w2_hi"] = nc.dram_tensor("w2_hi", [32, 128, D], F16).ap()
    H["w2_lo"] = nc.dram_tensor("w2_lo", [32, 128, D], F16).ap()
    H["b1r_hi"] = nc.dram_tensor("b1r_hi", [32, 1, 128], BF16).ap()
    H["b1r_lo"] = nc.dram_tensor("b1r_lo", [32, 1, 128], BF16).ap()

    from contextlib import ExitStack
    es = ExitStack()

    def sb(name, shape, dt=F32):
        return es.enter_context(nc.sbuf_tensor(name, shape, dt))

    B = {}
    B["x_t"] = None
    B["q_t"] = None
    B["bqk_sb"] = sb("bqk_sb", [128, 16])
    B["bvr_hi"] = sb("bvr_hi", [1, D], BF16)
    B["bvr_lo"] = sb("bvr_lo", [1, D], BF16)
    B["bor_hi"] = sb("bor_hi", [1, D], BF16)
    B["bor_lo"] = sb("bor_lo", [1, D], BF16)
    B["b2r_hi"] = sb("b2r_hi", [1, D], F16)
    B["b2r_lo"] = sb("b2r_lo", [1, D], F16)
    B["iota_p"] = sb("iota_p", [128, 1])
    B["iota_f"] = sb("iota_f", [128, 128])
    B["id_f32"] = sb("id_f32", [128, 128])
    B["id_bf"] = sb("id_bf", [128, 128], BF16)
    B["m0t"] = sb("m0t", [128, 128], BF16)
    B["m2t"] = sb("m2t", [128, 128], BF16)
    B["ones_bf"] = sb("ones_bf", [1, 512], BF16)
    B["ones_f16"] = sb("ones_f16", [1, 128], F16)
    B["kinkb"] = sb("kinkb", [128, NK])
    B["pmag"] = sb("pmag", [128, 1])
    B["nmag"] = sb("nmag", [128, 1])
    B["st_t"] = sb("st_t", [128, 16])

    with es, nc.Block() as block:
        sch = Sch(nc, es)
        st = St(sch, H, B)
        _emit_consts(nc, block, st)
        _emit_prep(nc, block, st)
        _barrier(block, st, include_y=True)

        B["x_t"] = sb("x_t", [128, 5 * D])
        B["y_t"] = sb("y_t", [128, 4 * D])
        B["q_t"] = sb("q_t", [128, 8 * 5 * 128])
        B["k_t"] = sb("k_t", [128, 8 * 6 * 128])
        B["vt_t"] = sb("vt_t", [128, 6 * 1040])
        B["xt_hi"] = sb("xt_hi", [128, 4 * 8 * 128], BF16)
        B["xt_lo"] = sb("xt_lo", [128, 4 * 8 * 128], BF16)
        B["at_hi"] = sb("at_hi", [128, 8 * 128], BF16)
        B["at_lo"] = sb("at_lo", [128, 8 * 128], BF16)
        B["attn_tm"] = sb("attn_tm", [128, D])
        B["e_t"] = sb("e_t", [128, 2 * 384])
        B["a_t"] = sb("a_t", [128, 32 * 512], F16)
        B["qks_hi"] = sb("qks_hi", [128, 2 * 1024], BF16)
        B["qks_lo"] = sb("qks_lo", [128, 2 * 1024], BF16)
        B["wvs_hi"] = sb("wvs_hi", [128, 2 * D], BF16)
        B["wvs_lo"] = sb("wvs_lo", [128, 2 * D], BF16)
        B["w1s_hi"] = B["qks_hi"]
        B["w1s_lo"] = B["qks_lo"]
        B["w2s_hi"] = B["wvs_hi"][:].bitcast(F16)
        B["w2s_lo"] = B["wvs_lo"][:].bitcast(F16)
        B["b1s_hi"] = sb("b1s_hi", [1, 2 * 128], BF16)
        B["b1s_lo"] = sb("b1s_lo", [1, 2 * 128], BF16)
        B["s1_t"] = sb("s1_t", [128, 2 * 512])
        B["rk_t"] = sb("rk_t", [128, 2 * 1024])
        B["s3_t"] = B["e_t"][:, 0:512]
        B["r2_t"] = B["a_t"][:, 0:2048].bitcast(F32)
        B["sq_t"] = B["a_t"][:, 2048:4096].bitcast(F32)
        B["oh16"] = B["attn_tm"][:].bitcast(F16)
        B["acc_t"] = B["attn_tm"][:, 0:512]
        B["t1_t"] = B["attn_tm"][:, 512:1024]
        B["u_t"] = None
        B["u0"] = B["at_hi"][:].bitcast(F32)
        B["u1"] = B["at_lo"][:].bitcast(F32)
        @block.vector
        def _(v):
            i = None
            for s in range(6):
                i = v.memset(B["vt_t"][:, s * 1040 + 64: (s + 1) * 1040: 65], 1.0)
            sch.inc("V", i)
        for t in range(limit_tiles):
            _emit_A1(nc, block, st, t, sub=("all" if phases not in ("x", "tr", "tr1", "tr1a", "qk") else phases))
            if phases in ("a1", "x", "tr", "tr1", "tr1a", "qk"):
                _barrier(block, st)
                continue
            _emit_A2(nc, block, st, t, ms=(None if t < NT - 1 else list(range(27, 31))))
            if phases == "a2":
                _barrier(block, st)
                continue
            if t == NT - 1:
                _emit_B(nc, block, st, t, list(range(27, 31)))
                _barrier(block, st)
                _emit_A2(nc, block, st, t, ms=[31], tag="x")
                _barrier(block, st)
                _emit_B(nc, block, st, t, [31])
            else:
                _emit_B(nc, block, st, t, list(range(max(0, 4 * t - 1), 4 * t + 3)))

        @block.gpsimd
        def _(g):
            g.wait_ge(sch.sem["DS"], sch.count["DS"])
    return nc


def _emit_consts(nc, block, st):
    sch, B = st.sch, st.B

    @block.gpsimd
    def _(g):
        g.iota(B["iota_p"][:], pattern=[[0, 1]], channel_multiplier=1,
               allow_small_or_imprecise_dtypes=True)
        i = g.iota(B["iota_f"][:], pattern=[[1, 128]], channel_multiplier=0,
                   allow_small_or_imprecise_dtypes=True)
        sch.inc("G", i)

    @block.vector
    def _(v):
        sch.wait(v, "V", sch.tok("G"))
        v.tensor_scalar(out=B["id_f32"][:], in0=B["iota_f"][:],
                        scalar1=B["iota_p"][:, 0:1], scalar2=None, op0=ALU.is_equal)
        v.tensor_copy(out=B["id_bf"][:], in_=B["id_f32"][:])
        v.tensor_scalar(out=B["m0t"][:], in0=B["iota_f"][:],
                        scalar1=B["iota_p"][:, 0:1], scalar2=-1.0e30,
                        op0=ALU.is_lt, op1=ALU.mult)
        v.tensor_scalar(out=B["m2t"][:], in0=B["iota_f"][:],
                        scalar1=B["iota_p"][:, 0:1], scalar2=-1.0e30,
                        op0=ALU.is_gt, op1=ALU.mult)
        v.memset(B["ones_bf"][:], 1.0)
        v.memset(B["ones_f16"][:], 1.0)
        for k in range(NK):
            v.memset(B["kinkb"][:, k:k + 1],
                     -KINK_K[k] if k in TK else -abs(KINK_C[k]) * KINK_K[k])
        v.memset(B["pmag"][:], MAGIC)
        i = v.memset(B["nmag"][:], -MAGIC)
        sch.inc("V", i)


def _emit_prep(nc, block, st):
    sch, H, B = st.sch, st.H, st.B
    with nc.sbuf_tensor("pf0", [128, 4096], F32) as pf0, \
         nc.sbuf_tensor("pf1", [128, 4096], F32) as pf1, \
         nc.sbuf_tensor("ph0", [128, 4096], BF16) as ph0, \
         nc.sbuf_tensor("ph1", [128, 4096], BF16) as ph1, \
         nc.sbuf_tensor("pl0", [128, 4096], BF16) as pl0, \
         nc.sbuf_tensor("pl1", [128, 4096], BF16) as pl1, \
         nc.sbuf_tensor("ph16", [128, 2 * 1024], F16) as ph16, \
         nc.sbuf_tensor("pl16", [128, 2 * 1024], F16) as pl16, \
         nc.sbuf_tensor("brow", [1, 4096], F32) as brow, \
         nc.sbuf_tensor("brh", [1, 4096], BF16) as brh, \
         nc.sbuf_tensor("brl", [1, 4096], BF16) as brl:

        pf = [pf0, pf1]
        phh = [ph0, ph1]
        pll = [pl0, pl1]
        jobs = []
        for dc in range(8):
            jobs.append((H["wqkv"][dc * 128:(dc + 1) * 128, 0:2048], 2048,
                         H["qk_hi"][:, dc], H["qk_lo"][:, dc], "cc"))
        for dc in range(8):
            jobs.append((H["wqkv"][dc * 128:(dc + 1) * 128, 2048:3072], 1024,
                         H["wv_hi"][dc], H["wv_lo"][dc], None))
        for dc in range(8):
            jobs.append((H["wout"][dc * 128:(dc + 1) * 128, :], 1024,
                         H["wo_hi"][dc], H["wo_lo"][dc], None))
        for dc in range(8):
            jobs.append((H["w1"][dc * 128:(dc + 1) * 128, :], 4096,
                         H["w1_hi"][:, dc], H["w1_lo"][:, dc], "fc"))
        for fc in range(32):
            jobs.append((H["w2"][fc * 128:(fc + 1) * 128, :], 1024,
                         H["w2_hi"][fc], H["w2_lo"][fc], "f16"))

        for j, (src, n, hid, lod, mode) in enumerate(jobs):
            p = j % 2

            @block.sync
            def _(y, j=j, p=p, src=src, n=n):
                sch.wait(y, "Y", st.r.get(("pst", j - 2), None))
                d = y.dma_start(out=pf[p][:, 0:n], in_=src)
                st.r[("pld", j)] = sch.inc("DL", d)

            @block.vector
            def _(v, j=j, p=p, n=n, mode=mode):
                sch.wait(v, "V", st.r[("pld", j)])
                sch.wait(v, "V", st.r.get(("pst", j - 2), None))
                if mode == "f16":
                    v.tensor_copy(out=ph16[:, p * 1024:p * 1024 + n], in_=pf[p][:, 0:n])
                    i = v.scalar_tensor_tensor(
                        out=pl16[:, p * 1024:p * 1024 + n],
                        in0=ph16[:, p * 1024:p * 1024 + n], scalar=-1.0,
                        in1=pf[p][:, 0:n], op0=ALU.mult, op1=ALU.add)
                else:
                    v.tensor_copy(out=phh[p][:, 0:n], in_=pf[p][:, 0:n])
                    i = v.scalar_tensor_tensor(
                        out=pll[p][:, 0:n], in0=phh[p][:, 0:n], scalar=-1.0,
                        in1=pf[p][:, 0:n], op0=ALU.mult, op1=ALU.add)
                st.r[("psp", j)] = sch.inc("V", i)

            @block.gpsimd
            def _(g, j=j, p=p, n=n, hid=hid, lod=lod, mode=mode):
                sch.wait(g, "G", st.r[("psp", j)])
                if mode == "f16":
                    shi = ph16[:, p * 1024:p * 1024 + n]
                    slo = pl16[:, p * 1024:p * 1024 + n]
                else:
                    shi = phh[p][:, 0:n]
                    slo = pll[p][:, 0:n]
                if mode in ("cc", "fc"):
                    shi = shi.rearrange("p (c m) -> p c m", m=128)
                    slo = slo.rearrange("p (c m) -> p c m", m=128)
                    hid = hid.rearrange("c p m -> p c m")
                    lod = lod.rearrange("c p m -> p c m")
                d = g.dma_start(out=hid, in_=shi)
                sch.inc("DS", d)
                d = g.dma_start(out=lod, in_=slo)
                st.r[("pst", j)] = sch.inc("DS", d)

        @block.sync
        def _(y):
            sch.wait(y, "Y", st.r[("pst", len(jobs) - 1)])
            for cc in range(16):
                d = y.dma_start(out=B["bqk_sb"][:, cc:cc + 1],
                                in_=H["bqkv"][cc * 128:(cc + 1) * 128])
                sch.inc("DL", d)
            d = y.dma_start(out=brow[0:1, 0:1024], in_=H["bqkv"][2048:3072])
            sch.inc("DL", d)
            d = y.dma_start(out=brow[0:1, 1024:2048], in_=H["bout"])
            sch.inc("DL", d)
            d = y.dma_start(out=brow[0:1, 2048:3072], in_=H["b2"])
            sch.inc("DL", d)

        @block.vector
        def _(v):
            sch.wait(v, "V", sch.tok("DL"))
            v.tensor_copy(out=B["bvr_hi"][:], in_=brow[0:1, 0:1024])
            v.scalar_tensor_tensor(out=B["bvr_lo"][:], in0=B["bvr_hi"][:], scalar=-1.0,
                                   in1=brow[0:1, 0:1024], op0=ALU.mult, op1=ALU.add)
            v.tensor_copy(out=B["bor_hi"][:], in_=brow[0:1, 1024:2048])
            v.scalar_tensor_tensor(out=B["bor_lo"][:], in0=B["bor_hi"][:], scalar=-1.0,
                                   in1=brow[0:1, 1024:2048], op0=ALU.mult, op1=ALU.add)
            v.tensor_scalar(out=brow[0:1, 3072:4096], in0=brow[0:1, 2048:3072],
                            scalar1=256.0, scalar2=None, op0=ALU.mult)
            v.tensor_copy(out=B["b2r_hi"][:], in_=brow[0:1, 3072:4096])
            i = v.scalar_tensor_tensor(out=B["b2r_lo"][:], in0=B["b2r_hi"][:], scalar=-1.0,
                                       in1=brow[0:1, 3072:4096], op0=ALU.mult, op1=ALU.add)
            st.r[("bprep",)] = sch.inc("V", i)

        @block.sync
        def _(y):
            sch.wait(y, "Y", st.r[("bprep",)])
            d = y.dma_start(out=brow[0:1, 0:4096], in_=H["b1"])
            sch.inc("DL", d)

        @block.vector
        def _(v):
            sch.wait(v, "V", sch.tok("DL"))
            v.tensor_copy(out=brh[:], in_=brow[0:1, 0:4096])
            i = v.scalar_tensor_tensor(out=brl[:], in0=brh[:], scalar=-1.0,
                                       in1=brow[0:1, 0:4096], op0=ALU.mult, op1=ALU.add)
            st.r[("b1sp",)] = sch.inc("V", i)

        @block.gpsimd
        def _(g):
            sch.wait(g, "G", st.r[("b1sp",)])
            d = g.dma_start(out=H["b1r_hi"].rearrange("fc o m -> o fc m"),
                            in_=brh[:].rearrange("o (fc m) -> o fc m", m=128))
            sch.inc("DS", d)
            d = g.dma_start(out=H["b1r_lo"].rearrange("fc o m -> o fc m"),
                            in_=brl[:].rearrange("o (fc m) -> o fc m", m=128))
            sch.inc("DS", d)


def _emit_A1(nc, block, st, t, sub="all"):
    sch, H, B = st.sch, st.H, st.B
    blocks = list(range(4 * t, 4 * t + 4))

    @block.sync
    def _(y):
        sch.wait(y, "Y", sch.tok("DS"))
        for n in blocks:
            sch.wait(y, "Y", st.r.get((("ln1", n - 5), "r"), None))
            d = y.dma_start(out=B["x_t"][:, (n % 5) * D:(n % 5 + 1) * D],
                            in_=H["x"][n * 128:(n + 1) * 128, :])
            st.r[("x", n)] = sch.inc("DL", d)

    if sub == "x":
        return
    with nc.psum_tensor(f"ptr_{t}", [128, 1024], F32) as ptr:
        for bi, n in enumerate(blocks):
            if sub == "tr1" and bi > 0:
                break
            for dc in range(8):
                if sub in ("tr1", "tr1a") and dc >= 2:
                    break
                @block.tensor
                def _(te, n=n, dc=dc, bi=bi):
                    if dc == 0:
                        if bi == 0:
                            sch.wait(te, "T", st.b_s1last)
                        sch.wait(te, "T", st.r[("x", n)])
                    pb = dc % 2
                    prev = ("trev", n, dc - 2) if dc >= 2 else ("trev", n - 1, 6 + dc)
                    sch.wait(te, "T", st.r.get(prev, None))
                    i = te.transpose(ptr[:, pb * 512: pb * 512 + 128],
                                     B["x_t"][:, (n % 5) * D + dc * 128:(n % 5) * D + (dc + 1) * 128],
                                     B["id_f32"][:])
                    st.r[("tr", n, dc)] = sch.inc("T", i)

                if sub != "tr1a":
                    @block.vector
                    def _(v, n=n, dc=dc, bi=bi):
                        pb = dc % 2
                        col = ((n % 4) * 8 + dc) * 128
                        if bi == 0 and dc == 0:
                            sch.wait(v, "V", st.qks_last)
                        sch.wait(v, "V", st.r[("tr", n, dc)])
                        v.tensor_copy(out=B["xt_hi"][:, col:col + 128],
                                      in_=ptr[:, pb * 512: pb * 512 + 128])
                        i = v.scalar_tensor_tensor(
                            out=B["xt_lo"][:, col:col + 128],
                            in0=B["xt_hi"][:, col:col + 128], scalar=-1.0,
                            in1=ptr[:, pb * 512: pb * 512 + 128], op0=ALU.mult, op1=ALU.add)
                        st.r[("trev", n, dc)] = sch.inc("V", i)

    xt_ready = sch.tok("V")
    if sub in ("tr", "tr1", "tr1a"):
        return

    with nc.psum_tensor(f"pqk_{t}", [128, 2 * 512], F32) as pqk:
        q_base = sch.count["S"]
        for cc in range(16):
            st.r[("qkev", t, cc)] = ("S", q_base + cc + 1)
        for cc in range(16):
            sl = cc % 2

            @block.sync
            def _(y, cc=cc, sl=sl):
                if cc < 2:
                    sch.wait(y, "Y", st.qks_last)
                sch.wait(y, "Y", st.r.get(("qkmm", t, cc - 2), None))
                d = y.dma_start(out=B["qks_hi"][:, sl * 1024:(sl + 1) * 1024].rearrange(
                    "p (dc m) -> p dc m", m=128), in_=H["qk_hi"][cc].rearrange("dc p m -> p dc m"))
                sch.inc("DL", d)
                d = y.dma_start(out=B["qks_lo"][:, sl * 1024:(sl + 1) * 1024].rearrange(
                    "p (dc m) -> p dc m", m=128), in_=H["qk_lo"][cc].rearrange("dc p m -> p dc m"))
                st.r[("qkw", t, cc)] = sch.inc("DL", d)

            @block.tensor
            def _(te, cc=cc, sl=sl):
                sch.wait(te, "T", st.r[("qkw", t, cc)])
                sch.wait(te, "T", xt_ready)
                sch.wait(te, "T", st.r.get(("qkev", t, cc - 2), None))
                rh4 = B["xt_hi"][:].rearrange("p (s dc m) -> p dc s m", dc=8, m=128)
                rl4 = B["xt_lo"][:].rearrange("p (s dc m) -> p dc s m", dc=8, m=128)
                i = None
                for dc in range(8):
                    whi = B["qks_hi"][:, sl * 1024 + dc * 128: sl * 1024 + (dc + 1) * 128]
                    wlo = B["qks_lo"][:, sl * 1024 + dc * 128: sl * 1024 + (dc + 1) * 128]
                    te.matmul(pqk[:, sl * 512:(sl + 1) * 512], whi, rh4[:, dc],
                              start=(dc == 0), stop=False)
                    te.matmul(pqk[:, sl * 512:(sl + 1) * 512], whi, rl4[:, dc],
                              start=False, stop=False)
                    i = te.matmul(pqk[:, sl * 512:(sl + 1) * 512], wlo, rh4[:, dc],
                                  start=False, stop=(dc == 7))
                st.r[("qkmm", t, cc)] = sch.inc("T", i)

        @block.scalar
        def _(s):
            sch.wait(s, "S", st.sc_guard)
            for cc in range(16):
                sl = cc % 2
                sch.wait(s, "S", st.r[("qkmm", t, cc)])
                dst = B["q_t"] if cc < 8 else B["k_t"]
                ring = 5 if cc < 8 else 6
                ccp = cc % 8
                for bi, n in enumerate(blocks):
                    rs = n % ring
                    i = s.activation(
                        out=dst[:, (ccp * ring + rs) * 128:(ccp * ring + rs + 1) * 128],
                        in_=pqk[:, sl * 512 + bi * 128: sl * 512 + (bi + 1) * 128],
                        func=AF.Identity, bias=B["bqk_sb"][:, cc:cc + 1], scale=1.0)
                sch.inc("S", i)

    qk_done = sch.tok("S")
    if sub == "qk":
        return

    with nc.psum_tensor(f"pv_{t}", [128, 4 * 1024], F32) as pv:
        for dc in range(8):
            sl = dc % 2

            @block.sync
            def _(y, dc=dc, sl=sl):
                if dc < 2:
                    sch.wait(y, "Y", st.wvs_last)
                sch.wait(y, "Y", st.r.get(("vmm", t, dc - 2), None))
                d = y.dma_start(out=B["wvs_hi"][:, sl * D:(sl + 1) * D], in_=H["wv_hi"][dc])
                sch.inc("DL", d)
                d = y.dma_start(out=B["wvs_lo"][:, sl * D:(sl + 1) * D], in_=H["wv_lo"][dc])
                st.r[("wv", t, dc)] = sch.inc("DL", d)

            @block.tensor
            def _(te, dc=dc, sl=sl):
                if dc == 0:
                    sch.wait(te, "T", qk_done)
                    sch.wait(te, "T", st.b_s3b)
                sch.wait(te, "T", st.r[("wv", t, dc)])
                i = None
                for bi, n in enumerate(blocks):
                    col = ((n % 4) * 8 + dc) * 128
                    xh = B["xt_hi"][:, col:col + 128]
                    xl = B["xt_lo"][:, col:col + 128]
                    for hf in range(2):
                        pvs = pv[:, bi * 1024 + hf * 512: bi * 1024 + (hf + 1) * 512]
                        te.matmul(pvs, xh,
                                  B["wvs_hi"][:, sl * D + hf * 512: sl * D + (hf + 1) * 512],
                                  start=(dc == 0), stop=False)
                        te.matmul(pvs, xh,
                                  B["wvs_lo"][:, sl * D + hf * 512: sl * D + (hf + 1) * 512],
                                  start=False, stop=False)
                        i = te.matmul(pvs, xl,
                                      B["wvs_hi"][:, sl * D + hf * 512: sl * D + (hf + 1) * 512],
                                      start=False, stop=False)
                st.r[("vmm", t, dc)] = sch.inc("T", i)

        @block.tensor
        def _(te):
            i = None
            for bi in range(4):
                for hf in range(2):
                    pvs = pv[:, bi * 1024 + hf * 512: bi * 1024 + (hf + 1) * 512]
                    te.matmul(pvs, B["ones_bf"][0:1, 0:128],
                              B["bvr_hi"][0:1, hf * 512:(hf + 1) * 512],
                              start=False, stop=False)
                    i = te.matmul(pvs, B["ones_bf"][0:1, 0:128],
                                  B["bvr_lo"][0:1, hf * 512:(hf + 1) * 512],
                                  start=False, stop=True)
            st.r[("vmm_all", t)] = sch.inc("T", i)
        st.qks_last = st.r[("qkmm", t, 15)]
        st.wvs_last = st.r[("vmm", t, 7)]

        @block.vector
        def _(v):
            sch.wait(v, "V", st.pv_guard)
            sch.wait(v, "V", st.r[("vmm_all", t)])
            for bi, n in enumerate(blocks):
                rs = n % 6
                dst = B["vt_t"][:, rs * 1040:(rs + 1) * 1040].rearrange(
                    "p (h c) -> p h c", c=65)[:, :, 0:64]
                src = pv[:, bi * 1024:(bi + 1) * 1024].rearrange(
                    "p (h c) -> p h c", c=64)
                i = v.tensor_copy(out=dst, in_=src)
                st.r[("v", n)] = sch.inc("V", i)


def _emit_A2(nc, block, st, t, ms=None, tag=""):
    sch, H, B = st.sch, st.H, st.B
    if ms is None:
        ms = list(range(max(0, 4 * t - 1), 4 * t + 3))

    with nc.psum_tensor(f"psc_{t}{tag}", [128, 2 * 512], F32) as psc, \
         nc.psum_tensor(f"ppv_{t}{tag}", [128, 1024], F32) as ppv, \
         nc.psum_tensor(f"ptr2_{t}{tag}", [128, 1024], F32) as ptr2, \
         nc.psum_tensor(f"pwo_{t}{tag}", [128, 1024], F32) as pwo:

        def emit_sc(m, h, chunks):
            cc = h // 2
            half = (h % 2) * 64
            pb = h % 2
            qs = m % 5

            @block.tensor
            def _(te):
                if m == ms[0] and h == 0:
                    sch.wait(te, "T", st.r.get(("qkev", t, 15), None))
                    sch.wait(te, "T", st.r.get(("v", 4 * t), None))
                    sch.wait(te, "T", st.b_s3b)
                sch.wait(te, "T", st.r.get(("exp", m, h - 2), None))
                if h <= 1:
                    sch.wait(te, "T", st.r.get(("exp", m - 1, 14 + h), None))
                i = None
                for (c, ks) in chunks:
                    lhs = B["k_t"][half:half + 64,
                                   (cc * 6 + ks) * 128:(cc * 6 + ks + 1) * 128]
                    rhs = B["q_t"][half:half + 64,
                                   (cc * 5 + qs) * 128:(cc * 5 + qs + 1) * 128]
                    i = te.matmul(psc[:, pb * 512 + c * 128: pb * 512 + (c + 1) * 128],
                                  lhs, rhs, start=True, stop=(c == 1))
                    if c == 0:
                        i = te.matmul(psc[:, pb * 512: pb * 512 + 128],
                                      B["m0t"][:], B["id_bf"][:], start=False, stop=True)
                    elif c == 2:
                        i = te.matmul(psc[:, pb * 512 + 256: pb * 512 + 384],
                                      B["m2t"][:], B["id_bf"][:], start=False, stop=True)
                st.r[("sc", m, h)] = sch.inc("T", i)

        def emit_exp_section(m, chunks):
            @block.scalar
            def _(s):
                if m == ms[0]:
                    sch.wait(s, "S", st.b_vdone)
                for h in range(16):
                    pb = h % 2
                    sch.wait(s, "S", st.r[("sc", m, h)])
                    sch.wait(s, "S", st.r.get(("pv", m, h - 2), None))
                    if h <= 1:
                        sch.wait(s, "S", st.r.get(("pv", m - 1, 14 + h), None))
                    c0, c1 = chunks[0][0], chunks[-1][0]
                    i = s.activation(out=B["e_t"][:, pb * 384 + c0 * 128: pb * 384 + (c1 + 1) * 128],
                                     in_=psc[:, pb * 512 + c0 * 128: pb * 512 + (c1 + 1) * 128],
                                     func=AF.Exp, bias=0.0, scale=SCALE)
                    sch.inc("S", i)

        def emit_pv(m, h, chunks):
            pb = h % 2

            @block.tensor
            def _(te):
                if m == ms[0] and h == 0:
                    sch.wait(te, "T", st.r.get(("v", 4 * t + 1), None))
                sch.wait(te, "T", st.r[("exp", m, h)])
                if h % 8 == 0:
                    sch.wait(te, "T", st.r.get(("anorm", m, h - 8), None))
                    if h == 0:
                        sch.wait(te, "T", st.r.get(("anorm", m - 1, 8), None))
                po = (h % 8) * 128
                first = True
                i = None
                for (c, ks) in chunks:
                    lhs = B["e_t"][:, pb * 384 + c * 128: pb * 384 + (c + 1) * 128]
                    rhs = B["vt_t"][:, ks * 1040 + h * 65: ks * 1040 + (h + 1) * 65]
                    i = te.matmul(ppv[:, po:po + 65], lhs, rhs,
                                  start=first, stop=(c == chunks[-1][0]))
                    first = False
                st.r[("pv", m, h)] = sch.inc("T", i)

        for m in ms:
            chunks = []
            if m > 0:
                chunks.append((0, (m - 1) % 6))
            chunks.append((1, m % 6))
            if m < 31:
                chunks.append((2, (m + 1) % 6))

            e_base = sch.count["S"]
            for h in range(16):
                st.r[("exp", m, h)] = ("S", e_base + h + 1)
            for h in range(16):
                emit_sc(m, h, chunks)
                if h >= 1:
                    emit_pv(m, h - 1, chunks)
                    if h == 8:
                        _emit_wave_norm(nc, block, st, m, 0, ppv)
            emit_pv(m, 15, chunks)
            emit_exp_section(m, chunks)
            _emit_wave_norm(nc, block, st, m, 1, ppv)

            for dc in range(8):
                @block.tensor
                def _(te, m=m, dc=dc):
                    if dc == 0:
                        if m == ms[0]:
                            sch.wait(te, "T", st.r.get(("v", 4 * t + 2), None))
                        sch.wait(te, "T", st.r[("anorm", m, 0)])
                    if dc == 4:
                        sch.wait(te, "T", st.r[("anorm", m, 8)])
                    pb2 = dc % 2
                    prev = ("atr_ev", m, dc - 2) if dc >= 2 else ("atr_ev", m - 1, 6 + dc)
                    sch.wait(te, "T", st.r.get(prev, None))
                    i = te.transpose(ptr2[:, pb2 * 512: pb2 * 512 + 128],
                                     B["attn_tm"][:, dc * 128:(dc + 1) * 128],
                                     B["id_f32"][:])
                    st.r[("atr", m, dc)] = sch.inc("T", i)

                @block.vector
                def _(v, m=m, dc=dc):
                    pb2 = dc % 2
                    sch.wait(v, "V", st.r[("atr", m, dc)])
                    sch.wait(v, "V", st.r.get(("womm", m - 1, dc), None))
                    v.tensor_copy(out=B["at_hi"][:, dc * 128:(dc + 1) * 128],
                                  in_=ptr2[:, pb2 * 512: pb2 * 512 + 128])
                    i = v.scalar_tensor_tensor(
                        out=B["at_lo"][:, dc * 128:(dc + 1) * 128],
                        in0=B["at_hi"][:, dc * 128:(dc + 1) * 128], scalar=-1.0,
                        in1=ptr2[:, pb2 * 512: pb2 * 512 + 128], op0=ALU.mult, op1=ALU.add)
                    st.r[("atr_ev", m, dc)] = sch.inc("V", i)

            for dc in range(8):
                sl = dc % 2

                @block.sync
                def _(y, m=m, dc=dc, sl=sl):
                    if dc < 2:
                        sch.wait(y, "Y", st.wvs_last)
                    sch.wait(y, "Y", st.r.get(("womm", m, dc - 2) if dc >= 2
                                              else ("womm", m - 1, 6 + dc), None))
                    d = y.dma_start(out=B["wvs_hi"][:, sl * D:(sl + 1) * D],
                                    in_=H["wo_hi"][dc])
                    sch.inc("DL", d)
                    d = y.dma_start(out=B["wvs_lo"][:, sl * D:(sl + 1) * D],
                                    in_=H["wo_lo"][dc])
                    st.r[("wow", m, dc)] = sch.inc("DL", d)

                @block.tensor
                def _(te, m=m, dc=dc, sl=sl):
                    if dc == 0:
                        if m == ms[0]:
                            sch.wait(te, "T", st.r.get(("v", 4 * t + 3), None))
                        sch.wait(te, "T", st.r.get((("ln1", m - 1), "r"), None))
                    sch.wait(te, "T", st.r[("atr_ev", m, dc)])
                    sch.wait(te, "T", st.r[("wow", m, dc)])
                    ah = B["at_hi"][:, dc * 128:(dc + 1) * 128]
                    al = B["at_lo"][:, dc * 128:(dc + 1) * 128]
                    i = None
                    for hf in range(2):
                        pws = pwo[:, hf * 512:(hf + 1) * 512]
                        te.matmul(pws, ah,
                                  B["wvs_hi"][:, sl * D + hf * 512: sl * D + (hf + 1) * 512],
                                  start=(dc == 0), stop=False)
                        te.matmul(pws, ah,
                                  B["wvs_lo"][:, sl * D + hf * 512: sl * D + (hf + 1) * 512],
                                  start=False, stop=False)
                        i = te.matmul(pws, al,
                                      B["wvs_hi"][:, sl * D + hf * 512: sl * D + (hf + 1) * 512],
                                      start=False, stop=False)
                    st.r[("womm", m, dc)] = sch.inc("T", i)

            @block.tensor
            def _(te, m=m):
                i = None
                for hf in range(2):
                    pws = pwo[:, hf * 512:(hf + 1) * 512]
                    te.matmul(pws, B["ones_bf"][0:1, 0:128],
                              B["bor_hi"][0:1, hf * 512:(hf + 1) * 512],
                              start=False, stop=False)
                    i = te.matmul(pws, B["ones_bf"][0:1, 0:128],
                                  B["bor_lo"][0:1, hf * 512:(hf + 1) * 512],
                                  start=False, stop=True)
                st.r[("wo_mm", m)] = sch.inc("T", i)
            st.wvs_last = st.r[("womm", m, 7)]

            _emit_ln(nc, block, st, src=pwo[:],
                     res_ap=B["x_t"][:, (m % 5) * D:(m % 5 + 1) * D],
                     out_ap=B["y_t"][:, (m % 4) * D:(m % 4 + 1) * D],
                     key=("ln1", m), dep=st.r[("wo_mm", m)])
            st.r[("y", m)] = st.r[("ln1", m)]
        st.sc_guard = st.r[("sc", ms[-1], 15)]
        st.pv_guard = st.r[("pv", ms[-1], 15)]
        st.a2_ln_last = st.r[("ln1", ms[-1])]
        st.a2_exp_last = st.r[("exp", ms[-1], 15)]


def _emit_wave_norm(nc, block, st, m, wave, ppv):
    sch, B = st.sch, st.B

    @block.vector
    def _(v):
        for h in range(wave * 8, wave * 8 + 8):
            sch.wait(v, "V", st.r[("pv", m, h)])
        if wave == 0:
            sch.wait(v, "V", st.r.get(("atr", m - 1, 7), None))
        i = v.reciprocal(out=B["st_t"][:, 8:16], in_=ppv[:, 64::128])
        sch.wait(v, "V", sch.inc("V", i))
        i = None
        for hh in range(8):
            h = wave * 8 + hh
            i = v.tensor_scalar(out=B["attn_tm"][:, h * 64:(h + 1) * 64],
                                in0=ppv[:, hh * 128: hh * 128 + 64],
                                scalar1=B["st_t"][:, 8 + hh:9 + hh], scalar2=None,
                                op0=ALU.mult)
        st.r[("anorm", m, wave * 8)] = sch.inc("V", i)


def _emit_ln(nc, block, st, src, res_ap, out_ap, key, dep):
    sch, B = st.sch, st.B
    r_buf = B["r2_t"]

    @block.vector
    def _(v):
        sch.wait(v, "V", dep)
        sch.wait(v, "V", st.last_ln_sq)
        i = v.scalar_tensor_tensor(out=r_buf, in0=src, scalar=1.0, in1=res_ap,
                                   op0=ALU.mult, op1=ALU.add,
                                   accum_out=B["st_t"][:, 0:1])
        st.r[(key, "r")] = sch.inc("V", i)

    _emit_ln_tail(nc, block, st, key, r_buf, out_ap)


def _emit_ln_tail(nc, block, st, key, r_buf, out_ap, pre_tok=None, sum2=False):
    sch, B = st.sch, st.B

    @block.scalar
    def _(s):
        sch.wait(s, "S", st.r[(key, "r")])
        i = s.activation(out=B["sq_t"], in_=r_buf, func=AF.Square,
                         bias=0.0, scale=1.0, accum_out=B["st_t"][:, 1:2])
        st.r[(key, "sq")] = sch.inc("S", i)
    st.last_ln_sq = st.r[(key, "sq")]

    @block.vector
    def _(v):
        sch.wait(v, "V", st.r[(key, "sq")])
        stt = B["st_t"]
        if sum2:
            i = v.tensor_tensor(out=stt[:, 0:1], in0=stt[:, 0:1],
                                in1=stt[:, 9:10], op=ALU.add)
            sch.wait(v, "V", sch.inc("V", i))
        i = v.tensor_scalar(out=stt[:, 2:3], in0=stt[:, 0:1], scalar1=1.0 / D,
                            scalar2=None, op0=ALU.mult)
        sch.wait(v, "V", sch.inc("V", i))
        i = v.tensor_tensor(out=stt[:, 3:4], in0=stt[:, 2:3], in1=stt[:, 2:3],
                            op=ALU.mult)
        sch.wait(v, "V", sch.inc("V", i))
        i = v.scalar_tensor_tensor(out=stt[:, 4:5], in0=stt[:, 1:2], scalar=1.0 / D,
                                   in1=stt[:, 3:4], op0=ALU.mult, op1=ALU.subtract)
        sch.wait(v, "V", sch.inc("V", i))
        i = v.tensor_scalar(out=stt[:, 4:5], in0=stt[:, 4:5], scalar1=1.0e-5,
                            scalar2=None, op0=ALU.add)
        st.r[(key, "v")] = sch.inc("V", i)

    @block.scalar
    def _(s):
        sch.wait(s, "S", st.r[(key, "v")])
        i = s.activation(out=B["st_t"][:, 5:6], in_=B["st_t"][:, 4:5],
                         func=AF.Sqrt, bias=0.0, scale=1.0)
        st.r[(key, "s0")] = sch.inc("S", i)

    @block.vector
    def _(v):
        sch.wait(v, "V", st.r[(key, "s0")])
        sch.wait(v, "V", pre_tok)
        stt = B["st_t"]
        i = v.reciprocal(out=stt[:, 6:7], in_=stt[:, 5:6])
        sch.wait(v, "V", sch.inc("V", i))
        if isinstance(out_ap, tuple):
            v.tensor_scalar(out=out_ap[0], in0=r_buf[:, 0:512], scalar1=stt[:, 2:3],
                            scalar2=stt[:, 6:7], op0=ALU.subtract, op1=ALU.mult)
            i = v.tensor_scalar(out=out_ap[1], in0=r_buf[:, 512:1024], scalar1=stt[:, 2:3],
                                scalar2=stt[:, 6:7], op0=ALU.subtract, op1=ALU.mult)
        else:
            i = v.tensor_scalar(out=out_ap, in0=r_buf, scalar1=stt[:, 2:3],
                                scalar2=stt[:, 6:7], op0=ALU.subtract, op1=ALU.mult)
        st.r[key] = sch.inc("V", i)


def _emit_B(nc, block, st, t, blocks):
    sch, H, B = st.sch, st.H, st.B
    nb = len(blocks)
    ntok = nb * 128
    bid = (t, blocks[0])

    if H.get("ydbg") is not None:
        @block.gpsimd
        def _(g):
            for m in blocks:
                sch.wait(g, "G", st.r[("y", m)])
                d = g.dma_start(out=H["ydbg"][m * 128:(m + 1) * 128, :],
                                in_=B["y_t"][:, (m % 4) * D:(m % 4 + 1) * D])
                sch.inc("DS", d)

    with nc.psum_tensor(f"ptr3_{t}_{blocks[0]}", [128, 1024], F32) as ptr3:
        for bi, m in enumerate(blocks):
            for dc in range(8):
                @block.tensor
                def _(te, m=m, bi=bi, dc=dc):
                    if dc == 0:
                        if bi == 0:
                            sch.wait(te, "T", st.a2_exp_last)
                        sch.wait(te, "T", st.r[("y", m)])
                    pb = dc % 2
                    prev = ("ytev", bid, bi, dc - 2) if dc >= 2 else ("ytev", bid, bi - 1, 6 + dc)
                    sch.wait(te, "T", st.r.get(prev, None))
                    i = te.transpose(ptr3[:, pb * 512: pb * 512 + 128],
                                     B["y_t"][:, (m % 4) * D + dc * 128:(m % 4) * D + (dc + 1) * 128],
                                     B["id_f32"][:])
                    st.r[("ytr", bid, bi, dc)] = sch.inc("T", i)

                @block.vector
                def _(v, m=m, bi=bi, dc=dc):
                    pb = dc % 2
                    col = (bi * 8 + dc) * 128
                    sch.wait(v, "V", st.r[("ytr", bid, bi, dc)])
                    i = v.tensor_copy(out=B["xt_hi"][:, col:col + 128],
                                      in_=ptr3[:, pb * 512: pb * 512 + 128])
                    i = v.scalar_tensor_tensor(
                        out=B["xt_lo"][:, col:col + 128],
                        in0=B["xt_hi"][:, col:col + 128], scalar=-1.0,
                        in1=ptr3[:, pb * 512: pb * 512 + 128], op0=ALU.mult, op1=ALU.add)
                    st.r[("ytev", bid, bi, dc)] = sch.inc("V", i)

    yt_ready = sch.tok("V")

    def emit_w1_stream(fc):
        sl = fc % 2

        @block.gpsimd
        def _(g):
            if fc < 2:
                sch.wait(g, "G", st.qks_last)
            sch.wait(g, "G", st.r.get(("ff1mm", bid, fc - 2), None))
            d = g.dma_start(out=B["w1s_hi"][:, sl * 1024:(sl + 1) * 1024].rearrange(
                "p (dc m) -> p dc m", m=128), in_=H["w1_hi"][fc].rearrange("dc p m -> p dc m"))
            sch.inc("DG", d)
            d = g.dma_start(out=B["w1s_lo"][:, sl * 1024:(sl + 1) * 1024].rearrange(
                "p (dc m) -> p dc m", m=128), in_=H["w1_lo"][fc].rearrange("dc p m -> p dc m"))
            sch.inc("DG", d)
            d = g.dma_start(out=B["b1s_hi"][0:1, sl * 128:(sl + 1) * 128],
                            in_=H["b1r_hi"][fc])
            sch.inc("DG", d)
            d = g.dma_start(out=B["b1s_lo"][0:1, sl * 128:(sl + 1) * 128],
                            in_=H["b1r_lo"][fc])
            st.r[("w1w", bid, fc)] = sch.inc("DG", d)

    def emit_w2_stream(fc):
        sl = fc % 2

        @block.sync
        def _(y):
            if fc < 2:
                sch.wait(y, "Y", st.wvs_last)
            sch.wait(y, "Y", st.r.get(("ff2b", bid, fc - 2), None))
            d = y.dma_start(out=B["w2s_hi"][:, sl * D: (sl + 1) * D],
                            in_=H["w2_hi"][fc])
            sch.inc("DL", d)
            d = y.dma_start(out=B["w2s_lo"][:, sl * D: (sl + 1) * D],
                            in_=H["w2_lo"][fc])
            st.r[("w2w", bid, fc)] = sch.inc("DL", d)

    with nc.psum_tensor(f"ph_{t}_{blocks[0]}", [128, 2 * 512], F32) as ph, \
         nc.psum_tensor(f"po_{t}_{blocks[0]}", [128, 4 * 512], F32) as po, \
         nc.psum_tensor(f"po1_{t}_{blocks[0]}", [128, 2 * 512], F32) as po1:

        def emit_ff1_T(fc):
            sl = fc % 2

            @block.tensor
            def _(te):
                sch.wait(te, "T", st.r[("w1w", bid, fc)])
                sch.wait(te, "T", yt_ready)
                if fc >= 2:
                    sch.wait(te, "T", ("S", st.b_sbase
                                       + S_PER * ((fc - 2) // gf)
                                       + ((fc - 2) % gf) + 1))
                rh4 = B["xt_hi"][:, 0:nb * 8 * 128].rearrange(
                    "p (s dc m) -> p dc s m", dc=8, m=128)
                rl4 = B["xt_lo"][:, 0:nb * 8 * 128].rearrange(
                    "p (s dc m) -> p dc s m", dc=8, m=128)
                pp = ph[:, (fc % 2) * 512:(fc % 2) * 512 + ntok]
                for dc in range(8):
                    whi = B["w1s_hi"][:, sl * 1024 + dc * 128: sl * 1024 + (dc + 1) * 128]
                    wlo = B["w1s_lo"][:, sl * 1024 + dc * 128: sl * 1024 + (dc + 1) * 128]
                    te.matmul(pp, whi, rh4[:, dc], start=(dc == 0), stop=False)
                    te.matmul(pp, whi, rl4[:, dc], start=False, stop=False)
                    te.matmul(pp, wlo, rh4[:, dc], start=False, stop=False)
                te.matmul(pp, B["b1s_hi"][0:1, sl * 128:(sl + 1) * 128],
                          B["ones_bf"][0:1, 0:ntok], start=False, stop=False)
                i = te.matmul(pp, B["b1s_lo"][0:1, sl * 128:(sl + 1) * 128],
                              B["ones_bf"][0:1, 0:ntok], start=False, stop=True)
                st.r[("ff1mm", bid, fc)] = sch.inc("T", i)

        def rslot(g, k):
            return (RK.index(k) % 4) * 512

        def s_cnt(g, off):
            return ("S", st.b_sbase + S_PER * g + off)

        def v_cnt(g, off):
            return ("V", st.b_vbase + V_PER * g + off)

        def emit_epi_S():
            @block.scalar
            def _(s):
                sch.wait(s, "S", st.last_store)
                for fc in range(32):
                    p5 = (fc % 2) * 512
                    sch.wait(s, "S", v_cnt(fc - 2, 2))
                    sch.wait(s, "S", st.r[("ff1mm", bid, fc)])
                    pp = ph[:, (fc % 2) * 512:(fc % 2) * 512 + ntok]
                    i = s.activation(out=B["s1_t"][:, p5:p5 + ntok], in_=pp,
                                     func=AF.Identity, bias=B["pmag"][:, 0:1],
                                     scale=256.0)
                    sch.inc("S", i)
                    uap = B["u0"] if fc % 2 == 0 else B["u1"]
                    i = s.activation(out=uap[:, 0:ntok],
                                     in_=B["s1_t"][:, p5:p5 + ntok],
                                     func=AF.Abs, bias=B["nmag"][:, 0:1], scale=1.0)
                    sch.inc("S", i)
                    sch.wait(s, "S", v_cnt(fc - 1, 19))
                    for k in RK:
                        i = s.activation(
                            out=B["rk_t"][:, rslot(fc, k): rslot(fc, k) + ntok],
                            in_=uap[:, 0:ntok], func=AF.Relu,
                            bias=B["kinkb"][:, k:k + 1],
                            scale=(1.0 if k in TK else abs(KINK_C[k])))
                        sch.inc("S", i)

        def emit_epi_V(g):
            @block.vector
            def _(v):
                p5 = (g % 2) * 512
                uap = B["u0"] if g % 2 == 0 else B["u1"]
                sch.wait(v, "V", s_cnt(g, gf))
                i = v.tensor_scalar(out=B["acc_t"][:, 0:ntok_g],
                                    in0=B["s1_t"][:, p5:p5 + ntok_g],
                                    scalar1=0.5, scalar2=-MAGIC / 2,
                                    op0=ALU.mult, op1=ALU.add)
                sch.inc("V", i)
                sch.wait(v, "V", s_cnt(g, gf + 1))
                i = v.scalar_tensor_tensor(out=B["acc_t"][:, 0:ntok_g],
                                           in0=uap[:, 0:ntok_g],
                                           scalar=A0, in1=B["acc_t"][:, 0:ntok_g],
                                           op0=ALU.mult, op1=ALU.add)
                sch.inc("V", i)
                for idx, k in enumerate(RK):
                    sch.wait(v, "V", s_cnt(g, gf + 2 + idx))
                    i = v.scalar_tensor_tensor(
                        out=B["acc_t"][:, 0:ntok_g],
                        in0=B["rk_t"][:, rslot(g, k): rslot(g, k) + ntok_g],
                        scalar=(KINK_C[k] if k in TK
                                else (1.0 if KINK_C[k] > 0 else -1.0)),
                        in1=B["acc_t"][:, 0:ntok_g], op0=ALU.mult, op1=ALU.add)
                    sch.inc("V", i)
                i = v.tensor_scalar(out=B["a_t"][:, g * 512: g * 512 + ntok_g],
                                    in0=B["acc_t"][:, 0:ntok_g], scalar1=MAGIC,
                                    scalar2=-MAGIC, op0=ALU.add, op1=ALU.add)
                st.r[("aq", bid, g)] = sch.inc("V", i)

        nb1 = min(2, nb)
        gf = 1
        ngrp = 32 // gf
        ntok_g = ntok * gf
        S_PER = gf + 1 + len(RK)
        V_PER = 20

        def a_slice(fc, sub):
            if gf == 1:
                return B["a_t"][:, fc * 512 + sub * 128: fc * 512 + (sub + 1) * 128]
            return B["a_t"][:, fc * 128: (fc + 1) * 128]

        def emit_ff2a(fc):
            sl2 = fc % 2

            @block.tensor
            def _(te):
                sch.wait(te, "T", ("V", st.b_vbase + V_PER * (fc // gf) + 20))
                sch.wait(te, "T", st.r[("w2w", bid, fc)])
                i = None
                for sub in range(nb):
                    a_ap = a_slice(fc, sub)
                    te.matmul(po[:, sub * 512:(sub + 1) * 512], a_ap,
                              B["w2s_hi"][:, sl2 * D: sl2 * D + 512],
                              start=(fc == 0), stop=False)
                    i = te.matmul(po[:, sub * 512:(sub + 1) * 512], a_ap,
                                  B["w2s_lo"][:, sl2 * D: sl2 * D + 512],
                                  start=False, stop=(fc == 31))
                    if fc == 0:
                        te.matmul(po[:, sub * 512:(sub + 1) * 512],
                                  B["ones_f16"][0:1, 0:128],
                                  B["b2r_hi"][0:1, 0:512],
                                  start=False, stop=False)
                        i = te.matmul(po[:, sub * 512:(sub + 1) * 512],
                                      B["ones_f16"][0:1, 0:128],
                                      B["b2r_lo"][0:1, 0:512],
                                      start=False, stop=False)
                st.r[("ff2a", bid, fc)] = sch.inc("T", i)

        def emit_ff2b(fc):
            sl2 = fc % 2

            @block.tensor
            def _(te):
                i = None
                for sub in range(nb1):
                    a_ap = a_slice(fc, sub)
                    te.matmul(po1[:, sub * 512:(sub + 1) * 512], a_ap,
                              B["w2s_hi"][:, sl2 * D + 512: (sl2 + 1) * D],
                              start=(fc == 0), stop=False)
                    i = te.matmul(po1[:, sub * 512:(sub + 1) * 512], a_ap,
                                  B["w2s_lo"][:, sl2 * D + 512: (sl2 + 1) * D],
                                  start=False, stop=(fc == 31))
                    if fc == 0:
                        te.matmul(po1[:, sub * 512:(sub + 1) * 512],
                                  B["ones_f16"][0:1, 0:128],
                                  B["b2r_hi"][0:1, 512:1024],
                                  start=False, stop=False)
                        i = te.matmul(po1[:, sub * 512:(sub + 1) * 512],
                                      B["ones_f16"][0:1, 0:128],
                                      B["b2r_lo"][0:1, 512:1024],
                                      start=False, stop=False)
                st.r[("ff2b", bid, fc)] = sch.inc("T", i)

        st.b_sbase = sch.count["S"]
        st.b_vbase = sch.count["V"]
        lag = 2 * gf
        emit_w2_stream(0)
        emit_w2_stream(1)
        for pr in range(16):
            f0 = 2 * pr
            emit_w1_stream(f0)
            emit_w1_stream(f0 + 1)
            emit_ff1_T(f0)
            emit_ff1_T(f0 + 1)
            if f0 >= lag:
                emit_ff2a(f0 - lag)
                emit_ff2b(f0 - lag)
                emit_ff2a(f0 - lag + 1)
                emit_ff2b(f0 - lag + 1)
                if f0 - lag + 2 < 32:
                    emit_w2_stream(f0 - lag + 2)
                if f0 - lag + 3 < 32:
                    emit_w2_stream(f0 - lag + 3)
        for fc in range(32 - lag, 32):
            emit_ff2a(fc)
            emit_ff2b(fc)
            if fc + 2 < 32:
                emit_w2_stream(fc + 2)
        emit_epi_S()
        for g in range(ngrp):
            emit_epi_V(g)

        for sub, m in enumerate(blocks):
            @block.scalar
            def _(s, sub=sub, m=m):
                sch.wait(s, "S", st.r[("ff2a", bid, 31)])
                if sub > 0:
                    sch.wait(s, "S", st.r.get(("oqa", blocks[sub - 1]), None))
                i = s.activation(out=B["s3_t"], in_=po[:, sub * 512:(sub + 1) * 512],
                                 func=AF.Identity, bias=B["pmag"][:, 0:1], scale=1.0)
                st.r[("s3a", m)] = sch.inc("S", i)

            @block.vector
            def _(v, sub=sub, m=m):
                sch.wait(v, "V", st.r[("s3a", m)])
                i = v.tensor_scalar(out=B["oh16"][:, sub * 512:(sub + 1) * 512],
                                    in0=B["s3_t"], scalar1=-MAGIC,
                                    scalar2=None, op0=ALU.add)
                st.r[("oqa", m)] = sch.inc("V", i)

        def emit_ln2(sub, m, src_po, gate_key):
            @block.scalar
            def _(s, sub=sub, m=m):
                sch.wait(s, "S", st.r[gate_key])
                if sub == 0:
                    sch.wait(s, "S", st.r[("oqa", blocks[-1])])
                sch.wait(s, "S", st.r.get(("r2", m - 1), None))
                i = s.activation(out=B["s3_t"], in_=src_po,
                                 func=AF.Identity, bias=B["pmag"][:, 0:1], scale=1.0)
                st.r[("s3b", m)] = sch.inc("S", i)

            @block.vector
            def _(v, sub=sub, m=m):
                sch.wait(v, "V", st.r[("s3b", m)])
                sch.wait(v, "V", st.last_ln_sq)
                sch.wait(v, "V", st.r.get(("ff2c", bid, 15), None))
                v.tensor_scalar(out=B["u0"][:, 0:512], in0=B["s3_t"],
                                scalar1=-MAGIC, scalar2=1.0 / 256.0,
                                op0=ALU.add, op1=ALU.mult)
                v.scalar_tensor_tensor(
                    out=B["r2_t"][:, 0:512], in0=B["oh16"][:, sub * 512:(sub + 1) * 512],
                    scalar=1.0 / 256.0, in1=B["y_t"][:, (m % 4) * D:(m % 4) * D + 512],
                    op0=ALU.mult, op1=ALU.add, accum_out=B["st_t"][:, 0:1])
                i = v.scalar_tensor_tensor(
                    out=B["r2_t"][:, 512:1024], in0=B["u0"][:, 0:512],
                    scalar=1.0, in1=B["y_t"][:, (m % 4) * D + 512:(m % 4 + 1) * D],
                    op0=ALU.mult, op1=ALU.add, accum_out=B["st_t"][:, 9:10])
                st.r[(("ln2", m), "r")] = sch.inc("V", i)
                st.r[("r2", m)] = st.r[(("ln2", m), "r")]

            _emit_ln_tail(nc, block, st, ("ln2", m), B["r2_t"],
                          (B["s1_t"][:, 0:512], B["s1_t"][:, 512:1024]),
                          pre_tok=st.last_store, sum2=True)
            st.r[("ob", m)] = st.r[("ln2", m)]

            @block.gpsimd
            def _(g, m=m):
                sch.wait(g, "G", st.r[("ob", m)])
                d = g.dma_start(out=st.H["out"][m * 128:(m + 1) * 128, 0:512],
                                in_=B["s1_t"][:, 0:512])
                sch.inc("DS", d)
                d = g.dma_start(out=st.H["out"][m * 128:(m + 1) * 128, 512:1024],
                                in_=B["s1_t"][:, 512:1024])
                st.last_store = sch.inc("DS", d)

        if nb > 2:
            for fc in range(32):
                sl = fc % 2

                @block.gpsimd
                def _(g, fc=fc, sl=sl):
                    if fc < 2:
                        sch.wait(g, "G", st.r[("ff2b", bid, 30 + sl)])
                    sch.wait(g, "G", st.r.get(("ff2c", bid, fc - 2), None))
                    d = g.dma_start(out=B["w2s_hi"][:, sl * D: sl * D + 512],
                                    in_=H["w2_hi"][fc][:, 512:1024])
                    sch.inc("DG", d)
                    d = g.dma_start(out=B["w2s_lo"][:, sl * D: sl * D + 512],
                                    in_=H["w2_lo"][fc][:, 512:1024])
                    st.r[("w2wc", bid, fc)] = sch.inc("DG", d)

                @block.tensor
                def _(te, fc=fc, sl=sl):
                    if fc == 0:
                        sch.wait(te, "T", st.r[("s3a", blocks[-1])])
                    sch.wait(te, "T", st.r[("w2wc", bid, fc)])
                    i = None
                    for j, sub in enumerate(range(2, nb)):
                        a_ap = a_slice(fc, sub)
                        te.matmul(po[:, j * 512:(j + 1) * 512], a_ap,
                                  B["w2s_hi"][:, sl * D: sl * D + 512],
                                  start=(fc == 0), stop=False)
                        i = te.matmul(po[:, j * 512:(j + 1) * 512], a_ap,
                                      B["w2s_lo"][:, sl * D: sl * D + 512],
                                      start=False, stop=(fc == 31))
                        if fc == 0:
                            te.matmul(po[:, j * 512:(j + 1) * 512],
                                      B["ones_f16"][0:1, 0:128],
                                      B["b2r_hi"][0:1, 512:1024],
                                      start=False, stop=False)
                            i = te.matmul(po[:, j * 512:(j + 1) * 512],
                                          B["ones_f16"][0:1, 0:128],
                                          B["b2r_lo"][0:1, 512:1024],
                                          start=False, stop=False)
                    st.r[("ff2c", bid, fc)] = sch.inc("T", i)

            st.wvs_last = st.r[("ff2c", bid, 31)]
        else:
            st.wvs_last = st.r[("ff2b", bid, 31)]
        for sub in range(nb1):
            emit_ln2(sub, blocks[sub], po1[:, sub * 512:(sub + 1) * 512],
                     ("ff2b", bid, 31))
        for j, sub in enumerate(range(2, nb)):
            emit_ln2(sub, blocks[sub], po[:, j * 512:(j + 1) * 512],
                     ("ff2c", bid, 31))
        st.qks_last = st.r[("ff1mm", bid, 31)]

        st.b_s3b = st.r[("s3b", blocks[-1])]
        st.b_s1last = ("S", st.b_sbase + S_PER * (ngrp - 1) + gf)
        st.b_vdone = sch.tok("V")


def kernel(L0_attn_x, L0_attn_Wqkv, L0_attn_bqkv, L0_attn_Wout, L0_attn_bout,
           L0_ffn_w1, L0_ffn_b1, L0_ffn_w2, L0_ffn_b2):
    nc = build()
    in_maps = []
    for c in range(8):
        in_maps.append(dict(
            x=np.ascontiguousarray(L0_attn_x[c]),
            wqkv=np.ascontiguousarray(L0_attn_Wqkv),
            bqkv=np.ascontiguousarray(L0_attn_bqkv),
            wout=np.ascontiguousarray(L0_attn_Wout),
            bout=np.ascontiguousarray(L0_attn_bout),
            w1=np.ascontiguousarray(L0_ffn_w1),
            b1=np.ascontiguousarray(L0_ffn_b1),
            w2=np.ascontiguousarray(L0_ffn_w2),
            b2=np.ascontiguousarray(L0_ffn_b2),
        ))
    res = run_bass_kernel_spmd(nc, in_maps, core_ids=list(range(8)))
    return np.stack([r["out"] for r in res.results], axis=0)
